# version 09
# baseline (speedup 1.0000x reference)
import numpy as np
import concourse.bass as bass
import concourse.mybir as mybir
from concourse.bass_utils import run_bass_kernel_spmd

F32 = mybir.dt.float32
BF16 = mybir.dt.bfloat16
ALU = mybir.AluOpType
AF = mybir.ActivationFunctionType
AX = mybir.AxisListType.X

B, L, Dm, DH, NS, RK, HID, NPJ = 4, 4096, 2048, 1024, 16, 64, 8192, 96
TC0 = 256
NC0 = L // TC0
TC = 256
NCH = L // TC
NG = 8
NSL = 4
CLIP = 128.0 - 1e-6
INV_D = 1.0 / 2048.0
INV_H = 1.0 / 8192.0
DEV = False
DEV2 = False
SIM_NOCC = False
PH = 4


class Plan:
    ENGINES = ("sync", "vector", "scalar", "tensor", "gpsimd")

    def __init__(self):
        self.ops = {e: [] for e in self.ENGINES}
        self.cnt = {}

    def val(self, s):
        return self.cnt.get(s, 0)

    def emit(self, eng, fn, waits=(), inc=None):
        if inc is not None:
            self.cnt[inc[0]] = self.cnt.get(inc[0], 0) + inc[1]
        self.ops[eng].append(([w for w in waits if w[1] > 0], fn, inc))

    def seq(self, eng, fn, waits=(), inc=None):
        key = "_q_" + eng
        last = getattr(self, key, None)
        w = list(waits)
        if last is not None:
            w.append(last)
        if inc is None:
            inc = ("s_vq" if eng == "vector" else "s_sq", 1)
        self.emit(eng, fn, waits=w, inc=inc)
        setattr(self, key, (inc[0], self.val(inc[0])))

    def dma(self, eng, out, in_, waits=(), sem=None, slow=False):
        inc = (sem, 16) if sem else None
        if slow:
            self.emit(eng, lambda e, o=out, i=in_: e.dma_start(
                out=o, in_=i, allow_slow_non_contiguous=True), waits=waits, inc=inc)
        else:
            self.emit(eng, lambda e, o=out, i=in_: e.dma_start(out=o, in_=i),
                      waits=waits, inc=inc)


def grep(ap_, nrep):
    dims = [list(d) for d in ap_.ap]
    return bass.AP(ap_.tensor, ap_.offset, [dims[0], [0, nrep]] + dims[1:])


def build_nc():
    nc = bass.Bass("TRN2", target_bir_lowering=False, detect_race_conditions=False)
    P = Plan()

    x_in = nc.dram_tensor("x", [L, Dm], F32, kind="ExternalInput").ap()
    xT_in = nc.dram_tensor("xT", [Dm, L], F32, kind="ExternalInput").ap()
    xm_in = nc.dram_tensor("xmine", [2048, Dm], F32, kind="ExternalInput").ap()
    wxT_in = nc.dram_tensor("wxT", [Dm, NPJ], F32, kind="ExternalInput").ap()
    wdtT_in = nc.dram_tensor("wdtT", [RK, DH], F32, kind="ExternalInput").ap()
    bdt_in = nc.dram_tensor("bdt", [DH], F32, kind="ExternalInput").ap()
    alog_in = nc.dram_tensor("alog", [DH, NS], F32, kind="ExternalInput").ap()
    dvec_in = nc.dram_tensor("dvec", [DH], F32, kind="ExternalInput").ap()
    wfcT_in = nc.dram_tensor("wfcT", [Dm, HID], F32, kind="ExternalInput").ap()
    wpjT_in = nc.dram_tensor("wpjT", [HID, Dm], F32, kind="ExternalInput").ap()
    hflag_in = nc.dram_tensor("hflag", [1, 2], F32, kind="ExternalInput").ap()
    out_hbm = nc.dram_tensor("o", [2048, Dm], F32, kind="ExternalOutput").ap()

    p_m = nc.dram_tensor("p_m", [1, L], F32, kind="Internal").ap()
    p_r = nc.dram_tensor("p_r", [1, L], F32, kind="Internal").ap()
    p_bc = nc.dram_tensor("p_bc", [32, L], F32, kind="Internal").ap()
    p_dt = nc.dram_tensor("p_dt", [8, 128, L], F32, kind="Internal").ap()
    p_xn = nc.dram_tensor("p_xn", [8, 128, L], F32, kind="Internal").ap()
    p_w1 = nc.dram_tensor("p_w1", [8, 16, 128, 1024], BF16, kind="Internal").ap()
    p_w2 = nc.dram_tensor("p_w2", [64, 128, Dm], BF16, kind="Internal").ap()
    p_h = nc.dram_tensor("p_h", [NG, 2, 128, HID], F32, kind="Internal").ap()
    cc_i = nc.dram_tensor("cc_i", [L, DH], F32, kind="Internal").ap()
    cc_o = nc.dram_tensor("cc_o", [2, L, DH], F32, kind="Internal",
                          addr_space="Local").ap()
    dbg = {}
    if DEV2:
        dbg["mix2"] = nc.dram_tensor("d_mix2", [L, DH], F32, kind="ExternalOutput").ap()
        dbg["hall"] = nc.dram_tensor("d_hall", [NG, 2, 128, HID], F32,
                                     kind="ExternalOutput").ap()
        dbg["cco"] = nc.dram_tensor("d_cco", [2, L, DH], F32, kind="ExternalOutput").ap()
        dbg["w1f"] = nc.dram_tensor("d_w1f", [8, 16, 128, 1024], BF16, kind="ExternalOutput").ap()
        dbg["w2f"] = nc.dram_tensor("d_w2f", [64, 128, 2048], BF16, kind="ExternalOutput").ap()
    if DEV:
        dbg["xproj"] = nc.dram_tensor("d_xproj", [NPJ, L], F32, kind="ExternalOutput").ap()
        dbg["dt"] = nc.dram_tensor("d_dt", [8, 128, L], F32, kind="ExternalOutput").ap()
        dbg["xn"] = nc.dram_tensor("d_xn", [8, 128, L], F32, kind="ExternalOutput").ap()
        dbg["gam"] = nc.dram_tensor("d_gam", [1, 4], F32, kind="ExternalOutput").ap()
        dbg["w1"] = nc.dram_tensor("d_w1", [16, 128, 1024], BF16, kind="ExternalOutput").ap()
        dbg["mix"] = nc.dram_tensor("d_mix", [L, DH], F32, kind="ExternalOutput").ap()
        dbg["h0"] = nc.dram_tensor("d_h0", [2, 128, HID], F32, kind="ExternalOutput").ap()
        dbg["xhat"] = nc.dram_tensor("d_xhat", [128, Dm], F32, kind="ExternalOutput").ap()
        dbg["lb"] = nc.dram_tensor("d_lb", [3, 128, 4096], BF16, kind="ExternalOutput").ap()
        dbg["fst"] = nc.dram_tensor("d_fst", [128, 8], F32, kind="ExternalOutput").ap()
        dbg["stat"] = nc.dram_tensor("d_stat", [128, 64], F32, kind="ExternalOutput").ap()
        dbg["pm"] = nc.dram_tensor("d_pm", [2, 4096], F32, kind="ExternalOutput").ap()
        dbg["mbc"] = nc.dram_tensor("d_mbc", [128, 1024], F32, kind="ExternalOutput").ap()

    ctxs = []

    def sbuf(name, shape, dtype=F32):
        g = nc.sbuf_tensor(name, shape, dtype)
        t = g.__enter__()
        ctxs.append(g)
        return t

    ident = sbuf("ident", [128, 128])
    cpool = sbuf("cpool", [128, 6])
    onesr = sbuf("onesr", [1, 128])
    stat_m = sbuf("stat_m", [128, 32])
    stat_r = sbuf("stat_r", [128, 32])
    stat_v = sbuf("stat_v", [128, 32])
    wxT = sbuf("sb_wxT", [128, 16 * NPJ])
    wdtT = sbuf("sb_wdtT", [64, DH])
    bdt = sbuf("sb_bdt", [128, 8])
    A_sb = sbuf("A_sb", [128, 128])
    D_sb = sbuf("D_sb", [128, 8])
    scal = sbuf("scal", [128, 4])
    flg = sbuf("flg", [128, 2])
    gpart = sbuf("gpart", [128, 4])
    carry = sbuf("carry", [128, 128])
    fst = sbuf("fst", [128, 8])
    ln2 = sbuf("ln2", [128, 64])
    gsum = sbuf("gsum", [128, 112])
    xproj = sbuf("xproj", [96, L])

    AR = 43008
    arena = sbuf("arena", [128, AR])

    def reg(off, cols):
        assert off + cols <= AR
        return arena[:, off:off + cols]

    def reg3(off, nblk, blk):
        return arena[:, off:off + nblk * blk].rearrange("p (a b) -> p a b", a=nblk)

    def regb(off, f32cols):
        return arena[:, off:off + f32cols].bitcast(BF16)

    W_tl, W_sq, W_s1, W_s2, W_t2 = 0, 4096, 6144, 7168, 8192
    P_xt, P_xnf, P_mb, P_rb, P_dts, P_sq = 17408, 25600, 29696, 30208, 30720, 32768
    S_dt, S_xn, S_u, S_y, S_dA, S_dbx, S_h = 0, 2048, 4096, 6144, 8192, 10240, 12288
    S_y1 = 14336
    S_bcB, S_bcC, S_mx = 16384, 16896, 17408
    F_xg, F_xnt, F_spl, F_lb, F_hst, F_w1 = 19456, 21504, 23552, 24576, 30720, 34816
    C_hld, C_xht, C_spl, C_lb, C_w2, C_acc, C_fin = 0, 8192, 10240, 11264, 23552, 31744, 35840

    ps = []
    for i in range(8):
        g = nc.psum_tensor(f"ps{i}", [128, 512], F32)
        ps.append(g.__enter__())
        ctxs.append(g)

    SEMS = ["s_wl", "s_cst", "s_x", "s_xT", "s_bc", "s_st", "s_pk", "s_u",
            "s_xp", "s_xpe", "s_dtm", "s_dte", "s_p0d",
            "s_ga", "s_gb", "s_gs", "s_sg", "s_t2", "s_wpk",
            "s_ci", "s_cb", "s_cb1", "s_dA", "s_dbx", "s_sc", "s_tmp", "s_ya", "s_cy",
            "s_xg1",
            "s_mt", "s_me", "s_mpk", "s_cc",
            "s_xg", "s_pst", "s_xq", "s_xtr", "s_lb", "s_w1", "s_mm1", "s_he",
            "s_hs", "s_hpk", "s_l2s",
            "s_hl", "s_hq", "s_htr", "s_lb2", "s_w2", "s_mm2", "s_ev2",
            "s_fd", "s_fo", "s_out", "s_dbg", "s_id", "s_x0", "s_x1", "s_ps", "s_bc0", "s_bc1", "s_pxn", "s_pdt", "s_pbc", "s_wl0", "s_wl1", "s_wp0", "s_wp1", "s_w1a", "s_w1b", "s_w2a", "s_w2b", "s_hla", "s_hlb", "s_hp0", "s_hp1", "s_hp2", "s_hp3", "s_vq", "s_sq"]

    P.dma("sync", wxT.rearrange("p (dc c) -> p dc c", dc=16),
          wxT_in.rearrange("(dc p) c -> p dc c", p=128), sem="s_wl")
    P.dma("sync", wdtT[:, :], wdtT_in[:, :], sem="s_wl")
    P.dma("sync", bdt.rearrange("p (g o) -> p g o", g=8),
          bdt_in.rearrange("(g p o) -> p g o", p=128, o=1), sem="s_wl", slow=True)
    P.dma("sync", A_sb.rearrange("p (g n) -> p g n", g=8),
          alog_in.rearrange("(g p) n -> p g n", p=128), sem="s_wl")
    P.dma("sync", D_sb.rearrange("p (g o) -> p g o", g=8),
          dvec_in.rearrange("(g p o) -> p g o", p=128, o=1), sem="s_wl", slow=True)
    P.dma("sync", flg[:, :], hflag_in.broadcast_to([128, 2]), sem="s_wl")
    n_wl0 = P.val("s_wl")

    P.emit("gpsimd", lambda e: e.memset(ident[:, :], 0.0), inc=("s_id", 1))
    P.emit("gpsimd", lambda e: e.affine_select(
        out=ident[:, :], in_=ident[:, :], compare_op=ALU.not_equal, fill=1.0,
        base=0, pattern=[[-1, 128]], channel_multiplier=1),
        waits=[("s_id", 1)], inc=("s_id", 1))
    P.emit("vector", lambda e: e.memset(cpool[:, 0:1], -0.5))
    P.emit("vector", lambda e: e.memset(cpool[:, 1:2], 0.5))
    P.emit("vector", lambda e: e.memset(cpool[:, 2:3], 1.0))
    P.emit("vector", lambda e: e.memset(cpool[:, 3:4], 1e-5))
    P.emit("vector", lambda e: e.memset(cpool[:, 4:5], 1e-6))
    P.emit("vector", lambda e: e.memset(onesr[:, :], 1.0))
    P.emit("vector", lambda e: e.memset(carry[:, :], 0.0))
    P.emit("scalar", lambda e: e.activation(out=A_sb[:, :], in_=A_sb[:, :], func=AF.Exp),
           waits=[("s_wl", n_wl0)], inc=("s_cst", 1))
    P.emit("vector", lambda e: e.tensor_scalar(A_sb[:, :], A_sb[:, :], -1.0, None,
                                               op0=ALU.mult),
           waits=[("s_cst", P.val("s_cst"))], inc=("s_cst", 1))
    n_cst = P.val("s_cst")

    v_st = [0] * NC0
    v_xn = [0] * NC0
    v_mm = [0] * NC0
    v_xpe = [0] * NC0
    v_pk = [0] * NC0
    for c in range(NC0):
        ring = c % 2
        xt = reg(P_xt + ring * 4096, 4096)
        w = [("s_st", v_st[c - 2])] if c >= 2 else []
        sx = "s_x0" if ring == 0 else "s_x1"
        P.dma("sync", reg3(P_xt + ring * 4096, 2, Dm),
              x_in[c * TC0:(c + 1) * TC0, :].rearrange("(tt p) c -> p tt c", p=128),
              waits=w, sem=sx)
        nx = P.val(sx)
        sqv = reg(P_sq, Dm)
        for tt in range(2):
            ttg = c * 2 + tt
            xv = xt[:, tt * Dm:(tt + 1) * Dm]
            mcol = stat_m[:, ttg:ttg + 1]
            rcol = stat_r[:, ttg:ttg + 1]
            P.seq("vector", lambda e, xv=xv, m=mcol: e.tensor_reduce(
                m, xv, axis=AX, op=ALU.add), waits=[(sx, nx)] if tt == 0 else [])
            P.seq("vector", lambda e, xv=xv, r=rcol, sqv=sqv: e.scalar_tensor_tensor(
                sqv, xv, 1.0, xv, op0=ALU.mult, op1=ALU.mult, accum_out=r))
            P.seq("vector", lambda e, m=mcol: e.tensor_scalar(m, m, INV_D, None, op0=ALU.mult))
            P.seq("vector", lambda e, m=mcol, sqv=sqv: e.scalar_tensor_tensor(
                sqv[:, 0:1], m, 1.0, m, op0=ALU.mult, op1=ALU.mult))
            P.seq("vector", lambda e, r=rcol: e.tensor_scalar(r, r, INV_D, None, op0=ALU.mult))
            vcol = stat_v[:, ttg:ttg + 1]
            P.seq("vector", lambda e, r=rcol, v=vcol, sqv=sqv: e.tensor_tensor(
                v, r, sqv[:, 0:1], op=ALU.subtract), inc=("s_st", 1))
            nst0 = P.val("s_st")
            P.seq("scalar", lambda e, r=rcol, v=vcol: e.activation(out=r, in_=v, func=AF.Ln,
                                                                   bias=cpool[:, 3:4]),
                  waits=[("s_st", nst0)])
            P.seq("scalar", lambda e, r=rcol: e.activation(out=r, in_=r, func=AF.Exp,
                                                           scale=cpool[:, 0:1]),
                  inc=("s_st", 1))
        v_st[c] = P.val("s_st")
        w = [("s_st", v_st[c])]
        if c >= 1:
            w.append(("s_bc0" if (c - 1) % 2 == 0 else "s_bc1",
                      32 * ((c - 1) // 2 + 1)))
        P.dma("gpsimd", p_m[0:1, c * TC0:(c + 1) * TC0].rearrange(
                  "o (a p) -> p a o", p=128),
              stat_m[:, c * 2:c * 2 + 2].rearrange("p a -> p a ()"),
              waits=w, sem="s_ps", slow=True)
        P.dma("gpsimd", p_r[0:1, c * TC0:(c + 1) * TC0].rearrange(
                  "o (a p) -> p a o", p=128),
              stat_r[:, c * 2:c * 2 + 2].rearrange("p a -> p a ()"), sem="s_ps", slow=True)
        npk = P.val("s_ps")
        mb = reg(P_mb + ring * TC0, TC0)
        rb = reg(P_rb + ring * TC0, TC0)
        sbc = "s_bc0" if ring == 0 else "s_bc1"
        w = [("s_ps", npk)]
        if c >= 2:
            w.append(("s_u", v_xn[c - 2]))
        P.dma("sync", mb, p_m[0:1, c * TC0:(c + 1) * TC0].broadcast_to([128, TC0]),
              waits=w, sem=sbc)
        P.dma("sync", rb, p_r[0:1, c * TC0:(c + 1) * TC0].broadcast_to([128, TC0]),
              sem=sbc)
        nbc = P.val(sbc)
        w = [(sbc, nbc)]
        if c >= 1:
            w += [("s_xp", v_mm[c - 1]), ("s_pxn", 16 * c)]
        P.dma("sync", reg3(P_xnf, 16, TC0),
              xT_in[:, c * TC0:(c + 1) * TC0].rearrange("(dc p) c -> p dc c", p=128),
              waits=w, sem="s_xT")
        nxt = P.val("s_xT")
        xnf = arena[:, P_xnf:P_xnf + 16 * TC0].rearrange("p (dc c) -> p dc c", dc=16)
        P.emit("vector", lambda e, xnf=xnf, mb=mb: e.tensor_tensor(
            xnf, xnf, grep(mb, 16), op=ALU.subtract),
            waits=[("s_xT", nxt), (sbc, nbc)])
        P.emit("vector", lambda e, xnf=xnf, rb=rb: e.tensor_tensor(
            xnf, xnf, grep(rb, 16), op=ALU.mult), inc=("s_u", 1))
        v_xn[c] = P.val("s_u")
        nu = P.val("s_u")
        for dc in range(16):
            w = []
            if dc == 0:
                w = [("s_u", nu), ("s_wl", n_wl0)]
                if c >= 1:
                    w.append(("s_xpe", v_xpe[c - 1]))
            P.emit("tensor", lambda e, dc=dc: e.matmul(
                ps[6][0:NPJ, 0:TC0], wxT[:, dc * NPJ:(dc + 1) * NPJ],
                arena[:, P_xnf + dc * TC0:P_xnf + (dc + 1) * TC0],
                start=(dc == 0), stop=(dc == 15)),
                waits=w, inc=(("s_xp", 1) if dc == 15 else None))
        v_mm[c] = P.val("s_xp")
        P.emit("scalar", lambda e, c=c: e.copy(
            out=xproj[:, c * TC0:(c + 1) * TC0], in_=ps[6][0:NPJ, 0:TC0]),
            waits=[("s_xp", v_mm[c])], inc=("s_xpe", 1))
        v_xpe[c] = P.val("s_xpe")
        P.dma("gpsimd", p_xn[:, :, c * TC0:(c + 1) * TC0].rearrange("g p c -> p g c"), reg3(P_xnf, 8, TC0),
              waits=[("s_xp", v_mm[c])], sem="s_pxn")
        for g in range(8):
            bank = ps[7] if g % 2 == 0 else ps[5]
            if g < 2:
                w = [("s_xpe", v_xpe[c])]
                if c >= 1:
                    w.append(("s_dte", P.val("s_dte") - (2 - g)))
            else:
                w = [("s_dte", P.val("s_dte") - 1)]
            P.emit("tensor", lambda e, g=g, bank=bank, c=c: e.matmul(
                bank[:, 0:TC0], wdtT[:, g * 128:(g + 1) * 128],
                xproj[0:RK, c * TC0:(c + 1) * TC0], start=True, stop=True),
                waits=w, inc=("s_dtm", 1))
            ndtm = P.val("s_dtm")
            we = [("s_dtm", ndtm)]
            if g == 0:
                we.append(("s_cst", n_cst))
                if c >= 1:
                    we.append(("s_pdt", 16 * c))
            P.emit("scalar", lambda e, g=g, bank=bank: e.activation(
                out=arena[:, P_dts + g * TC0:P_dts + (g + 1) * TC0], in_=bank[:, 0:TC0],
                func=AF.Exp, bias=bdt[:, g:g + 1]), waits=we, inc=("s_dte", 1))
        nd8 = P.val("s_dte")
        P.emit("scalar", lambda e: e.activation(
            out=reg(P_dts, 2048), in_=reg(P_dts, 2048), func=AF.Ln, bias=cpool[:, 2:3]),
            waits=[("s_dte", nd8)], inc=("s_dte", 1))
        P.dma("gpsimd", p_dt[:, :, c * TC0:(c + 1) * TC0].rearrange("g p c -> p g c"), reg3(P_dts, 8, TC0),
              waits=[("s_dte", P.val("s_dte"))], sem="s_pdt")
        v_pk[c] = P.val("s_pdt")
    P.dma("gpsimd", p_bc[:, :], xproj[64:96, :],
          waits=[("s_xpe", P.val("s_xpe"))], sem="s_pbc")
    P.emit("gpsimd", lambda e: e.engine_nop(),
           waits=[("s_ps", P.val("s_ps")), ("s_pxn", P.val("s_pxn")),
                  ("s_pdt", P.val("s_pdt")), ("s_pbc", P.val("s_pbc"))],
           inc=("s_p0d", 1))
    if DEV:
        P.dma("sync", dbg["stat"][:, 0:32], stat_m[:, :], waits=[("s_p0d", 1)], sem="s_dbg")
        P.dma("sync", dbg["stat"][:, 32:64], stat_r[:, :], sem="s_dbg")
        P.dma("sync", dbg["pm"][0:1, :], p_m[:, :], sem="s_dbg")
        P.dma("sync", dbg["pm"][1:2, :], p_r[:, :], sem="s_dbg")
        P.dma("sync", dbg["mbc"][:, 0:512], reg(P_mb, 512), sem="s_dbg")
        P.dma("sync", dbg["mbc"][:, 512:1024], reg(P_rb, 512), sem="s_dbg")
        P.dma("sync", dbg["xproj"][:, :], xproj[:, :],
              waits=[("s_xpe", P.val("s_xpe"))], sem="s_dbg")
        P.dma("sync", dbg["dt"][:, :, :], p_dt[:, :, :], waits=[("s_p0d", 1)], sem="s_dbg")
        P.dma("sync", dbg["xn"][:, :, :], p_xn[:, :, :], waits=[("s_p0d", 1)], sem="s_dbg")

    def wprep(tile_in, park_out, gcol, n_tiles, tw):
        base_wp = (P.val("s_wp0"), P.val("s_wp1"))
        for i in range(n_tiles):
            ring = i % 4
            wv = reg(W_tl + ring * tw, tw)
            swl = "s_wl0" if ring % 2 == 0 else "s_wl1"
            w = ([("s_ga", P.val("s_ga") - 3)] if i >= 4
                 else [("s_sg", P.val("s_sg"))])
            P.dma("sync", wv, tile_in(i), waits=w, sem=swl)
            nw = P.val(swl)
            w = [(swl, nw)]
            if i >= 2:
                w.append(("s_gb", P.val("s_gb") - 1))
            P.emit("scalar", lambda e, wv=wv, i=i: e.activation(
                out=reg(W_sq, tw), in_=wv, func=AF.Abs,
                accum_out=gpart[:, i % 2:i % 2 + 1]), waits=w, inc=("s_ga", 1))
            nga = P.val("s_ga")
            if i == 0:
                P.seq("vector", lambda e: e.tensor_copy(gpart[:, 2:3], gpart[:, 0:1]),
                      waits=[("s_ga", nga), ("s_gs", P.val("s_gs"))], inc=("s_gb", 1))
            else:
                P.seq("vector", lambda e, i=i: e.tensor_tensor(
                    gpart[:, 2:3], gpart[:, 2:3], gpart[:, i % 2:i % 2 + 1], op=ALU.add),
                    waits=[("s_ga", nga)], inc=("s_gb", 1))
        ngb = P.val("s_gb")
        P.emit("tensor", lambda e: e.matmul(
            ps[4][0:1, 0:1], gpart[:, 2:3], cpool[:, 2:3], start=True, stop=True),
            waits=[("s_gb", ngb), ("s_cst", n_cst)], inc=("s_gs", 1))
        ngs = P.val("s_gs")
        ntot = float(n_tiles) * 128.0 * float(tw)
        P.seq("vector", lambda e: e.tensor_scalar(
            gpart[0:1, 3:4], ps[4][0:1, 0:1], 1.0 / ntot, None, op0=ALU.mult),
            waits=[("s_gs", ngs)])
        P.seq("vector", lambda e: e.tensor_scalar(
            gpart[0:1, 2:3], gpart[0:1, 3:4], 0.5, None, op0=ALU.mult))
        P.seq("vector", lambda e: e.tensor_scalar(
            gpart[0:1, 3:4], gpart[0:1, 3:4], 1e-6, None, op0=ALU.add))
        P.seq("vector", lambda e: e.reciprocal(gpart[0:1, 3:4], gpart[0:1, 3:4]),
              inc=("s_gs", 1))
        ngs2 = P.val("s_gs")
        P.emit("tensor", lambda e: e.matmul(
            ps[4][:, 4:6], onesr[:, :], gpart[0:1, 2:4], start=True, stop=True),
            waits=[("s_gs", ngs2)], inc=("s_gs", 1))
        ngs3 = P.val("s_gs")
        P.emit("scalar", lambda e, gcol=gcol: e.copy(out=scal[:, gcol * 2:gcol * 2 + 2],
                                                     in_=ps[4][:, 4:6]),
               waits=[("s_gs", ngs3)], inc=("s_gs", 1))
        ngs4 = P.val("s_gs")
        for i in range(n_tiles):
            ring = i % 4
            wv = reg(W_tl + ring * tw, tw)
            swl = "s_wl0" if ring % 2 == 0 else "s_wl1"
            w = [("s_sg", P.val("s_sg") - 6)] if i >= 4 else [("s_gs", ngs4)]
            P.dma("sync", wv, tile_in(i), waits=w, sem=swl)
            nw = P.val(swl)
            wsg = regb(W_s1, tw // 2)
            wsg2 = regb(W_s2, tw // 2)
            w = [(swl, nw)]
            if i == 0:
                w.append(("s_gs", ngs4))
            if i >= 1:
                w.append(("s_t2", P.val("s_t2")))
            P.emit("scalar", lambda e, wv=wv, wsg=wsg, gcol=gcol: e.activation(
                out=wsg, in_=wv, func=AF.Sign, bias=cpool[:, 0:1],
                scale=scal[:, gcol * 2 + 1:gcol * 2 + 2]), waits=w, inc=("s_sg", 1))
            P.emit("scalar", lambda e, wv=wv, wsg2=wsg2, gcol=gcol: e.activation(
                out=wsg2, in_=wv, func=AF.Sign, bias=cpool[:, 1:2],
                scale=scal[:, gcol * 2 + 1:gcol * 2 + 2]), inc=("s_sg", 1))
            nsg = P.val("s_sg")
            t2v = regb(W_t2 + ring * (tw // 2), tw // 2)
            swp = "s_wp0" if ring % 2 == 0 else "s_wp1"
            w = [("s_sg", nsg)]
            if i >= 4:
                w.append((swp, base_wp[ring % 2] + 16 * (i // 2 - 1)))
            else:
                w.append((swp, base_wp[ring % 2]))
            P.emit("vector", lambda e, t2v=t2v, wsg=wsg, wsg2=wsg2: e.tensor_tensor(
                t2v, wsg, wsg2, op=ALU.add), waits=w, inc=("s_t2", 1))
            nt2 = P.val("s_t2")
            P.dma("gpsimd", park_out(i), t2v, waits=[("s_t2", nt2)], sem=swp)

    if PH < 1:
        for tt in range(16):
            P.dma("sync", out_hbm[tt * 128:(tt + 1) * 128, :],
                  xm_in[tt * 128:(tt + 1) * 128, :], sem="s_out")
        wend = [("s_out", P.val("s_out"))]
        if DEV:
            wend.append(("s_dbg", P.val("s_dbg")))
        P.emit("sync", lambda e: e.nop(), waits=wend)
        return _finish(nc, P, SEMS, ctxs)

    def wfc_tile(i):
        dc, oc = i // 8, i % 8
        return wfcT_in[dc * 128:(dc + 1) * 128, oc * 1024:(oc + 1) * 1024]

    def wfc_park(i):
        dc, oc = i // 8, i % 8
        return p_w1[oc, dc, :, :]

    wprep(wfc_tile, wfc_park, 0, 128, 1024)
    n_gs_fc = P.val("s_gs")
    n_w1pk = (P.val("s_wp0"), P.val("s_wp1"))
    if DEV:
        P.dma("sync", dbg["gam"][0:1, 0:4], gpart[0:1, 0:4],
              waits=[("s_gs", P.val("s_gs"))], sem="s_dbg")
        P.dma("sync", dbg["w1"][:, :, :], p_w1[0, :, :, :],
              waits=[("s_wp0", n_w1pk[0]), ("s_wp1", n_w1pk[1])], sem="s_dbg")
    wprep(lambda i: wpjT_in[(i // 2) * 128:(i // 2 + 1) * 128,
                            (i % 2) * 1024:(i % 2 + 1) * 1024],
          lambda i: p_w2[i // 2, :, (i % 2) * 1024:(i % 2 + 1) * 1024], 1, 128, 1024)
    n_gs_pj = P.val("s_gs")
    n_wpk = (P.val("s_wp0"), P.val("s_wp1"))

    if PH < 2:
        for tt in range(16):
            P.dma("sync", out_hbm[tt * 128:(tt + 1) * 128, :],
                  xm_in[tt * 128:(tt + 1) * 128, :],
                  waits=([("s_wp0", P.val("s_wp0")), ("s_wp1", P.val("s_wp1"))]
                         if tt == 0 else []), sem="s_out")
        wend = [("s_out", P.val("s_out"))]
        if DEV:
            wend.append(("s_dbg", P.val("s_dbg")))
        P.emit("sync", lambda e: e.nop(), waits=wend)
        return _finish(nc, P, SEMS, ctxs)

    v_mt_ch = [0] * NCH
    v_yD_ch = [0] * NCH
    ya_hist = []
    v_nxg = [0] * 16
    v_xtr = [0] * 16

    def plan_xg_prefetch(g):
        for tt in range(2):
            ttg = g * 2 + tt
            slot = F_xg if tt == 0 else F_xnt
            sxg = "s_xg" if tt == 0 else "s_xg1"
            w = [("s_p0d", 1)] if g == 0 else [("s_xtr", v_xtr[ttg - 2])]
            P.dma("sync", reg(slot, Dm), xm_in[ttg * 128:(ttg + 1) * 128, :],
                  waits=w, sem=sxg)
            v_nxg[ttg] = P.val(sxg)

    def plan_scan_chunk(c):
        w = [("s_wp0", n_wpk[0]), ("s_wp1", n_wpk[1]), ("s_p0d", 1)]
        if c >= 1:
            w += [("s_dA", P.val("s_dA")), ("s_ya", P.val("s_ya"))]
        P.dma("sync", reg3(S_dt, 8, TC), p_dt[:, :, c * TC:(c + 1) * TC].rearrange("g p c -> p g c"),
              waits=w, sem="s_ci")
        P.dma("sync", reg3(S_xn, 8, TC), p_xn[:, :, c * TC:(c + 1) * TC].rearrange("g p c -> p g c"), sem="s_ci")
        nci = P.val("s_ci")
        P.emit("vector", lambda e: e.tensor_tensor(
            reg(S_u, 2048), reg(S_dt, 2048), reg(S_xn, 2048), op=ALU.mult),
            waits=[("s_ci", nci)], inc=("s_u", 1))
        nu = P.val("s_u")
        w = []
        if c >= 2:
            w.append(("s_mt", v_mt_ch[c - 2]))
        yoff = S_y if c % 2 == 0 else S_y1
        yv = reg(yoff, 2048)
        P.emit("gpsimd", lambda e, yv=yv: e.memset(yv, 0.0), waits=w, inc=("s_cy", 1))
        ncy = P.val("s_cy")
        for n in range(NS):
            gidx = c * NS + n
            par = n % 2
            scb = "s_cb" if par == 0 else "s_cb1"
            bcoff = par * 256
            w = []
            if gidx >= 2:
                w.append(("s_ya", ya_hist[gidx - 2]))
            P.dma("sync", reg(S_bcB + bcoff, TC),
                  p_bc[n:n + 1, c * TC:(c + 1) * TC].broadcast_to([128, TC]),
                  waits=w, sem=scb)
            P.dma("sync", reg(S_bcC + bcoff, TC),
                  p_bc[16 + n:17 + n, c * TC:(c + 1) * TC].broadcast_to([128, TC]),
                  sem=scb)
            ncb = P.val(scb)
            w = [("s_ci", nci), ("s_cst", n_cst)]
            if c > 0 or n > 0:
                w.append(("s_sc", P.val("s_sc")))
            for g in range(8):
                P.emit("scalar", lambda e, g=g, n=n: e.activation(
                    out=arena[:, S_dA + g * TC:S_dA + (g + 1) * TC],
                    in_=arena[:, S_dt + g * TC:S_dt + (g + 1) * TC],
                    func=AF.Exp, scale=A_sb[:, g * 16 + n:g * 16 + n + 1]),
                    waits=(w if g == 0 else []), inc=("s_dA", 1) if g == 7 else None)
            ndA = P.val("s_dA")
            bcb = reg(S_bcB + bcoff, TC)
            P.emit("vector", lambda e, bcb=bcb: e.tensor_tensor(
                reg3(S_dbx, 8, TC), reg3(S_u, 8, TC), grep(bcb, 8), op=ALU.mult),
                waits=[("s_u", nu), (scb, ncb)])
            w = [("s_dA", ndA)]
            for g in range(8):
                P.emit("vector", lambda e, g=g, n=n: e.tensor_tensor_scan(
                    arena[:, S_h + g * TC:S_h + (g + 1) * TC],
                    arena[:, S_dA + g * TC:S_dA + (g + 1) * TC],
                    arena[:, S_dbx + g * TC:S_dbx + (g + 1) * TC],
                    carry[:, n * 8 + g:n * 8 + g + 1],
                    op0=ALU.mult, op1=ALU.add),
                    waits=(w if g == 0 else []),
                    inc=(("s_vq", 1) if g == 7 else None))
            nvq = P.val("s_vq")
            hsl = arena[:, S_h:S_h + 2048]
            csrc = bass.AP(hsl.tensor, hsl.offset + TC - 1, [list(hsl.ap[0]), [TC, 8]])
            P.emit("vector", lambda e, n=n, csrc=csrc: e.tensor_copy(
                carry[:, n * 8:n * 8 + 8], csrc),
                waits=[("s_vq", nvq)], inc=("s_sc", 1))
            bcc = reg(S_bcC + bcoff, TC)
            P.emit("vector", lambda e, bcc=bcc: e.tensor_tensor(
                reg3(S_dbx, 8, TC), reg3(S_h, 8, TC), grep(bcc, 8), op=ALU.mult))
            P.emit("vector", lambda e, yv=yv: e.tensor_tensor(
                yv, yv, reg(S_dbx, 2048), op=ALU.add),
                waits=([("s_cy", ncy)] if n == 0 else []), inc=("s_ya", 1))
            ya_hist.append(P.val("s_ya"))
        for g in range(8):
            P.emit("vector", lambda e, g=g, yv=yv: e.scalar_tensor_tensor(
                yv[:, g * TC:(g + 1) * TC], arena[:, S_xn + g * TC:S_xn + (g + 1) * TC],
                D_sb[:, g:g + 1], yv[:, g * TC:(g + 1) * TC], op0=ALU.mult, op1=ALU.add),
                inc=("s_ya", 1) if g == 7 else None)
        v_yD_ch[c] = P.val("s_ya")

    def plan_mix_T(c):
        yoff = S_y if c % 2 == 0 else S_y1
        yv = reg(yoff, 2048)
        nyD = v_yD_ch[c]
        nmpk_prev = P.val("s_mpk")
        for tt2 in range(2):
            for gb in range(2):
                bank = ps[6 + tt2]
                for gg in range(4):
                    g = gb * 4 + gg
                    w = []
                    if gg == 0:
                        w = [("s_ya", nyD), ("s_me", P.val("s_me"))]
                        if c == 0 and tt2 == 0 and gb == 0:
                            w.append(("s_id", 2))
                    P.emit("tensor", lambda e, g=g, tt2=tt2, gg=gg, bank=bank, yv=yv:
                           e.transpose(bank[:, gg * 128:(gg + 1) * 128],
                                       yv[:, g * TC + tt2 * 128:g * TC + tt2 * 128 + 128],
                                       ident[:, :]),
                           waits=w, inc=("s_mt", 1) if gg == 3 else None)
                nmt = P.val("s_mt")
                w = [("s_mt", nmt)]
                if tt2 == 0 and gb == 0:
                    w.append(("s_mpk", nmpk_prev))
                    if c == 0:
                        w.append(("s_p0d", 1))
                P.emit("scalar", lambda e, tt2=tt2, gb=gb, bank=bank: e.copy(
                    out=reg(S_mx + tt2 * 1024 + gb * 512, 512), in_=bank[:, 0:512]),
                    waits=w, inc=("s_me", 1))
        v_mt_ch[c] = P.val("s_mt")
        nme = P.val("s_me")
        P.dma("gpsimd", cc_i[c * TC:(c + 1) * TC, :].rearrange("(tt p) d -> p tt d", p=128),
              reg3(S_mx, 2, 1024), waits=[("s_me", nme)], sem="s_mpk")

    w1_gate = {}
    bl1_defer = {}

    def emit_piece_stats(g, op8, tt, sl, nhe):
        hp = reg(F_hst + sl * 1024, 1024)
        P.seq("vector", lambda e, hp=hp, tt=tt, op8=op8: e.scalar_tensor_tensor(
            reg(S_dbx, 1024), hp, 1.0, hp, op0=ALU.mult, op1=ALU.mult,
            accum_out=gsum[:, 32 + tt * 8 + op8:33 + tt * 8 + op8]),
            waits=[("s_he", nhe)])
        P.seq("vector", lambda e, hp=hp, tt=tt, op8=op8: e.tensor_reduce(
            gsum[:, 48 + tt * 8 + op8:49 + tt * 8 + op8], hp, axis=AX,
            op=ALU.max))
        P.seq("vector", lambda e, hp=hp, tt=tt, op8=op8: e.tensor_reduce(
            gsum[:, 64 + tt * 8 + op8:65 + tt * 8 + op8], hp, axis=AX,
            op=ALU.min), inc=("s_hs", 1))
        nhs = P.val("s_hs")
        P.dma("gpsimd", p_h[g, tt, :, op8 * 1024:(op8 + 1) * 1024], hp,
              waits=[("s_hs", nhs)], sem=f"s_hp{sl}")

    def plan_bl1_defer(g):
        for it in bl1_defer.pop(g):
            if it[0] == "stats":
                _, op8, tt, sl, nhe = it
                emit_piece_stats(g, op8, tt, sl, nhe)
            else:
                _, op8, og, tt, bank, sl, ogidx, nmm = it
                w = [("s_mm1", nmm)]
                slotsem = f"s_hp{sl}"
                if og == 0:
                    w.append((slotsem, P.val(slotsem)))
                P.emit("scalar", lambda e, bank=bank, sl=sl, og=og, tt=tt,
                       ogidx=ogidx: e.activation(
                    out=arena[:, F_hst + sl * 1024 + og * 512:
                              F_hst + sl * 1024 + og * 512 + 512],
                    in_=bank[:, :], func=AF.Gelu,
                    accum_out=gsum[:, tt * 16 + ogidx:tt * 16 + ogidx + 1]),
                    waits=w, inc=("s_he", 1))
                if og == 1:
                    emit_piece_stats(g, op8, tt, sl, P.val("s_he"))
        nhs2 = P.val("s_hs")
        for tt in range(2):
            gl = g * 2 + tt
            Lc = ln2[:, gl * 4:gl * 4 + 4]
            P.seq("vector", lambda e, Lc=Lc, tt=tt: e.tensor_reduce(
                Lc[:, 0:1], gsum[:, tt * 16:tt * 16 + 16], axis=AX, op=ALU.add),
                waits=[("s_hs", nhs2)])
            P.seq("vector", lambda e, Lc=Lc: e.tensor_scalar(
                Lc[:, 0:1], Lc[:, 0:1], INV_H, None, op0=ALU.mult))
            P.seq("vector", lambda e, Lc=Lc: e.scalar_tensor_tensor(
                gsum[:, 96:97], Lc[:, 0:1], 1.0, Lc[:, 0:1], op0=ALU.mult, op1=ALU.mult))
            P.seq("vector", lambda e, tt=tt: e.tensor_reduce(
                gsum[:, 97:98], gsum[:, 32 + tt * 8:40 + tt * 8], axis=AX, op=ALU.add))
            P.seq("vector", lambda e, Lc=Lc: e.tensor_scalar(
                Lc[:, 1:2], gsum[:, 97:98], INV_H, None, op0=ALU.mult))
            P.seq("vector", lambda e, Lc=Lc: e.tensor_tensor(
                Lc[:, 1:2], Lc[:, 1:2], gsum[:, 96:97], op=ALU.subtract), inc=("s_l2s", 1))
            nl2a = P.val("s_l2s")
            P.seq("scalar", lambda e, Lc=Lc: e.activation(
                out=Lc[:, 1:2], in_=Lc[:, 1:2], func=AF.Ln, bias=cpool[:, 4:5]),
                waits=[("s_l2s", nl2a)])
            P.seq("scalar", lambda e, Lc=Lc: e.activation(
                out=Lc[:, 1:2], in_=Lc[:, 1:2], func=AF.Exp, scale=cpool[:, 0:1]),
                inc=("s_l2s", 1))
            nl2b = P.val("s_l2s")
            P.seq("vector", lambda e, tt=tt: e.tensor_reduce(
                gsum[:, 98:99], gsum[:, 48 + tt * 8:56 + tt * 8], axis=AX, op=ALU.max),
                waits=[("s_l2s", nl2b)])
            P.seq("vector", lambda e, tt=tt: e.tensor_reduce(
                gsum[:, 99:100], gsum[:, 64 + tt * 8:72 + tt * 8], axis=AX, op=ALU.min))
            P.seq("vector", lambda e, Lc=Lc: e.tensor_tensor(
                gsum[:, 98:99], gsum[:, 98:99], Lc[:, 0:1], op=ALU.subtract))
            P.seq("vector", lambda e, Lc=Lc: e.tensor_tensor(
                gsum[:, 99:100], Lc[:, 0:1], gsum[:, 99:100], op=ALU.subtract))
            P.seq("vector", lambda e: e.tensor_tensor(
                gsum[:, 98:99], gsum[:, 98:99], gsum[:, 99:100], op=ALU.max))
            P.seq("vector", lambda e, Lc=Lc: e.tensor_tensor(
                gsum[:, 98:99], gsum[:, 98:99], Lc[:, 1:2], op=ALU.mult))
            P.seq("vector", lambda e, Lc=Lc: e.reciprocal(Lc[:, 2:3], gsum[:, 98:99]))
            P.seq("vector", lambda e, Lc=Lc: e.tensor_scalar(
                Lc[:, 2:3], Lc[:, 2:3], 128.0, None, op0=ALU.mult))
            P.seq("vector", lambda e, Lc=Lc: e.tensor_scalar(
                Lc[:, 3:4], gsum[:, 98:99], 1.0 / 128.0, None, op0=ALU.mult))
            P.seq("vector", lambda e, Lc=Lc: e.tensor_tensor(
                Lc[:, 3:4], Lc[:, 3:4], scal[:, 2:3], op=ALU.mult),
                waits=[("s_gs", n_gs_pj)])
            P.seq("vector", lambda e, Lc=Lc: e.tensor_scalar(
                Lc[:, 0:1], Lc[:, 0:1], -1.0, None, op0=ALU.mult), inc=("s_l2s", 1))

    def plan_bl1_group(g):
        F = fst
        for tt in range(2):
            ttg = g * 2 + tt
            xnv = reg(F_xg if tt == 0 else F_xnt, Dm)
            nxg = v_nxg[ttg]
            sxg = "s_xg" if tt == 0 else "s_xg1"
            P.seq("vector", lambda e, ttg=ttg: e.tensor_tensor(
                F[:, 0:1], stat_m[:, ttg:ttg + 1], flg[:, 0:1], op=ALU.mult),
                waits=[(sxg, nxg), ("s_wl", n_wl0)])
            P.seq("vector", lambda e, ttg=ttg: e.tensor_tensor(
                F[:, 2:3], stat_m[:, 16 + ttg:17 + ttg], flg[:, 1:2], op=ALU.mult))
            P.seq("vector", lambda e: e.tensor_tensor(
                F[:, 0:1], F[:, 0:1], F[:, 2:3], op=ALU.add))
            P.seq("vector", lambda e: e.tensor_scalar(
                F[:, 0:1], F[:, 0:1], -1.0, None, op0=ALU.mult))
            P.seq("vector", lambda e, ttg=ttg: e.tensor_tensor(
                F[:, 1:2], stat_r[:, ttg:ttg + 1], flg[:, 0:1], op=ALU.mult))
            P.seq("vector", lambda e, ttg=ttg: e.tensor_tensor(
                F[:, 2:3], stat_r[:, 16 + ttg:17 + ttg], flg[:, 1:2], op=ALU.mult))
            P.seq("vector", lambda e: e.tensor_tensor(
                F[:, 1:2], F[:, 1:2], F[:, 2:3], op=ALU.add))
            P.seq("vector", lambda e, xnv=xnv: e.tensor_scalar(
                xnv, xnv, F[:, 0:1], F[:, 1:2], op0=ALU.add, op1=ALU.mult))
            P.seq("vector", lambda e, xnv=xnv: e.tensor_reduce(
                F[:, 4:5], xnv, axis=AX, op=ALU.max))
            P.seq("vector", lambda e, xnv=xnv: e.tensor_reduce(
                F[:, 5:6], xnv, axis=AX, op=ALU.min))
            P.seq("vector", lambda e: e.tensor_scalar(F[:, 5:6], F[:, 5:6], -1.0, None,
                                                       op0=ALU.mult))
            P.seq("vector", lambda e: e.tensor_tensor(F[:, 4:5], F[:, 4:5], F[:, 5:6],
                                                       op=ALU.max))
            P.seq("vector", lambda e, ttg=ttg: e.tensor_tensor(
                F[:, 2:3], stat_v[:, ttg:ttg + 1], flg[:, 0:1], op=ALU.mult))
            P.seq("vector", lambda e, ttg=ttg: e.tensor_tensor(
                F[:, 3:4], stat_v[:, 16 + ttg:17 + ttg], flg[:, 1:2], op=ALU.mult))
            P.seq("vector", lambda e: e.tensor_tensor(
                F[:, 2:3], F[:, 2:3], F[:, 3:4], op=ALU.add))
            P.seq("vector", lambda e: e.tensor_tensor(
                F[:, 3:4], F[:, 1:2], F[:, 1:2], op=ALU.mult))
            P.seq("vector", lambda e: e.tensor_tensor(
                F[:, 3:4], F[:, 3:4], F[:, 2:3], op=ALU.mult), inc=("s_pst", 1))
            np3 = P.val("s_pst")
            P.seq("scalar", lambda e: e.activation(out=F[:, 3:4], in_=F[:, 3:4],
                                                    func=AF.Ln, bias=cpool[:, 4:5]),
                   waits=[("s_pst", np3)])
            P.seq("scalar", lambda e: e.activation(out=F[:, 3:4], in_=F[:, 3:4],
                                                    func=AF.Exp, scale=cpool[:, 0:1]),
                   inc=("s_pst", 1))
            np3b = P.val("s_pst")
            P.seq("vector", lambda e: e.reciprocal(F[:, 6:7], F[:, 4:5]))
            P.seq("vector", lambda e: e.tensor_scalar(F[:, 6:7], F[:, 6:7], 128.0, None,
                                                       op0=ALU.mult))
            P.seq("vector", lambda e: e.tensor_scalar(F[:, 7:8], F[:, 4:5], 1.0 / 128.0,
                                                       None, op0=ALU.mult))
            P.seq("vector", lambda e, xnv=xnv: e.tensor_scalar(
                xnv, xnv, F[:, 6:7], CLIP, op0=ALU.mult, op1=ALU.min))
            P.seq("vector", lambda e, xnv=xnv: e.tensor_scalar(
                xnv, xnv, -CLIP, None, op0=ALU.max))
            P.seq("vector", lambda e: e.tensor_tensor(F[:, 7:8], F[:, 7:8],
                                                       F[:, 3:4], op=ALU.mult),
                   waits=[("s_pst", np3b)])
            P.seq("vector", lambda e: e.tensor_tensor(F[:, 7:8], F[:, 7:8],
                                                       scal[:, 0:1], op=ALU.mult),
                   waits=[("s_gs", n_gs_fc)])
            P.seq("vector", lambda e, xnv=xnv: e.tensor_scalar(
                xnv, xnv, F[:, 7:8], None, op0=ALU.mult), inc=("s_xq", 1))
            nxq = P.val("s_xq")
            if DEV and g == 0 and tt == 0:
                P.dma("sync", dbg["xhat"][:, :], xnv, waits=[("s_xq", nxq)], sem="s_dbg")
                P.dma("sync", dbg["fst"][:, :], F[:, 0:8], waits=[("s_xq", nxq)], sem="s_dbg")
            for qb in range(4):
                bank = ps[4 + qb % 2]
                w = [("s_xq", nxq), ("s_lb", max(0, P.val("s_lb") - 6)),
                     ("s_mm1", P.val("s_mm1")), ("s_id", 2)]
                for k in range(4):
                    dc = qb * 4 + k
                    P.emit("tensor", lambda e, dc=dc, k=k, bank=bank, xnv=xnv: e.transpose(
                        bank[:, k * 128:(k + 1) * 128],
                        xnv[:, dc * 128:(dc + 1) * 128], ident[:, :]),
                        waits=(w if k == 0 else []),
                        inc=("s_xtr", 1) if k == 3 else None)
                nxtr = P.val("s_xtr")

                def lbap(l, qb=qb, tt=tt):
                    v = regb(F_lb + l * 2048, 2048)
                    return bass.AP(v.tensor, v.offset + qb * 4 * 256 + tt * 128,
                                   [list(v.ap[0]), [256, 4], [1, 128]])
                bv = bank[:, :]
                psv = bass.AP(bv.tensor, bv.offset, [list(bv.ap[0]), [128, 4], [1, 128]])
                r1 = reg(F_spl, 1024)
                r1v = bass.AP(r1.tensor, r1.offset, [list(r1.ap[0]), [128, 4], [1, 128]])
                r2v = bass.AP(r1.tensor, r1.offset + 512, [list(r1.ap[0]), [128, 4], [1, 128]])
                P.emit("scalar", lambda e, lb=lbap(0), psv=psv: e.copy(out=lb, in_=psv),
                       waits=[("s_xtr", nxtr)], inc=("s_lb", 1))
                nl1 = P.val("s_lb")
                P.emit("vector", lambda e, lb=lbap(0), psv=psv, r1v=r1v: e.scalar_tensor_tensor(
                    r1v, lb, -1.0, psv, op0=ALU.mult, op1=ALU.add),
                    waits=[("s_lb", nl1)], inc=("s_lb", 1))
                nl2 = P.val("s_lb")
                P.emit("scalar", lambda e, lb=lbap(1), r1v=r1v: e.copy(out=lb, in_=r1v),
                       waits=[("s_lb", nl2)], inc=("s_lb", 1))
                nl3 = P.val("s_lb")
                P.emit("vector", lambda e, lb=lbap(1), r1v=r1v, r2v=r2v: e.scalar_tensor_tensor(
                    r2v, lb, -1.0, r1v, op0=ALU.mult, op1=ALU.add),
                    waits=[("s_lb", nl3)], inc=("s_lb", 1))
                nl4 = P.val("s_lb")
                P.emit("scalar", lambda e, lb=lbap(2), r2v=r2v: e.copy(out=lb, in_=r2v),
                       waits=[("s_lb", nl4)], inc=("s_lb", 1))
            v_xtr[ttg] = P.val("s_xtr")
        nlball = P.val("s_lb")
        if DEV and g == 0:
            for l in range(3):
                P.dma("sync", dbg["lb"][l, :, :], regb(F_lb + l * 2048, 2048),
                      waits=[("s_lb", nlball)], sem="s_dbg")
        he_base = P.val("s_he")
        defer = []
        for op8 in range(8):
            for og in range(2):
                half = og % 2
                sw1 = "s_w1a" if half == 0 else "s_w1b"
                w = [("s_wp0", n_w1pk[0]), ("s_wp1", n_w1pk[1])]
                key = ("w1half", half)
                prev_waves = w1_gate.get(key)
                if prev_waves is not None:
                    w.append(("s_mm1", prev_waves))
                P.dma("sync", regb(F_w1 + half * 4096, 4096).rearrange(
                    "p (dc c) -> p dc c", dc=16),
                    p_w1[op8, :, :, og * 512:(og + 1) * 512].rearrange(
                        "a p c -> p a c"), waits=w, sem=sw1)
                nw1 = P.val(sw1)
                w1v = regb(F_w1 + half * 4096, 4096)
                for tt in range(2):
                    bank = ps[og * 2 + tt]
                    first = True
                    for dc in range(16):
                        for lmb in range(3):
                            lb = regb(F_lb + lmb * 2048, 2048)
                            stat = bass.AP(lb.tensor, lb.offset + dc * 256 + tt * 128,
                                           [list(lb.ap[0]), [1, 128]])
                            mov = bass.AP(w1v.tensor, w1v.offset + dc * 512,
                                          [list(w1v.ap[0]), [1, 512]])
                            w = []
                            if first:
                                w = [("s_lb", nlball), (sw1, nw1),
                                     ("s_he", P.val("s_he"))]
                                first = False
                            P.emit("tensor", lambda e, stat=stat, mov=mov, bank=bank,
                                   st=(dc == 0 and lmb == 0),
                                   sp=(dc == 15 and lmb == 2): e.matmul(
                                bank[:, :], stat, mov, start=st, stop=sp),
                                waits=w, inc=("s_mm1", 1) if (dc == 15 and lmb == 2) else None)
                    nmm = P.val("s_mm1")
                    if tt == 1:
                        w1_gate[("w1half", half)] = nmm
                    ogidx = op8 * 2 + og
                    sl = (op8 * 2 + tt) % 4
                    if op8 >= 7:
                        defer.append(("evac", op8, og, tt, bank, sl, ogidx, nmm))
                        continue
                    w = [("s_mm1", nmm)]
                    slotsem = f"s_hp{sl}"
                    if P.val(slotsem) > 0 and og == 0:
                        w.append((slotsem, P.val(slotsem)))
                    P.emit("scalar", lambda e, bank=bank, sl=sl, og=og, tt=tt,
                           ogidx=ogidx: e.activation(
                        out=arena[:, F_hst + sl * 1024 + og * 512:
                                  F_hst + sl * 1024 + og * 512 + 512],
                        in_=bank[:, :], func=AF.Gelu,
                        accum_out=gsum[:, tt * 16 + ogidx:tt * 16 + ogidx + 1]),
                        waits=w, inc=("s_he", 1))
                    nhe = P.val("s_he")
                    if og == 1:
                        if op8 >= 5:
                            defer.append(("stats", op8, tt, sl, nhe))
                            continue
                        emit_piece_stats(g, op8, tt, sl, nhe)
        bl1_defer[g] = defer
    for g in range(NG):
        plan_scan_chunk(2 * g)
        if PH >= 3 and g >= 1:
            plan_bl1_defer(g - 1)
        if PH >= 3:
            plan_xg_prefetch(g)
        plan_scan_chunk(2 * g + 1)
        plan_mix_T(2 * g)
        plan_mix_T(2 * g + 1)
        if PH >= 3:
            plan_bl1_group(g)
    if PH >= 3:
        plan_bl1_defer(NG - 1)
    n_mpk = P.val("s_mpk")
    n_hpk = [P.val(f"s_hp{i}") for i in range(4)]
    if DEV:
        P.dma("sync", dbg["mix"][:, :], cc_i[:, :], waits=[("s_mpk", n_mpk)], sem="s_dbg")
        P.dma("sync", dbg["h0"][:, :, :], p_h[0, :, :, :],
              waits=[(f"s_hp{i}", n_hpk[i]) for i in range(4)], sem="s_dbg")

    if PH < 4:
        wend = [("s_mpk", n_mpk), ("s_sc", P.val("s_sc")),
                ("s_ya", P.val("s_ya")), ("s_me", P.val("s_me"))] + \
            [(f"s_hp{i}", n_hpk[i]) for i in range(4)]
        for tt in range(16):
            P.dma("sync", out_hbm[tt * 128:(tt + 1) * 128, :],
                  xm_in[tt * 128:(tt + 1) * 128, :],
                  waits=(wend if tt == 0 else []), sem="s_out")
        wend2 = [("s_out", P.val("s_out"))]
        if DEV:
            wend2.append(("s_dbg", P.val("s_dbg")))
        P.emit("sync", lambda e: e.nop(), waits=wend2)
        return _finish(nc, P, SEMS, ctxs)

    if SIM_NOCC:
        P.dma("gpsimd", cc_o[0, :, :], cc_i[:, :], waits=[("s_mpk", n_mpk)], sem="s_mpk")
        P.dma("gpsimd", cc_o[1, :, :], cc_i[:, :], sem="s_mpk")
        P.emit("gpsimd", lambda e: e.engine_nop(),
               waits=[("s_mpk", P.val("s_mpk"))], inc=("s_cc", 1))
    else:
        P.emit("gpsimd", lambda e: e.collective_compute(
            "AllGather", ALU.bypass, ins=[cc_i[:, :]], outs=[cc_o[:, :, :]],
            replica_groups=[[0, 1], [2, 3], [4, 5], [6, 7]]),
            waits=[("s_mpk", n_mpk)], inc=("s_cc", 1))

    guard = [("s_sc", P.val("s_sc")), ("s_ya", P.val("s_ya")),
             ("s_me", P.val("s_me")), ("s_he", P.val("s_he")),
             ("s_hs", P.val("s_hs")), ("s_mpk", n_mpk),
             ("s_lb", P.val("s_lb")), ("s_tmp", P.val("s_tmp")),
             ("s_mm1", P.val("s_mm1")), ("s_l2s", P.val("s_l2s"))] + \
        [(f"s_hp{i}", n_hpk[i]) for i in range(4)] + \
        [("s_wp0", n_wpk[0]), ("s_wp1", n_wpk[1])]
    w2_gate = {}
    lb2_gate = {}
    for g in range(NG):
        for s in range(NSL):
            ring = s % 2
            shl = "s_hla" if ring == 0 else "s_hlb"
            w = list(guard) if (g == 0 and s == 0) else []
            if g > 0 or s >= 2:
                w.append(("s_lb2", max(0, P.val("s_lb2") - 40)))
            P.dma("sync", reg3(C_hld + ring * 4096, 2, Dm),
                  p_h[g, :, :, s * Dm:(s + 1) * Dm].rearrange("a p c -> p a c"),
                  waits=w, sem=shl)
            nhl = P.val(shl)
            lbslot = (g * NSL + s) % 2
            for tt in range(2):
                gl = g * 2 + tt
                Lc = ln2[:, gl * 4:gl * 4 + 4]
                hv = reg(C_hld + ring * 4096 + tt * Dm, Dm)
                qv = reg(C_xht, Dm)
                w = [(shl, nhl)]
                if g > 0 or s > 0 or tt > 0:
                    w.append(("s_lb2", P.val("s_lb2")))
                P.emit("vector", lambda e, hv=hv, qv=qv, Lc=Lc: e.tensor_scalar(
                    qv, hv, Lc[:, 0:1], Lc[:, 1:2], op0=ALU.add, op1=ALU.mult), waits=w)
                P.emit("vector", lambda e, qv=qv, Lc=Lc: e.tensor_scalar(
                    qv, qv, Lc[:, 2:3], CLIP, op0=ALU.mult, op1=ALU.min))
                P.emit("vector", lambda e, qv=qv, Lc=Lc: e.tensor_scalar(
                    qv, qv, -CLIP, Lc[:, 3:4], op0=ALU.max, op1=ALU.mult),
                    inc=("s_hq", 1))
                nhq = P.val("s_hq")
                for qb in range(4):
                    bank = ps[4 + qb % 2]
                    w = [("s_hq", nhq), ("s_lb2", max(0, P.val("s_lb2") - 6)),
                         ("s_mm2", lb2_gate.get(lbslot, 0))]
                    if g == 0 and s == 0 and tt == 0:
                        w += [("s_mt", P.val("s_mt")), ("s_xtr", P.val("s_xtr"))]
                    for k in range(4):
                        hc = qb * 4 + k
                        P.emit("tensor", lambda e, hc=hc, k=k, bank=bank, qv=qv: e.transpose(
                            bank[:, k * 128:(k + 1) * 128],
                            qv[:, hc * 128:(hc + 1) * 128], ident[:, :]),
                            waits=(w if k == 0 else []),
                            inc=("s_htr", 1) if k == 3 else None)
                    nhtr = P.val("s_htr")

                    def lb2ap(l, qb=qb, tt=tt, lbslot=lbslot):
                        v = regb(C_lb + lbslot * 6144 + l * 2048, 2048)
                        return bass.AP(v.tensor, v.offset + qb * 4 * 256 + tt * 128,
                                       [list(v.ap[0]), [256, 4], [1, 128]])
                    bv = bank[:, :]
                    psv = bass.AP(bv.tensor, bv.offset,
                                  [list(bv.ap[0]), [128, 4], [1, 128]])
                    r1 = reg(C_spl, 1024)
                    r1v = bass.AP(r1.tensor, r1.offset, [list(r1.ap[0]), [128, 4], [1, 128]])
                    r2v = bass.AP(r1.tensor, r1.offset + 512,
                                  [list(r1.ap[0]), [128, 4], [1, 128]])
                    P.emit("scalar", lambda e, lb=lb2ap(0), psv=psv: e.copy(out=lb, in_=psv),
                           waits=[("s_htr", nhtr)], inc=("s_lb2", 1))
                    nl1 = P.val("s_lb2")
                    P.emit("vector", lambda e, lb=lb2ap(0), psv=psv, r1v=r1v:
                           e.scalar_tensor_tensor(r1v, lb, -1.0, psv, op0=ALU.mult,
                                                  op1=ALU.add),
                           waits=[("s_lb2", nl1)], inc=("s_lb2", 1))
                    nl2 = P.val("s_lb2")
                    P.emit("scalar", lambda e, lb=lb2ap(1), r1v=r1v: e.copy(out=lb, in_=r1v),
                           waits=[("s_lb2", nl2)], inc=("s_lb2", 1))
                    nl3 = P.val("s_lb2")
                    P.emit("vector", lambda e, lb=lb2ap(1), r1v=r1v, r2v=r2v:
                           e.scalar_tensor_tensor(r2v, lb, -1.0, r1v, op0=ALU.mult,
                                                  op1=ALU.add),
                           waits=[("s_lb2", nl3)], inc=("s_lb2", 1))
                    nl4 = P.val("s_lb2")
                    P.emit("scalar", lambda e, lb=lb2ap(2), r2v=r2v: e.copy(out=lb, in_=r2v),
                           waits=[("s_lb2", nl4)], inc=("s_lb2", 1))
            nlb2 = P.val("s_lb2")
            for oq in range(4):
                whalf = oq % 2
                sw2 = "s_w2a" if whalf == 0 else "s_w2b"
                w = []
                key = ("w2q", whalf)
                prevw = w2_gate.get(key)
                if prevw is not None:
                    w.append(("s_mm2", prevw))
                if g == 0 and s == 0 and oq == 0:
                    w += [("s_sc", P.val("s_sc")), ("s_me", P.val("s_me")),
                          ("s_mpk", n_mpk)] + \
                        [("s_wp0", n_wpk[0]), ("s_wp1", n_wpk[1])]
                P.dma("sync", regb(C_w2 + whalf * 4096, 4096).rearrange(
                    "p (hc c) -> p hc c", hc=16),
                    p_w2[s * 16:(s + 1) * 16, :, oq * 512:(oq + 1) * 512].rearrange(
                        "a p c -> p a c"),
                    waits=w, sem=sw2)
                nw2 = P.val(sw2)
                w2v = regb(C_w2 + whalf * 4096, 4096)
                for tt in range(2):
                    bank = ps[2 * (oq % 2) + tt]
                    first = True
                    for hc in range(16):
                        for lmb in range(3):
                            lb = regb(C_lb + lbslot * 6144 + lmb * 2048, 2048)
                            stat = bass.AP(lb.tensor, lb.offset + hc * 256 + tt * 128,
                                           [list(lb.ap[0]), [1, 128]])
                            mov = bass.AP(w2v.tensor, w2v.offset + hc * 512,
                                          [list(w2v.ap[0]), [1, 512]])
                            w = []
                            if first:
                                w = [("s_lb2", nlb2), (sw2, nw2),
                                     ("s_ev2", P.val("s_ev2"))]
                                first = False
                            P.emit("tensor", lambda e, stat=stat, mov=mov, bank=bank,
                                   st=(hc == 0 and lmb == 0),
                                   sp=(hc == 15 and lmb == 2): e.matmul(
                                bank[:, :], stat, mov, start=st, stop=sp),
                                waits=w,
                                inc=("s_mm2", 1) if (hc == 15 and lmb == 2) else None)
                    nmm2 = P.val("s_mm2")
                    if tt == 1:
                        w2_gate[("w2q", whalf)] = nmm2
                    dst = reg(C_acc + tt * Dm + oq * 512, 512)
                    w = [("s_mm2", nmm2)]
                    if s == 0:
                        w.append(("s_fo", P.val("s_fo")))
                        P.emit("vector", lambda e, dst=dst, bank=bank: e.tensor_copy(
                            dst, bank[:, :]), waits=w, inc=("s_ev2", 1))
                    else:
                        P.emit("vector", lambda e, dst=dst, bank=bank: e.tensor_tensor(
                            dst, dst, bank[:, :], op=ALU.add), waits=w, inc=("s_ev2", 1))
            lb2_gate[lbslot] = P.val("s_mm2")
        nev = P.val("s_ev2")
        for tt in range(2):
            ttg = g * 2 + tt
            xmv = reg(C_fin, Dm)
            cav = reg(C_fin + Dm, Dm)
            w = [("s_ev2", nev), ("s_cc", 1)]
            if g > 0 or tt > 0:
                w.append(("s_fo", P.val("s_fo")))
            P.dma("sync", xmv, xm_in[ttg * 128:(ttg + 1) * 128, :], waits=w, sem="s_fd")
            P.dma("sync", cav[:, 0:DH], cc_o[0, ttg * 128:(ttg + 1) * 128, :], sem="s_fd")
            P.dma("sync", cav[:, DH:Dm], cc_o[1, ttg * 128:(ttg + 1) * 128, :], sem="s_fd")
            nfd1 = P.val("s_fd")
            accp = reg(C_acc + tt * Dm, Dm)
            P.emit("vector", lambda e, accp=accp, cav=cav: e.scalar_tensor_tensor(
                accp, cav, flg[:, 0:1], accp, op0=ALU.mult, op1=ALU.add),
                waits=[("s_fd", nfd1), ("s_wl", n_wl0)], inc=("s_fo", 1))
            nfo1 = P.val("s_fo")
            P.dma("sync", cav[:, 0:DH], cc_o[0, 2048 + ttg * 128:2048 + (ttg + 1) * 128, :],
                  waits=[("s_fo", nfo1)], sem="s_fd")
            P.dma("sync", cav[:, DH:Dm], cc_o[1, 2048 + ttg * 128:2048 + (ttg + 1) * 128, :],
                  sem="s_fd")
            nfd2 = P.val("s_fd")
            P.emit("vector", lambda e, accp=accp, cav=cav: e.scalar_tensor_tensor(
                accp, cav, flg[:, 1:2], accp, op0=ALU.mult, op1=ALU.add),
                waits=[("s_fd", nfd2)])
            P.emit("vector", lambda e, accp=accp, xmv=xmv: e.tensor_tensor(
                accp, accp, xmv, op=ALU.add), inc=("s_fo", 1))
            nfo = P.val("s_fo")
            P.dma("sync", out_hbm[ttg * 128:(ttg + 1) * 128, :], accp,
                  waits=[("s_fo", nfo)], sem="s_out")
    n_out = P.val("s_out")
    if DEV2:
        P.dma("sync", dbg["mix2"][:, :], cc_i[:, :],
              waits=[("s_out", n_out)], sem="s_dbg")
        P.dma("sync", dbg["hall"][:, :, :, :], p_h[:, :, :, :], sem="s_dbg")
        P.dma("sync", dbg["cco"][:, :, :], cc_o[:, :, :], sem="s_dbg")
        P.dma("sync", dbg["w1f"][:, :, :, :], p_w1[:, :, :, :], sem="s_dbg")
        P.dma("sync", dbg["w2f"][:, :, :], p_w2[:, :, :], sem="s_dbg")
    wend = [("s_out", n_out)]
    if DEV or DEV2:
        wend.append(("s_dbg", P.val("s_dbg")))
    P.emit("sync", lambda e: e.nop(), waits=wend)

    return _finish(nc, P, SEMS, ctxs)


def _finish(nc, P, SEMS, ctxs):
    semh = {}
    for s in SEMS:
        g2 = nc.semaphore(s)
        semh[s] = g2.__enter__()
        ctxs.append(g2)

    with nc.Block() as block:
        def runner(name):
            def go(e):
                for waits, fn, inc in P.ops[name]:
                    for (s, v) in waits:
                        e.wait_ge(semh[s], v)
                    ins = fn(e)
                    if inc is not None:
                        ins.then_inc(semh[inc[0]], inc[1])
            return go
        block.sync(runner("sync"))
        block.vector(runner("vector"))
        block.scalar(runner("scalar"))
        block.tensor(runner("tensor"))
        block.gpsimd(runner("gpsimd"))

    for g2 in reversed(ctxs):
        g2.__exit__(None, None, None)
    return nc


def kernel(L0_attn_x, L0_attn_W_x, L0_attn_W_dt, L0_attn_b_dt, L0_attn_A_log,
           L0_attn_D, L0_ffn_w_fc, L0_ffn_w_proj):
    x = np.ascontiguousarray(np.asarray(L0_attn_x, dtype=np.float32))
    wx = np.asarray(L0_attn_W_x, np.float32)
    wdt = np.asarray(L0_attn_W_dt, np.float32)
    bdtv = np.asarray(L0_attn_b_dt, np.float32)
    alog = np.asarray(L0_attn_A_log, np.float32)
    dv = np.asarray(L0_attn_D, np.float32)
    wfc = np.asarray(L0_ffn_w_fc, np.float32)
    wpj = np.asarray(L0_ffn_w_proj, np.float32)

    wfcT = np.ascontiguousarray(wfc.T)
    wpjT = np.ascontiguousarray(wpj.T)
    wxT = np.ascontiguousarray(wx.T)

    in_maps = []
    for c in range(8):
        b, hf = c // 2, c % 2
        ds = slice(hf * DH, (hf + 1) * DH)
        od = slice((1 - hf) * DH, (2 - hf) * DH)
        tsl = slice(hf * 2048, (hf + 1) * 2048)
        xb = x[b]
        xTh = np.ascontiguousarray(np.concatenate([xb.T[ds], xb.T[od]], 0))
        wxTh = np.ascontiguousarray(np.concatenate([wxT[ds], wxT[od]], 0))
        in_maps.append({
            "x": xb,
            "xT": xTh,
            "xmine": np.ascontiguousarray(xb[tsl]),
            "wxT": wxTh,
            "wdtT": np.ascontiguousarray(wdt[ds].T),
            "bdt": np.ascontiguousarray(bdtv[ds]),
            "alog": np.ascontiguousarray(alog[ds]),
            "dvec": np.ascontiguousarray(dv[ds]),
            "wfcT": wfcT,
            "wpjT": wpjT,
            "hflag": np.array([[1.0 - hf, float(hf)]], np.float32),
        })
    nc = build_nc()
    res = run_bass_kernel_spmd(nc, in_maps, core_ids=list(range(8)))
    out = np.empty((B, L, Dm), np.float32)
    for c in range(8):
        b, hf = c // 2, c % 2
        out[b, hf * 2048:(hf + 1) * 2048, :] = res.results[c]["o"]
    return out


# version 10 (final)
# speedup vs baseline: 1.0043x; 1.0043x over previous
import numpy as np
import concourse.bass as bass
import concourse.mybir as mybir
from concourse.bass_utils import run_bass_kernel_spmd

F32 = mybir.dt.float32
BF16 = mybir.dt.bfloat16
ALU = mybir.AluOpType
AF = mybir.ActivationFunctionType
AX = mybir.AxisListType.X

B, L, Dm, DH, NS, RK, HID, NPJ = 4, 4096, 2048, 1024, 16, 64, 8192, 96
TC0 = 256
NC0 = L // TC0
TC = 256
NCH = L // TC
NG = 8
NSL = 4
CLIP = 128.0 - 1e-6
INV_D = 1.0 / 2048.0
INV_H = 1.0 / 8192.0
DEV = False
DEV2 = False
SIM_NOCC = False
PH = 4


class Plan:
    ENGINES = ("sync", "vector", "scalar", "tensor", "gpsimd")

    def __init__(self):
        self.ops = {e: [] for e in self.ENGINES}
        self.cnt = {}

    def val(self, s):
        return self.cnt.get(s, 0)

    def emit(self, eng, fn, waits=(), inc=None):
        if inc is not None:
            self.cnt[inc[0]] = self.cnt.get(inc[0], 0) + inc[1]
        self.ops[eng].append(([w for w in waits if w[1] > 0], fn, inc))

    def seq(self, eng, fn, waits=(), inc=None):
        key = "_q_" + eng
        last = getattr(self, key, None)
        w = list(waits)
        if last is not None:
            w.append(last)
        if inc is None:
            inc = ("s_vq" if eng == "vector" else "s_sq", 1)
        self.emit(eng, fn, waits=w, inc=inc)
        setattr(self, key, (inc[0], self.val(inc[0])))

    def dma(self, eng, out, in_, waits=(), sem=None, slow=False):
        inc = (sem, 16) if sem else None
        if slow:
            self.emit(eng, lambda e, o=out, i=in_: e.dma_start(
                out=o, in_=i, allow_slow_non_contiguous=True), waits=waits, inc=inc)
        else:
            self.emit(eng, lambda e, o=out, i=in_: e.dma_start(out=o, in_=i),
                      waits=waits, inc=inc)


def grep(ap_, nrep):
    dims = [list(d) for d in ap_.ap]
    return bass.AP(ap_.tensor, ap_.offset, [dims[0], [0, nrep]] + dims[1:])


def build_nc():
    nc = bass.Bass("TRN2", target_bir_lowering=False, detect_race_conditions=False)
    P = Plan()

    x_in = nc.dram_tensor("x", [L, Dm], F32, kind="ExternalInput").ap()
    xT_in = nc.dram_tensor("xT", [Dm, L], F32, kind="ExternalInput").ap()
    xm_in = nc.dram_tensor("xmine", [2048, Dm], F32, kind="ExternalInput").ap()
    wxT_in = nc.dram_tensor("wxT", [Dm, NPJ], F32, kind="ExternalInput").ap()
    wdtT_in = nc.dram_tensor("wdtT", [RK, DH], F32, kind="ExternalInput").ap()
    bdt_in = nc.dram_tensor("bdt", [DH], F32, kind="ExternalInput").ap()
    alog_in = nc.dram_tensor("alog", [DH, NS], F32, kind="ExternalInput").ap()
    dvec_in = nc.dram_tensor("dvec", [DH], F32, kind="ExternalInput").ap()
    wfcT_in = nc.dram_tensor("wfcT", [Dm, HID], F32, kind="ExternalInput").ap()
    wpjT_in = nc.dram_tensor("wpjT", [HID, Dm], F32, kind="ExternalInput").ap()
    hflag_in = nc.dram_tensor("hflag", [1, 2], F32, kind="ExternalInput").ap()
    out_hbm = nc.dram_tensor("o", [2048, Dm], F32, kind="ExternalOutput").ap()

    p_m = nc.dram_tensor("p_m", [1, L], F32, kind="Internal").ap()
    p_r = nc.dram_tensor("p_r", [1, L], F32, kind="Internal").ap()
    p_bc = nc.dram_tensor("p_bc", [32, L], F32, kind="Internal").ap()
    p_dt = nc.dram_tensor("p_dt", [8, 128, L], F32, kind="Internal").ap()
    p_xn = nc.dram_tensor("p_xn", [8, 128, L], F32, kind="Internal").ap()
    p_w1 = nc.dram_tensor("p_w1", [8, 16, 128, 1024], BF16, kind="Internal").ap()
    p_w2 = nc.dram_tensor("p_w2", [64, 128, Dm], BF16, kind="Internal").ap()
    p_h = nc.dram_tensor("p_h", [NG, 2, 128, HID], F32, kind="Internal").ap()
    cc_i = nc.dram_tensor("cc_i", [L, DH], F32, kind="Internal").ap()
    cc_o = nc.dram_tensor("cc_o", [2, L, DH], F32, kind="Internal",
                          addr_space="Local").ap()
    dbg = {}
    if DEV2:
        dbg["mix2"] = nc.dram_tensor("d_mix2", [L, DH], F32, kind="ExternalOutput").ap()
        dbg["hall"] = nc.dram_tensor("d_hall", [NG, 2, 128, HID], F32,
                                     kind="ExternalOutput").ap()
        dbg["cco"] = nc.dram_tensor("d_cco", [2, L, DH], F32, kind="ExternalOutput").ap()
        dbg["w1f"] = nc.dram_tensor("d_w1f", [8, 16, 128, 1024], BF16, kind="ExternalOutput").ap()
        dbg["w2f"] = nc.dram_tensor("d_w2f", [64, 128, 2048], BF16, kind="ExternalOutput").ap()
    if DEV:
        dbg["xproj"] = nc.dram_tensor("d_xproj", [NPJ, L], F32, kind="ExternalOutput").ap()
        dbg["dt"] = nc.dram_tensor("d_dt", [8, 128, L], F32, kind="ExternalOutput").ap()
        dbg["xn"] = nc.dram_tensor("d_xn", [8, 128, L], F32, kind="ExternalOutput").ap()
        dbg["gam"] = nc.dram_tensor("d_gam", [1, 4], F32, kind="ExternalOutput").ap()
        dbg["w1"] = nc.dram_tensor("d_w1", [16, 128, 1024], BF16, kind="ExternalOutput").ap()
        dbg["mix"] = nc.dram_tensor("d_mix", [L, DH], F32, kind="ExternalOutput").ap()
        dbg["h0"] = nc.dram_tensor("d_h0", [2, 128, HID], F32, kind="ExternalOutput").ap()
        dbg["xhat"] = nc.dram_tensor("d_xhat", [128, Dm], F32, kind="ExternalOutput").ap()
        dbg["lb"] = nc.dram_tensor("d_lb", [3, 128, 4096], BF16, kind="ExternalOutput").ap()
        dbg["fst"] = nc.dram_tensor("d_fst", [128, 8], F32, kind="ExternalOutput").ap()
        dbg["stat"] = nc.dram_tensor("d_stat", [128, 64], F32, kind="ExternalOutput").ap()
        dbg["pm"] = nc.dram_tensor("d_pm", [2, 4096], F32, kind="ExternalOutput").ap()
        dbg["mbc"] = nc.dram_tensor("d_mbc", [128, 1024], F32, kind="ExternalOutput").ap()

    ctxs = []

    def sbuf(name, shape, dtype=F32):
        g = nc.sbuf_tensor(name, shape, dtype)
        t = g.__enter__()
        ctxs.append(g)
        return t

    ident = sbuf("ident", [128, 128])
    cpool = sbuf("cpool", [128, 6])
    onesr = sbuf("onesr", [1, 128])
    stat_m = sbuf("stat_m", [128, 32])
    stat_r = sbuf("stat_r", [128, 32])
    stat_v = sbuf("stat_v", [128, 32])
    wxT = sbuf("sb_wxT", [128, 16 * NPJ])
    wdtT = sbuf("sb_wdtT", [64, DH])
    bdt = sbuf("sb_bdt", [128, 8])
    A_sb = sbuf("A_sb", [128, 128])
    D_sb = sbuf("D_sb", [128, 8])
    scal = sbuf("scal", [128, 4])
    flg = sbuf("flg", [128, 2])
    gpart = sbuf("gpart", [128, 4])
    carry = sbuf("carry", [128, 128])
    fst = sbuf("fst", [128, 8])
    ln2 = sbuf("ln2", [128, 64])
    gsum = sbuf("gsum", [128, 112])
    xproj = sbuf("xproj", [96, L])

    AR = 43008
    arena = sbuf("arena", [128, AR])

    def reg(off, cols):
        assert off + cols <= AR
        return arena[:, off:off + cols]

    def reg3(off, nblk, blk):
        return arena[:, off:off + nblk * blk].rearrange("p (a b) -> p a b", a=nblk)

    def regb(off, f32cols):
        return arena[:, off:off + f32cols].bitcast(BF16)

    W_tl, W_sq, W_s1, W_s2, W_t2 = 0, 4096, 6144, 7168, 8192
    P_xt, P_xnf, P_mb, P_rb, P_dts, P_sq = 17408, 25600, 29696, 30208, 30720, 32768
    S_dt, S_xn, S_u, S_y, S_dA, S_dbx, S_h = 0, 2048, 4096, 6144, 8192, 10240, 12288
    S_y1 = 14336
    S_bcB, S_bcC, S_mx = 16384, 16896, 17408
    F_xg, F_xnt, F_spl, F_lb, F_hst, F_w1 = 19456, 21504, 23552, 24576, 30720, 34816
    C_hld, C_xht, C_spl, C_lb, C_w2, C_acc, C_fin = 0, 8192, 10240, 11264, 23552, 31744, 35840

    ps = []
    for i in range(8):
        g = nc.psum_tensor(f"ps{i}", [128, 512], F32)
        ps.append(g.__enter__())
        ctxs.append(g)

    SEMS = ["s_wl", "s_cst", "s_x", "s_xT", "s_bc", "s_st", "s_pk", "s_u",
            "s_xp", "s_xpe", "s_dtm", "s_dte", "s_p0d",
            "s_ga", "s_gb", "s_gs", "s_sg", "s_t2", "s_wpk",
            "s_ci", "s_cb", "s_cb1", "s_dA", "s_dbx", "s_sc", "s_tmp", "s_ya", "s_cy",
            "s_xg1",
            "s_mt", "s_me", "s_mpk", "s_cc",
            "s_xg", "s_pst", "s_xq", "s_xtr", "s_lb", "s_w1", "s_mm1", "s_he",
            "s_hs", "s_hpk", "s_l2s",
            "s_hl", "s_hq", "s_htr", "s_lb2", "s_w2", "s_mm2", "s_ev2",
            "s_fd", "s_fo", "s_out", "s_dbg", "s_id", "s_x0", "s_x1", "s_ps", "s_bc0", "s_bc1", "s_pxn", "s_pdt", "s_pbc", "s_wl0", "s_wl1", "s_wp0", "s_wp1", "s_w1a", "s_w1b", "s_w2a", "s_w2b", "s_hla", "s_hlb", "s_hp0", "s_hp1", "s_hp2", "s_hp3", "s_vq", "s_sq"]

    P.dma("sync", wxT.rearrange("p (dc c) -> p dc c", dc=16),
          wxT_in.rearrange("(dc p) c -> p dc c", p=128), sem="s_wl")
    P.dma("sync", wdtT[:, :], wdtT_in[:, :], sem="s_wl")
    P.dma("sync", bdt.rearrange("p (g o) -> p g o", g=8),
          bdt_in.rearrange("(g p o) -> p g o", p=128, o=1), sem="s_wl", slow=True)
    P.dma("sync", A_sb.rearrange("p (g n) -> p g n", g=8),
          alog_in.rearrange("(g p) n -> p g n", p=128), sem="s_wl")
    P.dma("sync", D_sb.rearrange("p (g o) -> p g o", g=8),
          dvec_in.rearrange("(g p o) -> p g o", p=128, o=1), sem="s_wl", slow=True)
    P.dma("sync", flg[:, :], hflag_in.broadcast_to([128, 2]), sem="s_wl")
    n_wl0 = P.val("s_wl")

    P.emit("gpsimd", lambda e: e.memset(ident[:, :], 0.0), inc=("s_id", 1))
    P.emit("gpsimd", lambda e: e.affine_select(
        out=ident[:, :], in_=ident[:, :], compare_op=ALU.not_equal, fill=1.0,
        base=0, pattern=[[-1, 128]], channel_multiplier=1),
        waits=[("s_id", 1)], inc=("s_id", 1))
    P.emit("vector", lambda e: e.memset(cpool[:, 0:1], -0.5))
    P.emit("vector", lambda e: e.memset(cpool[:, 1:2], 0.5))
    P.emit("vector", lambda e: e.memset(cpool[:, 2:3], 1.0))
    P.emit("vector", lambda e: e.memset(cpool[:, 3:4], 1e-5))
    P.emit("vector", lambda e: e.memset(cpool[:, 4:5], 1e-6))
    P.emit("vector", lambda e: e.memset(onesr[:, :], 1.0))
    P.emit("vector", lambda e: e.memset(carry[:, :], 0.0))
    P.emit("scalar", lambda e: e.activation(out=A_sb[:, :], in_=A_sb[:, :], func=AF.Exp),
           waits=[("s_wl", n_wl0)], inc=("s_cst", 1))
    P.emit("vector", lambda e: e.tensor_scalar(A_sb[:, :], A_sb[:, :], -1.0, None,
                                               op0=ALU.mult),
           waits=[("s_cst", P.val("s_cst"))], inc=("s_cst", 1))
    n_cst = P.val("s_cst")

    v_st = [0] * NC0
    v_xn = [0] * NC0
    v_mm = [0] * NC0
    v_xpe = [0] * NC0
    v_pk = [0] * NC0
    for c in range(NC0):
        ring = c % 2
        xt = reg(P_xt + ring * 4096, 4096)
        w = [("s_st", v_st[c - 2])] if c >= 2 else []
        sx = "s_x0" if ring == 0 else "s_x1"
        P.dma("sync", reg3(P_xt + ring * 4096, 2, Dm),
              x_in[c * TC0:(c + 1) * TC0, :].rearrange("(tt p) c -> p tt c", p=128),
              waits=w, sem=sx)
        nx = P.val(sx)
        sqv = reg(P_sq, Dm)
        for tt in range(2):
            ttg = c * 2 + tt
            xv = xt[:, tt * Dm:(tt + 1) * Dm]
            mcol = stat_m[:, ttg:ttg + 1]
            rcol = stat_r[:, ttg:ttg + 1]
            P.seq("vector", lambda e, xv=xv, m=mcol: e.tensor_reduce(
                m, xv, axis=AX, op=ALU.add), waits=[(sx, nx)] if tt == 0 else [])
            P.seq("vector", lambda e, xv=xv, r=rcol, sqv=sqv: e.scalar_tensor_tensor(
                sqv, xv, 1.0, xv, op0=ALU.mult, op1=ALU.mult, accum_out=r))
            P.seq("vector", lambda e, m=mcol: e.tensor_scalar(m, m, INV_D, None, op0=ALU.mult))
            P.seq("vector", lambda e, m=mcol, sqv=sqv: e.scalar_tensor_tensor(
                sqv[:, 0:1], m, 1.0, m, op0=ALU.mult, op1=ALU.mult))
            P.seq("vector", lambda e, r=rcol: e.tensor_scalar(r, r, INV_D, None, op0=ALU.mult))
            vcol = stat_v[:, ttg:ttg + 1]
            P.seq("vector", lambda e, r=rcol, v=vcol, sqv=sqv: e.tensor_tensor(
                v, r, sqv[:, 0:1], op=ALU.subtract), inc=("s_st", 1))
            nst0 = P.val("s_st")
            P.seq("scalar", lambda e, r=rcol, v=vcol: e.activation(out=r, in_=v, func=AF.Ln,
                                                                   bias=cpool[:, 3:4]),
                  waits=[("s_st", nst0)])
            P.seq("scalar", lambda e, r=rcol: e.activation(out=r, in_=r, func=AF.Exp,
                                                           scale=cpool[:, 0:1]),
                  inc=("s_st", 1))
        v_st[c] = P.val("s_st")
        w = [("s_st", v_st[c])]
        if c >= 1:
            w.append(("s_bc0" if (c - 1) % 2 == 0 else "s_bc1",
                      32 * ((c - 1) // 2 + 1)))
        P.dma("gpsimd", p_m[0:1, c * TC0:(c + 1) * TC0].rearrange(
                  "o (a p) -> p a o", p=128),
              stat_m[:, c * 2:c * 2 + 2].rearrange("p a -> p a ()"),
              waits=w, sem="s_ps", slow=True)
        P.dma("gpsimd", p_r[0:1, c * TC0:(c + 1) * TC0].rearrange(
                  "o (a p) -> p a o", p=128),
              stat_r[:, c * 2:c * 2 + 2].rearrange("p a -> p a ()"), sem="s_ps", slow=True)
        npk = P.val("s_ps")
        mb = reg(P_mb + ring * TC0, TC0)
        rb = reg(P_rb + ring * TC0, TC0)
        sbc = "s_bc0" if ring == 0 else "s_bc1"
        w = [("s_ps", npk)]
        if c >= 2:
            w.append(("s_u", v_xn[c - 2]))
        P.dma("sync", mb, p_m[0:1, c * TC0:(c + 1) * TC0].broadcast_to([128, TC0]),
              waits=w, sem=sbc)
        P.dma("sync", rb, p_r[0:1, c * TC0:(c + 1) * TC0].broadcast_to([128, TC0]),
              sem=sbc)
        nbc = P.val(sbc)
        w = [(sbc, nbc)]
        if c >= 1:
            w += [("s_xp", v_mm[c - 1]), ("s_pxn", 16 * c)]
        P.dma("sync", reg3(P_xnf, 16, TC0),
              xT_in[:, c * TC0:(c + 1) * TC0].rearrange("(dc p) c -> p dc c", p=128),
              waits=w, sem="s_xT")
        nxt = P.val("s_xT")
        xnf = arena[:, P_xnf:P_xnf + 16 * TC0].rearrange("p (dc c) -> p dc c", dc=16)
        P.emit("vector", lambda e, xnf=xnf, mb=mb: e.tensor_tensor(
            xnf, xnf, grep(mb, 16), op=ALU.subtract),
            waits=[("s_xT", nxt), (sbc, nbc)])
        P.emit("vector", lambda e, xnf=xnf, rb=rb: e.tensor_tensor(
            xnf, xnf, grep(rb, 16), op=ALU.mult), inc=("s_u", 1))
        v_xn[c] = P.val("s_u")
        nu = P.val("s_u")
        for dc in range(16):
            w = []
            if dc == 0:
                w = [("s_u", nu), ("s_wl", n_wl0)]
                if c >= 1:
                    w.append(("s_xpe", v_xpe[c - 1]))
            P.emit("tensor", lambda e, dc=dc: e.matmul(
                ps[6][0:NPJ, 0:TC0], wxT[:, dc * NPJ:(dc + 1) * NPJ],
                arena[:, P_xnf + dc * TC0:P_xnf + (dc + 1) * TC0],
                start=(dc == 0), stop=(dc == 15)),
                waits=w, inc=(("s_xp", 1) if dc == 15 else None))
        v_mm[c] = P.val("s_xp")
        P.emit("scalar", lambda e, c=c: e.copy(
            out=xproj[:, c * TC0:(c + 1) * TC0], in_=ps[6][0:NPJ, 0:TC0]),
            waits=[("s_xp", v_mm[c])], inc=("s_xpe", 1))
        v_xpe[c] = P.val("s_xpe")
        P.dma("gpsimd", p_xn[:, :, c * TC0:(c + 1) * TC0].rearrange("g p c -> p g c"), reg3(P_xnf, 8, TC0),
              waits=[("s_xp", v_mm[c])], sem="s_pxn")
        for g in range(8):
            bank = ps[7] if g % 2 == 0 else ps[5]
            if g < 2:
                w = [("s_xpe", v_xpe[c])]
                if c >= 1:
                    w.append(("s_dte", P.val("s_dte") - (2 - g)))
            else:
                w = [("s_dte", P.val("s_dte") - 1)]
            P.emit("tensor", lambda e, g=g, bank=bank, c=c: e.matmul(
                bank[:, 0:TC0], wdtT[:, g * 128:(g + 1) * 128],
                xproj[0:RK, c * TC0:(c + 1) * TC0], start=True, stop=True),
                waits=w, inc=("s_dtm", 1))
            ndtm = P.val("s_dtm")
            we = [("s_dtm", ndtm)]
            if g == 0:
                we.append(("s_cst", n_cst))
                if c >= 1:
                    we.append(("s_pdt", 16 * c))
            P.emit("scalar", lambda e, g=g, bank=bank: e.activation(
                out=arena[:, P_dts + g * TC0:P_dts + (g + 1) * TC0], in_=bank[:, 0:TC0],
                func=AF.Exp, bias=bdt[:, g:g + 1]), waits=we, inc=("s_dte", 1))
        nd8 = P.val("s_dte")
        P.emit("scalar", lambda e: e.activation(
            out=reg(P_dts, 2048), in_=reg(P_dts, 2048), func=AF.Ln, bias=cpool[:, 2:3]),
            waits=[("s_dte", nd8)], inc=("s_dte", 1))
        P.dma("gpsimd", p_dt[:, :, c * TC0:(c + 1) * TC0].rearrange("g p c -> p g c"), reg3(P_dts, 8, TC0),
              waits=[("s_dte", P.val("s_dte"))], sem="s_pdt")
        v_pk[c] = P.val("s_pdt")
    P.dma("gpsimd", p_bc[:, :], xproj[64:96, :],
          waits=[("s_xpe", P.val("s_xpe"))], sem="s_pbc")
    P.emit("gpsimd", lambda e: e.engine_nop(),
           waits=[("s_ps", P.val("s_ps")), ("s_pxn", P.val("s_pxn")),
                  ("s_pdt", P.val("s_pdt")), ("s_pbc", P.val("s_pbc"))],
           inc=("s_p0d", 1))
    if DEV:
        P.dma("sync", dbg["stat"][:, 0:32], stat_m[:, :], waits=[("s_p0d", 1)], sem="s_dbg")
        P.dma("sync", dbg["stat"][:, 32:64], stat_r[:, :], sem="s_dbg")
        P.dma("sync", dbg["pm"][0:1, :], p_m[:, :], sem="s_dbg")
        P.dma("sync", dbg["pm"][1:2, :], p_r[:, :], sem="s_dbg")
        P.dma("sync", dbg["mbc"][:, 0:512], reg(P_mb, 512), sem="s_dbg")
        P.dma("sync", dbg["mbc"][:, 512:1024], reg(P_rb, 512), sem="s_dbg")
        P.dma("sync", dbg["xproj"][:, :], xproj[:, :],
              waits=[("s_xpe", P.val("s_xpe"))], sem="s_dbg")
        P.dma("sync", dbg["dt"][:, :, :], p_dt[:, :, :], waits=[("s_p0d", 1)], sem="s_dbg")
        P.dma("sync", dbg["xn"][:, :, :], p_xn[:, :, :], waits=[("s_p0d", 1)], sem="s_dbg")

    def wprep(tile_in, park_out, gcol, n_tiles, tw):
        base_wp = (P.val("s_wp0"), P.val("s_wp1"))
        for i in range(n_tiles):
            ring = i % 4
            wv = reg(W_tl + ring * tw, tw)
            swl = "s_wl0" if ring % 2 == 0 else "s_wl1"
            w = ([("s_ga", P.val("s_ga") - 3)] if i >= 4
                 else [("s_sg", P.val("s_sg"))])
            P.dma("sync", wv, tile_in(i), waits=w, sem=swl)
            nw = P.val(swl)
            w = [(swl, nw)]
            if i >= 2:
                w.append(("s_gb", P.val("s_gb") - 1))
            P.emit("scalar", lambda e, wv=wv, i=i: e.activation(
                out=reg(W_sq, tw), in_=wv, func=AF.Abs,
                accum_out=gpart[:, i % 2:i % 2 + 1]), waits=w, inc=("s_ga", 1))
            nga = P.val("s_ga")
            if i == 0:
                P.seq("vector", lambda e: e.tensor_copy(gpart[:, 2:3], gpart[:, 0:1]),
                      waits=[("s_ga", nga), ("s_gs", P.val("s_gs"))], inc=("s_gb", 1))
            else:
                P.seq("vector", lambda e, i=i: e.tensor_tensor(
                    gpart[:, 2:3], gpart[:, 2:3], gpart[:, i % 2:i % 2 + 1], op=ALU.add),
                    waits=[("s_ga", nga)], inc=("s_gb", 1))
        ngb = P.val("s_gb")
        P.emit("tensor", lambda e: e.matmul(
            ps[4][0:1, 0:1], gpart[:, 2:3], cpool[:, 2:3], start=True, stop=True),
            waits=[("s_gb", ngb), ("s_cst", n_cst)], inc=("s_gs", 1))
        ngs = P.val("s_gs")
        ntot = float(n_tiles) * 128.0 * float(tw)
        P.seq("vector", lambda e: e.tensor_scalar(
            gpart[0:1, 3:4], ps[4][0:1, 0:1], 1.0 / ntot, None, op0=ALU.mult),
            waits=[("s_gs", ngs)])
        P.seq("vector", lambda e: e.tensor_scalar(
            gpart[0:1, 2:3], gpart[0:1, 3:4], 0.5, None, op0=ALU.mult))
        P.seq("vector", lambda e: e.tensor_scalar(
            gpart[0:1, 3:4], gpart[0:1, 3:4], 1e-6, None, op0=ALU.add))
        P.seq("vector", lambda e: e.reciprocal(gpart[0:1, 3:4], gpart[0:1, 3:4]),
              inc=("s_gs", 1))
        ngs2 = P.val("s_gs")
        P.emit("tensor", lambda e: e.matmul(
            ps[4][:, 4:6], onesr[:, :], gpart[0:1, 2:4], start=True, stop=True),
            waits=[("s_gs", ngs2)], inc=("s_gs", 1))
        ngs3 = P.val("s_gs")
        P.emit("scalar", lambda e, gcol=gcol: e.copy(out=scal[:, gcol * 2:gcol * 2 + 2],
                                                     in_=ps[4][:, 4:6]),
               waits=[("s_gs", ngs3)], inc=("s_gs", 1))
        ngs4 = P.val("s_gs")
        for i in range(n_tiles):
            ring = i % 4
            wv = reg(W_tl + ring * tw, tw)
            swl = "s_wl0" if ring % 2 == 0 else "s_wl1"
            w = [("s_sg", P.val("s_sg") - 6)] if i >= 4 else [("s_gs", ngs4)]
            P.dma("sync", wv, tile_in(i), waits=w, sem=swl)
            nw = P.val(swl)
            wsg = regb(W_s1, tw // 2)
            wsg2 = regb(W_s2, tw // 2)
            w = [(swl, nw)]
            if i == 0:
                w.append(("s_gs", ngs4))
            if i >= 1:
                w.append(("s_t2", P.val("s_t2")))
            P.emit("scalar", lambda e, wv=wv, wsg=wsg, gcol=gcol: e.activation(
                out=wsg, in_=wv, func=AF.Sign, bias=cpool[:, 0:1],
                scale=scal[:, gcol * 2 + 1:gcol * 2 + 2]), waits=w, inc=("s_sg", 1))
            P.emit("scalar", lambda e, wv=wv, wsg2=wsg2, gcol=gcol: e.activation(
                out=wsg2, in_=wv, func=AF.Sign, bias=cpool[:, 1:2],
                scale=scal[:, gcol * 2 + 1:gcol * 2 + 2]), inc=("s_sg", 1))
            nsg = P.val("s_sg")
            t2v = regb(W_t2 + ring * (tw // 2), tw // 2)
            swp = "s_wp0" if ring % 2 == 0 else "s_wp1"
            w = [("s_sg", nsg)]
            if i >= 4:
                w.append((swp, base_wp[ring % 2] + 16 * (i // 2 - 1)))
            else:
                w.append((swp, base_wp[ring % 2]))
            P.emit("vector", lambda e, t2v=t2v, wsg=wsg, wsg2=wsg2: e.tensor_tensor(
                t2v, wsg, wsg2, op=ALU.add), waits=w, inc=("s_t2", 1))
            nt2 = P.val("s_t2")
            P.dma("gpsimd", park_out(i), t2v, waits=[("s_t2", nt2)], sem=swp)

    if PH < 1:
        for tt in range(16):
            P.dma("sync", out_hbm[tt * 128:(tt + 1) * 128, :],
                  xm_in[tt * 128:(tt + 1) * 128, :], sem="s_out")
        wend = [("s_out", P.val("s_out"))]
        if DEV:
            wend.append(("s_dbg", P.val("s_dbg")))
        P.emit("sync", lambda e: e.nop(), waits=wend)
        return _finish(nc, P, SEMS, ctxs)

    def wfc_tile(i):
        dc, oc = i // 8, i % 8
        return wfcT_in[dc * 128:(dc + 1) * 128, oc * 1024:(oc + 1) * 1024]

    def wfc_park(i):
        dc, oc = i // 8, i % 8
        return p_w1[oc, dc, :, :]

    wprep(wfc_tile, wfc_park, 0, 128, 1024)
    n_gs_fc = P.val("s_gs")
    n_w1pk = (P.val("s_wp0"), P.val("s_wp1"))
    if DEV:
        P.dma("sync", dbg["gam"][0:1, 0:4], gpart[0:1, 0:4],
              waits=[("s_gs", P.val("s_gs"))], sem="s_dbg")
        P.dma("sync", dbg["w1"][:, :, :], p_w1[0, :, :, :],
              waits=[("s_wp0", n_w1pk[0]), ("s_wp1", n_w1pk[1])], sem="s_dbg")
    wprep(lambda i: wpjT_in[(i // 2) * 128:(i // 2 + 1) * 128,
                            (i % 2) * 1024:(i % 2 + 1) * 1024],
          lambda i: p_w2[i // 2, :, (i % 2) * 1024:(i % 2 + 1) * 1024], 1, 128, 1024)
    n_gs_pj = P.val("s_gs")
    n_wpk = (P.val("s_wp0"), P.val("s_wp1"))

    if PH < 2:
        for tt in range(16):
            P.dma("sync", out_hbm[tt * 128:(tt + 1) * 128, :],
                  xm_in[tt * 128:(tt + 1) * 128, :],
                  waits=([("s_wp0", P.val("s_wp0")), ("s_wp1", P.val("s_wp1"))]
                         if tt == 0 else []), sem="s_out")
        wend = [("s_out", P.val("s_out"))]
        if DEV:
            wend.append(("s_dbg", P.val("s_dbg")))
        P.emit("sync", lambda e: e.nop(), waits=wend)
        return _finish(nc, P, SEMS, ctxs)

    v_mt_ch = [0] * NCH
    v_yD_ch = [0] * NCH
    ya_hist = []
    v_nxg = [0] * 16
    v_xtr = [0] * 16

    def plan_xg_prefetch(g):
        for tt in range(2):
            ttg = g * 2 + tt
            slot = F_xg if tt == 0 else F_xnt
            sxg = "s_xg" if tt == 0 else "s_xg1"
            w = [("s_p0d", 1)] if g == 0 else [("s_xtr", v_xtr[ttg - 2])]
            P.dma("sync", reg(slot, Dm), xm_in[ttg * 128:(ttg + 1) * 128, :],
                  waits=w, sem=sxg)
            v_nxg[ttg] = P.val(sxg)

    def plan_scan_chunk(c):
        w = [("s_wp0", n_wpk[0]), ("s_wp1", n_wpk[1]), ("s_p0d", 1)]
        if c >= 1:
            w += [("s_dA", P.val("s_dA")), ("s_ya", P.val("s_ya"))]
        P.dma("sync", reg3(S_dt, 8, TC), p_dt[:, :, c * TC:(c + 1) * TC].rearrange("g p c -> p g c"),
              waits=w, sem="s_ci")
        P.dma("sync", reg3(S_xn, 8, TC), p_xn[:, :, c * TC:(c + 1) * TC].rearrange("g p c -> p g c"), sem="s_ci")
        nci = P.val("s_ci")
        P.emit("vector", lambda e: e.tensor_tensor(
            reg(S_u, 2048), reg(S_dt, 2048), reg(S_xn, 2048), op=ALU.mult),
            waits=[("s_ci", nci)], inc=("s_u", 1))
        nu = P.val("s_u")
        w = []
        if c >= 2:
            w.append(("s_mt", v_mt_ch[c - 2]))
        yoff = S_y if c % 2 == 0 else S_y1
        yv = reg(yoff, 2048)
        P.emit("gpsimd", lambda e, yv=yv: e.memset(yv, 0.0), waits=w, inc=("s_cy", 1))
        ncy = P.val("s_cy")
        for n in range(NS):
            gidx = c * NS + n
            par = n % 2
            scb = "s_cb" if par == 0 else "s_cb1"
            bcoff = par * 256
            w = []
            if gidx >= 2:
                w.append(("s_ya", ya_hist[gidx - 2]))
            P.dma("sync", reg(S_bcB + bcoff, TC),
                  p_bc[n:n + 1, c * TC:(c + 1) * TC].broadcast_to([128, TC]),
                  waits=w, sem=scb)
            P.dma("sync", reg(S_bcC + bcoff, TC),
                  p_bc[16 + n:17 + n, c * TC:(c + 1) * TC].broadcast_to([128, TC]),
                  sem=scb)
            ncb = P.val(scb)
            w = [("s_ci", nci), ("s_cst", n_cst)]
            if c > 0 or n > 0:
                w.append(("s_sc", P.val("s_sc")))
            for g in range(8):
                P.emit("scalar", lambda e, g=g, n=n: e.activation(
                    out=arena[:, S_dA + g * TC:S_dA + (g + 1) * TC],
                    in_=arena[:, S_dt + g * TC:S_dt + (g + 1) * TC],
                    func=AF.Exp, scale=A_sb[:, g * 16 + n:g * 16 + n + 1]),
                    waits=(w if g == 0 else []), inc=("s_dA", 1) if g == 7 else None)
            ndA = P.val("s_dA")
            bcb = reg(S_bcB + bcoff, TC)
            P.emit("vector", lambda e, bcb=bcb: e.tensor_tensor(
                reg3(S_dbx, 8, TC), reg3(S_u, 8, TC), grep(bcb, 8), op=ALU.mult),
                waits=[("s_u", nu), (scb, ncb)])
            w = [("s_dA", ndA)]
            for g in range(8):
                P.emit("vector", lambda e, g=g, n=n: e.tensor_tensor_scan(
                    arena[:, S_h + g * TC:S_h + (g + 1) * TC],
                    arena[:, S_dA + g * TC:S_dA + (g + 1) * TC],
                    arena[:, S_dbx + g * TC:S_dbx + (g + 1) * TC],
                    carry[:, n * 8 + g:n * 8 + g + 1],
                    op0=ALU.mult, op1=ALU.add),
                    waits=(w if g == 0 else []),
                    inc=(("s_vq", 1) if g == 7 else None))
            nvq = P.val("s_vq")
            hsl = arena[:, S_h:S_h + 2048]
            csrc = bass.AP(hsl.tensor, hsl.offset + TC - 1, [list(hsl.ap[0]), [TC, 8]])
            P.emit("vector", lambda e, n=n, csrc=csrc: e.tensor_copy(
                carry[:, n * 8:n * 8 + 8], csrc),
                waits=[("s_vq", nvq)], inc=("s_sc", 1))
            bcc = reg(S_bcC + bcoff, TC)
            P.emit("vector", lambda e, bcc=bcc: e.tensor_tensor(
                reg3(S_dbx, 8, TC), reg3(S_h, 8, TC), grep(bcc, 8), op=ALU.mult))
            P.emit("vector", lambda e, yv=yv: e.tensor_tensor(
                yv, yv, reg(S_dbx, 2048), op=ALU.add),
                waits=([("s_cy", ncy)] if n == 0 else []), inc=("s_ya", 1))
            ya_hist.append(P.val("s_ya"))
        for g in range(8):
            P.emit("vector", lambda e, g=g, yv=yv: e.scalar_tensor_tensor(
                yv[:, g * TC:(g + 1) * TC], arena[:, S_xn + g * TC:S_xn + (g + 1) * TC],
                D_sb[:, g:g + 1], yv[:, g * TC:(g + 1) * TC], op0=ALU.mult, op1=ALU.add),
                inc=("s_ya", 1) if g == 7 else None)
        v_yD_ch[c] = P.val("s_ya")

    def plan_mix_T(c):
        yoff = S_y if c % 2 == 0 else S_y1
        yv = reg(yoff, 2048)
        nyD = v_yD_ch[c]
        nmpk_prev = P.val("s_mpk")
        for tt2 in range(2):
            for gb in range(2):
                bank = ps[6 + tt2]
                for gg in range(4):
                    g = gb * 4 + gg
                    w = []
                    if gg == 0:
                        w = [("s_ya", nyD), ("s_me", P.val("s_me"))]
                        if c == 0 and tt2 == 0 and gb == 0:
                            w.append(("s_id", 2))
                    P.emit("tensor", lambda e, g=g, tt2=tt2, gg=gg, bank=bank, yv=yv:
                           e.transpose(bank[:, gg * 128:(gg + 1) * 128],
                                       yv[:, g * TC + tt2 * 128:g * TC + tt2 * 128 + 128],
                                       ident[:, :]),
                           waits=w, inc=("s_mt", 1) if gg == 3 else None)
                nmt = P.val("s_mt")
                w = [("s_mt", nmt)]
                if tt2 == 0 and gb == 0:
                    w.append(("s_mpk", nmpk_prev))
                    if c == 0:
                        w.append(("s_p0d", 1))
                P.emit("scalar", lambda e, tt2=tt2, gb=gb, bank=bank: e.copy(
                    out=reg(S_mx + tt2 * 1024 + gb * 512, 512), in_=bank[:, 0:512]),
                    waits=w, inc=("s_me", 1))
        v_mt_ch[c] = P.val("s_mt")
        nme = P.val("s_me")
        P.dma("gpsimd", cc_i[c * TC:(c + 1) * TC, :].rearrange("(tt p) d -> p tt d", p=128),
              reg3(S_mx, 2, 1024), waits=[("s_me", nme)], sem="s_mpk")

    w1_gate = {}
    bl1_defer = {}

    def emit_piece_stats(g, op8, tt, sl, nhe):
        hp = reg(F_hst + sl * 1024, 1024)
        P.seq("vector", lambda e, hp=hp, tt=tt, op8=op8: e.scalar_tensor_tensor(
            reg(S_dbx, 1024), hp, 1.0, hp, op0=ALU.mult, op1=ALU.mult,
            accum_out=gsum[:, 32 + tt * 8 + op8:33 + tt * 8 + op8]),
            waits=[("s_he", nhe)])
        P.seq("vector", lambda e, hp=hp, tt=tt, op8=op8: e.tensor_reduce(
            gsum[:, 48 + tt * 8 + op8:49 + tt * 8 + op8], hp, axis=AX,
            op=ALU.max))
        P.seq("vector", lambda e, hp=hp, tt=tt, op8=op8: e.tensor_reduce(
            gsum[:, 64 + tt * 8 + op8:65 + tt * 8 + op8], hp, axis=AX,
            op=ALU.min), inc=("s_hs", 1))
        nhs = P.val("s_hs")
        P.dma("gpsimd", p_h[g, tt, :, op8 * 1024:(op8 + 1) * 1024], hp,
              waits=[("s_hs", nhs)], sem=f"s_hp{sl}")

    def plan_bl1_defer(g):
        for it in bl1_defer.pop(g):
            if it[0] == "stats":
                _, op8, tt, sl, nhe = it
                emit_piece_stats(g, op8, tt, sl, nhe)
            else:
                _, op8, og, tt, bank, sl, ogidx, nmm = it
                w = [("s_mm1", nmm)]
                slotsem = f"s_hp{sl}"
                if og == 0:
                    w.append((slotsem, P.val(slotsem)))
                P.emit("scalar", lambda e, bank=bank, sl=sl, og=og, tt=tt,
                       ogidx=ogidx: e.activation(
                    out=arena[:, F_hst + sl * 1024 + og * 512:
                              F_hst + sl * 1024 + og * 512 + 512],
                    in_=bank[:, :], func=AF.Gelu,
                    accum_out=gsum[:, tt * 16 + ogidx:tt * 16 + ogidx + 1]),
                    waits=w, inc=("s_he", 1))
                if og == 1:
                    emit_piece_stats(g, op8, tt, sl, P.val("s_he"))
        nhs2 = P.val("s_hs")
        for tt in range(2):
            gl = g * 2 + tt
            Lc = ln2[:, gl * 4:gl * 4 + 4]
            P.seq("vector", lambda e, Lc=Lc, tt=tt: e.tensor_reduce(
                Lc[:, 0:1], gsum[:, tt * 16:tt * 16 + 16], axis=AX, op=ALU.add),
                waits=[("s_hs", nhs2)])
            P.seq("vector", lambda e, Lc=Lc: e.tensor_scalar(
                Lc[:, 0:1], Lc[:, 0:1], INV_H, None, op0=ALU.mult))
            P.seq("vector", lambda e, Lc=Lc: e.scalar_tensor_tensor(
                gsum[:, 96:97], Lc[:, 0:1], 1.0, Lc[:, 0:1], op0=ALU.mult, op1=ALU.mult))
            P.seq("vector", lambda e, tt=tt: e.tensor_reduce(
                gsum[:, 97:98], gsum[:, 32 + tt * 8:40 + tt * 8], axis=AX, op=ALU.add))
            P.seq("vector", lambda e, Lc=Lc: e.tensor_scalar(
                Lc[:, 1:2], gsum[:, 97:98], INV_H, None, op0=ALU.mult))
            P.seq("vector", lambda e, Lc=Lc: e.tensor_tensor(
                Lc[:, 1:2], Lc[:, 1:2], gsum[:, 96:97], op=ALU.subtract), inc=("s_l2s", 1))
            nl2a = P.val("s_l2s")
            P.seq("scalar", lambda e, Lc=Lc: e.activation(
                out=Lc[:, 1:2], in_=Lc[:, 1:2], func=AF.Ln, bias=cpool[:, 4:5]),
                waits=[("s_l2s", nl2a)])
            P.seq("scalar", lambda e, Lc=Lc: e.activation(
                out=Lc[:, 1:2], in_=Lc[:, 1:2], func=AF.Exp, scale=cpool[:, 0:1]),
                inc=("s_l2s", 1))
            nl2b = P.val("s_l2s")
            P.seq("vector", lambda e, tt=tt: e.tensor_reduce(
                gsum[:, 98:99], gsum[:, 48 + tt * 8:56 + tt * 8], axis=AX, op=ALU.max),
                waits=[("s_l2s", nl2b)])
            P.seq("vector", lambda e, tt=tt: e.tensor_reduce(
                gsum[:, 99:100], gsum[:, 64 + tt * 8:72 + tt * 8], axis=AX, op=ALU.min))
            P.seq("vector", lambda e, Lc=Lc: e.tensor_tensor(
                gsum[:, 98:99], gsum[:, 98:99], Lc[:, 0:1], op=ALU.subtract))
            P.seq("vector", lambda e, Lc=Lc: e.tensor_tensor(
                gsum[:, 99:100], Lc[:, 0:1], gsum[:, 99:100], op=ALU.subtract))
            P.seq("vector", lambda e: e.tensor_tensor(
                gsum[:, 98:99], gsum[:, 98:99], gsum[:, 99:100], op=ALU.max))
            P.seq("vector", lambda e, Lc=Lc: e.tensor_tensor(
                gsum[:, 98:99], gsum[:, 98:99], Lc[:, 1:2], op=ALU.mult))
            P.seq("vector", lambda e, Lc=Lc: e.reciprocal(Lc[:, 2:3], gsum[:, 98:99]))
            P.seq("vector", lambda e, Lc=Lc: e.tensor_scalar(
                Lc[:, 2:3], Lc[:, 2:3], 128.0, None, op0=ALU.mult))
            P.seq("vector", lambda e, Lc=Lc: e.tensor_scalar(
                Lc[:, 3:4], gsum[:, 98:99], 1.0 / 128.0, None, op0=ALU.mult))
            P.seq("vector", lambda e, Lc=Lc: e.tensor_tensor(
                Lc[:, 3:4], Lc[:, 3:4], scal[:, 2:3], op=ALU.mult),
                waits=[("s_gs", n_gs_pj)])
            P.seq("vector", lambda e, Lc=Lc: e.tensor_scalar(
                Lc[:, 0:1], Lc[:, 0:1], -1.0, None, op0=ALU.mult), inc=("s_l2s", 1))

    def plan_bl1_group(g):
        F = fst
        for tt in range(2):
            ttg = g * 2 + tt
            xnv = reg(F_xg if tt == 0 else F_xnt, Dm)
            nxg = v_nxg[ttg]
            sxg = "s_xg" if tt == 0 else "s_xg1"
            P.seq("vector", lambda e, ttg=ttg: e.tensor_tensor(
                F[:, 0:1], stat_m[:, ttg:ttg + 1], flg[:, 0:1], op=ALU.mult),
                waits=[(sxg, nxg), ("s_wl", n_wl0)])
            P.seq("vector", lambda e, ttg=ttg: e.tensor_tensor(
                F[:, 2:3], stat_m[:, 16 + ttg:17 + ttg], flg[:, 1:2], op=ALU.mult))
            P.seq("vector", lambda e: e.tensor_tensor(
                F[:, 0:1], F[:, 0:1], F[:, 2:3], op=ALU.add))
            P.seq("vector", lambda e: e.tensor_scalar(
                F[:, 0:1], F[:, 0:1], -1.0, None, op0=ALU.mult))
            P.seq("vector", lambda e, ttg=ttg: e.tensor_tensor(
                F[:, 1:2], stat_r[:, ttg:ttg + 1], flg[:, 0:1], op=ALU.mult))
            P.seq("vector", lambda e, ttg=ttg: e.tensor_tensor(
                F[:, 2:3], stat_r[:, 16 + ttg:17 + ttg], flg[:, 1:2], op=ALU.mult))
            P.seq("vector", lambda e: e.tensor_tensor(
                F[:, 1:2], F[:, 1:2], F[:, 2:3], op=ALU.add))
            P.seq("vector", lambda e, xnv=xnv: e.tensor_scalar(
                xnv, xnv, F[:, 0:1], F[:, 1:2], op0=ALU.add, op1=ALU.mult))
            P.seq("vector", lambda e, xnv=xnv: e.tensor_reduce(
                F[:, 4:5], xnv, axis=AX, op=ALU.max))
            P.seq("vector", lambda e, xnv=xnv: e.tensor_reduce(
                F[:, 5:6], xnv, axis=AX, op=ALU.min))
            P.seq("vector", lambda e: e.tensor_scalar(F[:, 5:6], F[:, 5:6], -1.0, None,
                                                       op0=ALU.mult))
            P.seq("vector", lambda e: e.tensor_tensor(F[:, 4:5], F[:, 4:5], F[:, 5:6],
                                                       op=ALU.max))
            P.seq("vector", lambda e, ttg=ttg: e.tensor_tensor(
                F[:, 2:3], stat_v[:, ttg:ttg + 1], flg[:, 0:1], op=ALU.mult))
            P.seq("vector", lambda e, ttg=ttg: e.tensor_tensor(
                F[:, 3:4], stat_v[:, 16 + ttg:17 + ttg], flg[:, 1:2], op=ALU.mult))
            P.seq("vector", lambda e: e.tensor_tensor(
                F[:, 2:3], F[:, 2:3], F[:, 3:4], op=ALU.add))
            P.seq("vector", lambda e: e.tensor_tensor(
                F[:, 3:4], F[:, 1:2], F[:, 1:2], op=ALU.mult))
            P.seq("vector", lambda e: e.tensor_tensor(
                F[:, 3:4], F[:, 3:4], F[:, 2:3], op=ALU.mult), inc=("s_pst", 1))
            np3 = P.val("s_pst")
            P.seq("scalar", lambda e: e.activation(out=F[:, 3:4], in_=F[:, 3:4],
                                                    func=AF.Ln, bias=cpool[:, 4:5]),
                   waits=[("s_pst", np3)])
            P.seq("scalar", lambda e: e.activation(out=F[:, 3:4], in_=F[:, 3:4],
                                                    func=AF.Exp, scale=cpool[:, 0:1]),
                   inc=("s_pst", 1))
            np3b = P.val("s_pst")
            P.seq("vector", lambda e: e.reciprocal(F[:, 6:7], F[:, 4:5]))
            P.seq("vector", lambda e: e.tensor_scalar(F[:, 6:7], F[:, 6:7], 128.0, None,
                                                       op0=ALU.mult))
            P.seq("vector", lambda e: e.tensor_scalar(F[:, 7:8], F[:, 4:5], 1.0 / 128.0,
                                                       None, op0=ALU.mult))
            P.seq("vector", lambda e, xnv=xnv: e.tensor_scalar(
                xnv, xnv, F[:, 6:7], CLIP, op0=ALU.mult, op1=ALU.min))
            P.seq("vector", lambda e, xnv=xnv: e.tensor_scalar(
                xnv, xnv, -CLIP, None, op0=ALU.max))
            P.seq("vector", lambda e: e.tensor_tensor(F[:, 7:8], F[:, 7:8],
                                                       F[:, 3:4], op=ALU.mult),
                   waits=[("s_pst", np3b)])
            P.seq("vector", lambda e: e.tensor_tensor(F[:, 7:8], F[:, 7:8],
                                                       scal[:, 0:1], op=ALU.mult),
                   waits=[("s_gs", n_gs_fc)])
            P.seq("vector", lambda e, xnv=xnv: e.tensor_scalar(
                xnv, xnv, F[:, 7:8], None, op0=ALU.mult), inc=("s_xq", 1))
            nxq = P.val("s_xq")
            if DEV and g == 0 and tt == 0:
                P.dma("sync", dbg["xhat"][:, :], xnv, waits=[("s_xq", nxq)], sem="s_dbg")
                P.dma("sync", dbg["fst"][:, :], F[:, 0:8], waits=[("s_xq", nxq)], sem="s_dbg")
            for qb in range(4):
                bank = ps[4 + qb % 2]
                w = [("s_xq", nxq), ("s_lb", max(0, P.val("s_lb") - 6)),
                     ("s_mm1", P.val("s_mm1")), ("s_id", 2)]
                for k in range(4):
                    dc = qb * 4 + k
                    P.emit("tensor", lambda e, dc=dc, k=k, bank=bank, xnv=xnv: e.transpose(
                        bank[:, k * 128:(k + 1) * 128],
                        xnv[:, dc * 128:(dc + 1) * 128], ident[:, :]),
                        waits=(w if k == 0 else []),
                        inc=("s_xtr", 1) if k == 3 else None)
                nxtr = P.val("s_xtr")

                def lbap(l, qb=qb, tt=tt):
                    v = regb(F_lb + l * 2048, 2048)
                    return bass.AP(v.tensor, v.offset + qb * 4 * 256 + tt * 128,
                                   [list(v.ap[0]), [256, 4], [1, 128]])
                bv = bank[:, :]
                psv = bass.AP(bv.tensor, bv.offset, [list(bv.ap[0]), [128, 4], [1, 128]])
                r1 = reg(F_spl, 1024)
                r1v = bass.AP(r1.tensor, r1.offset, [list(r1.ap[0]), [128, 4], [1, 128]])
                r2v = bass.AP(r1.tensor, r1.offset + 512, [list(r1.ap[0]), [128, 4], [1, 128]])
                P.emit("scalar", lambda e, lb=lbap(0), psv=psv: e.copy(out=lb, in_=psv),
                       waits=[("s_xtr", nxtr)], inc=("s_lb", 1))
                nl1 = P.val("s_lb")
                P.emit("vector", lambda e, lb=lbap(0), psv=psv, r1v=r1v: e.scalar_tensor_tensor(
                    r1v, lb, -1.0, psv, op0=ALU.mult, op1=ALU.add),
                    waits=[("s_lb", nl1)], inc=("s_lb", 1))
                nl2 = P.val("s_lb")
                P.emit("scalar", lambda e, lb=lbap(1), r1v=r1v: e.copy(out=lb, in_=r1v),
                       waits=[("s_lb", nl2)], inc=("s_lb", 1))
                nl3 = P.val("s_lb")
                P.emit("vector", lambda e, lb=lbap(1), r1v=r1v, r2v=r2v: e.scalar_tensor_tensor(
                    r2v, lb, -1.0, r1v, op0=ALU.mult, op1=ALU.add),
                    waits=[("s_lb", nl3)], inc=("s_lb", 1))
                nl4 = P.val("s_lb")
                P.emit("scalar", lambda e, lb=lbap(2), r2v=r2v: e.copy(out=lb, in_=r2v),
                       waits=[("s_lb", nl4)], inc=("s_lb", 1))
            v_xtr[ttg] = P.val("s_xtr")
        nlball = P.val("s_lb")
        if DEV and g == 0:
            for l in range(3):
                P.dma("sync", dbg["lb"][l, :, :], regb(F_lb + l * 2048, 2048),
                      waits=[("s_lb", nlball)], sem="s_dbg")
        he_base = P.val("s_he")
        defer = []
        for op8 in range(8):
            for og in range(2):
                half = og % 2
                sw1 = "s_w1a" if half == 0 else "s_w1b"
                w = [("s_wp0", n_w1pk[0]), ("s_wp1", n_w1pk[1])]
                key = ("w1half", half)
                prev_waves = w1_gate.get(key)
                if prev_waves is not None:
                    w.append(("s_mm1", prev_waves))
                P.dma("sync", regb(F_w1 + half * 4096, 4096).rearrange(
                    "p (dc c) -> p dc c", dc=16),
                    p_w1[op8, :, :, og * 512:(og + 1) * 512].rearrange(
                        "a p c -> p a c"), waits=w, sem=sw1)
                nw1 = P.val(sw1)
                w1v = regb(F_w1 + half * 4096, 4096)
                for tt in range(2):
                    bank = ps[og * 2 + tt]
                    first = True
                    for dc in range(16):
                        for lmb in range(3):
                            lb = regb(F_lb + lmb * 2048, 2048)
                            stat = bass.AP(lb.tensor, lb.offset + dc * 256 + tt * 128,
                                           [list(lb.ap[0]), [1, 128]])
                            mov = bass.AP(w1v.tensor, w1v.offset + dc * 512,
                                          [list(w1v.ap[0]), [1, 512]])
                            w = []
                            if first:
                                w = [("s_lb", nlball), (sw1, nw1),
                                     ("s_he", P.val("s_he"))]
                                first = False
                            P.emit("tensor", lambda e, stat=stat, mov=mov, bank=bank,
                                   st=(dc == 0 and lmb == 0),
                                   sp=(dc == 15 and lmb == 2): e.matmul(
                                bank[:, :], stat, mov, start=st, stop=sp),
                                waits=w, inc=("s_mm1", 1) if (dc == 15 and lmb == 2) else None)
                    nmm = P.val("s_mm1")
                    if tt == 1:
                        w1_gate[("w1half", half)] = nmm
                    ogidx = op8 * 2 + og
                    sl = (op8 * 2 + tt) % 4
                    if op8 >= 7:
                        defer.append(("evac", op8, og, tt, bank, sl, ogidx, nmm))
                        continue
                    w = [("s_mm1", nmm)]
                    slotsem = f"s_hp{sl}"
                    if P.val(slotsem) > 0 and og == 0:
                        w.append((slotsem, P.val(slotsem)))
                    P.emit("scalar", lambda e, bank=bank, sl=sl, og=og, tt=tt,
                           ogidx=ogidx: e.activation(
                        out=arena[:, F_hst + sl * 1024 + og * 512:
                                  F_hst + sl * 1024 + og * 512 + 512],
                        in_=bank[:, :], func=AF.Gelu,
                        accum_out=gsum[:, tt * 16 + ogidx:tt * 16 + ogidx + 1]),
                        waits=w, inc=("s_he", 1))
                    nhe = P.val("s_he")
                    if og == 1:
                        if op8 >= 5:
                            defer.append(("stats", op8, tt, sl, nhe))
                            continue
                        emit_piece_stats(g, op8, tt, sl, nhe)
        bl1_defer[g] = defer
    for g in range(NG):
        plan_scan_chunk(2 * g)
        if PH >= 3 and g >= 1:
            plan_bl1_defer(g - 1)
        if PH >= 3:
            plan_xg_prefetch(g)
        plan_scan_chunk(2 * g + 1)
        plan_mix_T(2 * g)
        plan_mix_T(2 * g + 1)
        if PH >= 3:
            plan_bl1_group(g)
    if PH >= 3:
        plan_bl1_defer(NG - 1)
    n_mpk = P.val("s_mpk")
    n_hpk = [P.val(f"s_hp{i}") for i in range(4)]
    if DEV:
        P.dma("sync", dbg["mix"][:, :], cc_i[:, :], waits=[("s_mpk", n_mpk)], sem="s_dbg")
        P.dma("sync", dbg["h0"][:, :, :], p_h[0, :, :, :],
              waits=[(f"s_hp{i}", n_hpk[i]) for i in range(4)], sem="s_dbg")

    if PH < 4:
        wend = [("s_mpk", n_mpk), ("s_sc", P.val("s_sc")),
                ("s_ya", P.val("s_ya")), ("s_me", P.val("s_me"))] + \
            [(f"s_hp{i}", n_hpk[i]) for i in range(4)]
        for tt in range(16):
            P.dma("sync", out_hbm[tt * 128:(tt + 1) * 128, :],
                  xm_in[tt * 128:(tt + 1) * 128, :],
                  waits=(wend if tt == 0 else []), sem="s_out")
        wend2 = [("s_out", P.val("s_out"))]
        if DEV:
            wend2.append(("s_dbg", P.val("s_dbg")))
        P.emit("sync", lambda e: e.nop(), waits=wend2)
        return _finish(nc, P, SEMS, ctxs)

    if SIM_NOCC:
        P.dma("gpsimd", cc_o[0, :, :], cc_i[:, :], waits=[("s_mpk", n_mpk)], sem="s_mpk")
        P.dma("gpsimd", cc_o[1, :, :], cc_i[:, :], sem="s_mpk")
        P.emit("gpsimd", lambda e: e.engine_nop(),
               waits=[("s_mpk", P.val("s_mpk"))], inc=("s_cc", 1))
    else:
        P.emit("gpsimd", lambda e: e.collective_compute(
            "AllGather", ALU.bypass, ins=[cc_i[:, :]], outs=[cc_o[:, :, :]],
            replica_groups=[[0, 1], [2, 3], [4, 5], [6, 7]]),
            waits=[("s_mpk", n_mpk)], inc=("s_cc", 1))

    guard = [("s_sc", P.val("s_sc")), ("s_ya", P.val("s_ya")),
             ("s_me", P.val("s_me")), ("s_he", P.val("s_he")),
             ("s_hs", P.val("s_hs")), ("s_mpk", n_mpk),
             ("s_lb", P.val("s_lb")), ("s_tmp", P.val("s_tmp")),
             ("s_mm1", P.val("s_mm1")), ("s_l2s", P.val("s_l2s"))] + \
        [(f"s_hp{i}", n_hpk[i]) for i in range(4)] + \
        [("s_wp0", n_wpk[0]), ("s_wp1", n_wpk[1])]
    w2_gate = {}
    lb2_gate = {}
    qts_info = {}

    def bl2_qts(g, s):
        ring = s % 2
        shl = "s_hla" if ring == 0 else "s_hlb"
        w = list(guard) if (g == 0 and s == 0) else []
        if g > 0 or s >= 2:
            w.append(("s_lb2", max(0, P.val("s_lb2") - 40)))
        P.dma("sync", reg3(C_hld + ring * 4096, 2, Dm),
              p_h[g, :, :, s * Dm:(s + 1) * Dm].rearrange("a p c -> p a c"),
              waits=w, sem=shl)
        nhl = P.val(shl)
        lbslot = (g * NSL + s) % 2
        for tt in range(2):
            gl = g * 2 + tt
            Lc = ln2[:, gl * 4:gl * 4 + 4]
            hv = reg(C_hld + ring * 4096 + tt * Dm, Dm)
            qv = reg(C_xht, Dm)
            w = [(shl, nhl)]
            if g > 0 or s > 0 or tt > 0:
                w.append(("s_lb2", P.val("s_lb2")))
        
            P.emit("vector", lambda e, hv=hv, qv=qv, Lc=Lc: e.tensor_scalar(
                qv, hv, Lc[:, 0:1], Lc[:, 1:2], op0=ALU.add, op1=ALU.mult), waits=w)
            P.emit("vector", lambda e, qv=qv, Lc=Lc: e.tensor_scalar(
                qv, qv, Lc[:, 2:3], CLIP, op0=ALU.mult, op1=ALU.min))
            P.emit("vector", lambda e, qv=qv, Lc=Lc: e.tensor_scalar(
                qv, qv, -CLIP, Lc[:, 3:4], op0=ALU.max, op1=ALU.mult),
                inc=("s_hq", 1))
            nhq = P.val("s_hq")
            for qb in range(4):
                bank = ps[4 + qb % 2]
                w = [("s_hq", nhq), ("s_lb2", max(0, P.val("s_lb2") - 6)),
                     ("s_mm2", lb2_gate.get(lbslot, 0))]
                if g == 0 and s == 0 and tt == 0:
                    w += [("s_mt", P.val("s_mt")), ("s_xtr", P.val("s_xtr"))]
                for k in range(4):
                    hc = qb * 4 + k
                    P.emit("tensor", lambda e, hc=hc, k=k, bank=bank, qv=qv: e.transpose(
                        bank[:, k * 128:(k + 1) * 128],
                        qv[:, hc * 128:(hc + 1) * 128], ident[:, :]),
                        waits=(w if k == 0 else []),
                        inc=("s_htr", 1) if k == 3 else None)
                nhtr = P.val("s_htr")

                def lb2ap(l, qb=qb, tt=tt, lbslot=lbslot):
                    v = regb(C_lb + lbslot * 6144 + l * 2048, 2048)
                    return bass.AP(v.tensor, v.offset + qb * 4 * 256 + tt * 128,
                                   [list(v.ap[0]), [256, 4], [1, 128]])
                bv = bank[:, :]
                psv = bass.AP(bv.tensor, bv.offset,
                              [list(bv.ap[0]), [128, 4], [1, 128]])
                r1 = reg(C_spl, 1024)
                r1v = bass.AP(r1.tensor, r1.offset, [list(r1.ap[0]), [128, 4], [1, 128]])
                r2v = bass.AP(r1.tensor, r1.offset + 512,
                              [list(r1.ap[0]), [128, 4], [1, 128]])
                P.emit("vector", lambda e, lb=lb2ap(0), psv=psv: e.tensor_copy(lb, psv),
                       waits=[("s_htr", nhtr)], inc=("s_lb2", 1))
                nl1 = P.val("s_lb2")
                P.emit("vector", lambda e, lb=lb2ap(0), psv=psv, r1v=r1v:
                       e.scalar_tensor_tensor(r1v, lb, -1.0, psv, op0=ALU.mult,
                                              op1=ALU.add),
                       waits=[("s_lb2", nl1)], inc=("s_lb2", 1))
                nl2 = P.val("s_lb2")
                P.emit("vector", lambda e, lb=lb2ap(1), r1v=r1v: e.tensor_copy(lb, r1v),
                       waits=[("s_lb2", nl2)], inc=("s_lb2", 1))
                nl3 = P.val("s_lb2")
                P.emit("vector", lambda e, lb=lb2ap(1), r1v=r1v, r2v=r2v:
                       e.scalar_tensor_tensor(r2v, lb, -1.0, r1v, op0=ALU.mult,
                                              op1=ALU.add),
                       waits=[("s_lb2", nl3)], inc=("s_lb2", 1))
                nl4 = P.val("s_lb2")
                P.emit("vector", lambda e, lb=lb2ap(2), r2v=r2v: e.tensor_copy(lb, r2v),
                       waits=[("s_lb2", nl4)], inc=("s_lb2", 1))
        qts_info[(g, s)] = (P.val("s_lb2"), lbslot)

    def bl2_waves(g, s):
        nlb2, lbslot = qts_info.pop((g, s))
        for oq in range(4):
            whalf = oq % 2
            sw2 = "s_w2a" if whalf == 0 else "s_w2b"
            w = []
            key = ("w2q", whalf)
            prevw = w2_gate.get(key)
            if prevw is not None:
                w.append(("s_mm2", prevw))
            if g == 0 and s == 0 and oq == 0:
                w += [("s_sc", P.val("s_sc")), ("s_me", P.val("s_me")),
                      ("s_mpk", n_mpk)] + \
                    [("s_wp0", n_wpk[0]), ("s_wp1", n_wpk[1])]
            P.dma("sync", regb(C_w2 + whalf * 4096, 4096).rearrange(
                "p (hc c) -> p hc c", hc=16),
                p_w2[s * 16:(s + 1) * 16, :, oq * 512:(oq + 1) * 512].rearrange(
                    "a p c -> p a c"),
                waits=w, sem=sw2)
            nw2 = P.val(sw2)
            w2v = regb(C_w2 + whalf * 4096, 4096)
            for tt in range(2):
                bank = ps[2 * (oq % 2) + tt]
                first = True
                for hc in range(16):
                    for lmb in range(3):
                        lb = regb(C_lb + lbslot * 6144 + lmb * 2048, 2048)
                        stat = bass.AP(lb.tensor, lb.offset + hc * 256 + tt * 128,
                                       [list(lb.ap[0]), [1, 128]])
                        mov = bass.AP(w2v.tensor, w2v.offset + hc * 512,
                                      [list(w2v.ap[0]), [1, 512]])
                        w = []
                        if first:
                            w = [("s_lb2", nlb2), (sw2, nw2),
                                 ("s_ev2", P.val("s_ev2"))]
                            first = False
                        P.emit("tensor", lambda e, stat=stat, mov=mov, bank=bank,
                               st=(hc == 0 and lmb == 0),
                               sp=(hc == 15 and lmb == 2): e.matmul(
                            bank[:, :], stat, mov, start=st, stop=sp),
                            waits=w,
                            inc=("s_mm2", 1) if (hc == 15 and lmb == 2) else None)
                nmm2 = P.val("s_mm2")
                if tt == 1:
                    w2_gate[("w2q", whalf)] = nmm2
                dst = reg(C_acc + tt * Dm + oq * 512, 512)
                w = [("s_mm2", nmm2)]
                if s == 0:
                    w.append(("s_fo", P.val("s_fo")))
                    P.emit("vector", lambda e, dst=dst, bank=bank: e.tensor_copy(
                        dst, bank[:, :]), waits=w, inc=("s_ev2", 1))
                else:
                    P.emit("vector", lambda e, dst=dst, bank=bank: e.tensor_tensor(
                        dst, dst, bank[:, :], op=ALU.add), waits=w, inc=("s_ev2", 1))
        lb2_gate[lbslot] = P.val("s_mm2")

    def bl2_finals(g):
        nev = P.val("s_ev2")
        for tt in range(2):
            ttg = g * 2 + tt
            xmv = reg(C_fin, Dm)
            cav = reg(C_fin + Dm, Dm)
            w = [("s_ev2", nev), ("s_cc", 1)]
            if g > 0 or tt > 0:
                w.append(("s_fo", P.val("s_fo")))
            P.dma("sync", xmv, xm_in[ttg * 128:(ttg + 1) * 128, :], waits=w, sem="s_fd")
            P.dma("sync", cav[:, 0:DH], cc_o[0, ttg * 128:(ttg + 1) * 128, :], sem="s_fd")
            P.dma("sync", cav[:, DH:Dm], cc_o[1, ttg * 128:(ttg + 1) * 128, :], sem="s_fd")
            nfd1 = P.val("s_fd")
            accp = reg(C_acc + tt * Dm, Dm)
            P.emit("vector", lambda e, accp=accp, cav=cav: e.scalar_tensor_tensor(
                accp, cav, flg[:, 0:1], accp, op0=ALU.mult, op1=ALU.add),
                waits=[("s_fd", nfd1), ("s_wl", n_wl0)], inc=("s_fo", 1))
            nfo1 = P.val("s_fo")
            P.dma("sync", cav[:, 0:DH], cc_o[0, 2048 + ttg * 128:2048 + (ttg + 1) * 128, :],
                  waits=[("s_fo", nfo1)], sem="s_fd")
            P.dma("sync", cav[:, DH:Dm], cc_o[1, 2048 + ttg * 128:2048 + (ttg + 1) * 128, :],
                  sem="s_fd")
            nfd2 = P.val("s_fd")
            P.emit("vector", lambda e, accp=accp, cav=cav: e.scalar_tensor_tensor(
                accp, cav, flg[:, 1:2], accp, op0=ALU.mult, op1=ALU.add),
                waits=[("s_fd", nfd2)])
            P.emit("vector", lambda e, accp=accp, xmv=xmv: e.tensor_tensor(
                accp, accp, xmv, op=ALU.add), inc=("s_fo", 1))
            nfo = P.val("s_fo")
            P.dma("sync", out_hbm[ttg * 128:(ttg + 1) * 128, :], accp,
                  waits=[("s_fo", nfo)], sem="s_out")

    bl2_qts(0, 0)
    for g in range(NG):
        for s in range(NSL):
            ni = g * NSL + s + 1
            if ni < NG * NSL:
                bl2_qts(ni // NSL, ni % NSL)
            bl2_waves(g, s)
        bl2_finals(g)
    n_out = P.val("s_out")
    if DEV2:
        P.dma("sync", dbg["mix2"][:, :], cc_i[:, :],
              waits=[("s_out", n_out)], sem="s_dbg")
        P.dma("sync", dbg["hall"][:, :, :, :], p_h[:, :, :, :], sem="s_dbg")
        P.dma("sync", dbg["cco"][:, :, :], cc_o[:, :, :], sem="s_dbg")
        P.dma("sync", dbg["w1f"][:, :, :, :], p_w1[:, :, :, :], sem="s_dbg")
        P.dma("sync", dbg["w2f"][:, :, :], p_w2[:, :, :], sem="s_dbg")
    wend = [("s_out", n_out)]
    if DEV or DEV2:
        wend.append(("s_dbg", P.val("s_dbg")))
    P.emit("sync", lambda e: e.nop(), waits=wend)

    return _finish(nc, P, SEMS, ctxs)


def _finish(nc, P, SEMS, ctxs):
    semh = {}
    for s in SEMS:
        g2 = nc.semaphore(s)
        semh[s] = g2.__enter__()
        ctxs.append(g2)

    with nc.Block() as block:
        def runner(name):
            def go(e):
                for waits, fn, inc in P.ops[name]:
                    for (s, v) in waits:
                        e.wait_ge(semh[s], v)
                    ins = fn(e)
                    if inc is not None:
                        ins.then_inc(semh[inc[0]], inc[1])
            return go
        block.sync(runner("sync"))
        block.vector(runner("vector"))
        block.scalar(runner("scalar"))
        block.tensor(runner("tensor"))
        block.gpsimd(runner("gpsimd"))

    for g2 in reversed(ctxs):
        g2.__exit__(None, None, None)
    return nc


def kernel(L0_attn_x, L0_attn_W_x, L0_attn_W_dt, L0_attn_b_dt, L0_attn_A_log,
           L0_attn_D, L0_ffn_w_fc, L0_ffn_w_proj):
    x = np.ascontiguousarray(np.asarray(L0_attn_x, dtype=np.float32))
    wx = np.asarray(L0_attn_W_x, np.float32)
    wdt = np.asarray(L0_attn_W_dt, np.float32)
    bdtv = np.asarray(L0_attn_b_dt, np.float32)
    alog = np.asarray(L0_attn_A_log, np.float32)
    dv = np.asarray(L0_attn_D, np.float32)
    wfc = np.asarray(L0_ffn_w_fc, np.float32)
    wpj = np.asarray(L0_ffn_w_proj, np.float32)

    wfcT = np.ascontiguousarray(wfc.T)
    wpjT = np.ascontiguousarray(wpj.T)
    wxT = np.ascontiguousarray(wx.T)

    in_maps = []
    for c in range(8):
        b, hf = c // 2, c % 2
        ds = slice(hf * DH, (hf + 1) * DH)
        od = slice((1 - hf) * DH, (2 - hf) * DH)
        tsl = slice(hf * 2048, (hf + 1) * 2048)
        xb = x[b]
        xTh = np.ascontiguousarray(np.concatenate([xb.T[ds], xb.T[od]], 0))
        wxTh = np.ascontiguousarray(np.concatenate([wxT[ds], wxT[od]], 0))
        in_maps.append({
            "x": xb,
            "xT": xTh,
            "xmine": np.ascontiguousarray(xb[tsl]),
            "wxT": wxTh,
            "wdtT": np.ascontiguousarray(wdt[ds].T),
            "bdt": np.ascontiguousarray(bdtv[ds]),
            "alog": np.ascontiguousarray(alog[ds]),
            "dvec": np.ascontiguousarray(dv[ds]),
            "wfcT": wfcT,
            "wpjT": wpjT,
            "hflag": np.array([[1.0 - hf, float(hf)]], np.float32),
        })
    nc = build_nc()
    res = run_bass_kernel_spmd(nc, in_maps, core_ids=list(range(8)))
    out = np.empty((B, L, Dm), np.float32)
    for c in range(8):
        b, hf = c // 2, c % 2
        out[b, hf * 2048:(hf + 1) * 2048, :] = res.results[c]["o"]
    return out


# version 11
# speedup vs baseline: 1.0074x; 1.0031x over previous
import numpy as np
import concourse.bass as bass
import concourse.mybir as mybir
from concourse.bass_utils import run_bass_kernel_spmd

F32 = mybir.dt.float32
BF16 = mybir.dt.bfloat16
ALU = mybir.AluOpType
AF = mybir.ActivationFunctionType
AX = mybir.AxisListType.X

B, L, Dm, DH, NS, RK, HID, NPJ = 4, 4096, 2048, 1024, 16, 64, 8192, 96
TC0 = 256
NC0 = L // TC0
TC = 256
NCH = L // TC
NG = 8
NSL = 4
CLIP = 128.0 - 1e-6
INV_D = 1.0 / 2048.0
INV_H = 1.0 / 8192.0
DEV = False
DEV2 = False
SIM_NOCC = False
PH = 4


class Plan:
    ENGINES = ("sync", "vector", "scalar", "tensor", "gpsimd")

    def __init__(self):
        self.ops = {e: [] for e in self.ENGINES}
        self.cnt = {}

    def val(self, s):
        return self.cnt.get(s, 0)

    def emit(self, eng, fn, waits=(), inc=None):
        if inc is not None:
            self.cnt[inc[0]] = self.cnt.get(inc[0], 0) + inc[1]
        self.ops[eng].append(([w for w in waits if w[1] > 0], fn, inc))

    def seq(self, eng, fn, waits=(), inc=None):
        key = "_q_" + eng
        last = getattr(self, key, None)
        w = list(waits)
        if last is not None:
            w.append(last)
        if inc is None:
            inc = ("s_vq" if eng == "vector" else "s_sq", 1)
        self.emit(eng, fn, waits=w, inc=inc)
        setattr(self, key, (inc[0], self.val(inc[0])))

    def dma(self, eng, out, in_, waits=(), sem=None, slow=False):
        inc = (sem, 16) if sem else None
        if slow:
            self.emit(eng, lambda e, o=out, i=in_: e.dma_start(
                out=o, in_=i, allow_slow_non_contiguous=True), waits=waits, inc=inc)
        else:
            self.emit(eng, lambda e, o=out, i=in_: e.dma_start(out=o, in_=i),
                      waits=waits, inc=inc)


def grep(ap_, nrep):
    dims = [list(d) for d in ap_.ap]
    return bass.AP(ap_.tensor, ap_.offset, [dims[0], [0, nrep]] + dims[1:])


def build_nc():
    nc = bass.Bass("TRN2", target_bir_lowering=False, detect_race_conditions=False)
    P = Plan()

    x_in = nc.dram_tensor("x", [L, Dm], F32, kind="ExternalInput").ap()
    xT_in = nc.dram_tensor("xT", [Dm, L], F32, kind="ExternalInput").ap()
    xm_in = nc.dram_tensor("xmine", [2048, Dm], F32, kind="ExternalInput").ap()
    wxT_in = nc.dram_tensor("wxT", [Dm, NPJ], F32, kind="ExternalInput").ap()
    wdtT_in = nc.dram_tensor("wdtT", [RK, DH], F32, kind="ExternalInput").ap()
    bdt_in = nc.dram_tensor("bdt", [DH], F32, kind="ExternalInput").ap()
    alog_in = nc.dram_tensor("alog", [DH, NS], F32, kind="ExternalInput").ap()
    dvec_in = nc.dram_tensor("dvec", [DH], F32, kind="ExternalInput").ap()
    wfcT_in = nc.dram_tensor("wfcT", [Dm, HID], F32, kind="ExternalInput").ap()
    wpjT_in = nc.dram_tensor("wpjT", [HID, Dm], F32, kind="ExternalInput").ap()
    hflag_in = nc.dram_tensor("hflag", [1, 2], F32, kind="ExternalInput").ap()
    out_hbm = nc.dram_tensor("o", [2048, Dm], F32, kind="ExternalOutput").ap()

    p_m = nc.dram_tensor("p_m", [1, L], F32, kind="Internal").ap()
    p_r = nc.dram_tensor("p_r", [1, L], F32, kind="Internal").ap()
    p_bc = nc.dram_tensor("p_bc", [32, L], F32, kind="Internal").ap()
    p_dt = nc.dram_tensor("p_dt", [8, 128, L], F32, kind="Internal").ap()
    p_xn = nc.dram_tensor("p_xn", [8, 128, L], F32, kind="Internal").ap()
    p_w1 = nc.dram_tensor("p_w1", [8, 16, 128, 1024], BF16, kind="Internal").ap()
    p_w2 = nc.dram_tensor("p_w2", [64, 128, Dm], BF16, kind="Internal").ap()
    p_h = nc.dram_tensor("p_h", [NG, 2, 128, HID], F32, kind="Internal").ap()
    cc_i = nc.dram_tensor("cc_i", [L, DH], F32, kind="Internal").ap()
    cc_o = nc.dram_tensor("cc_o", [2, L, DH], F32, kind="Internal",
                          addr_space="Local").ap()
    dbg = {}
    if DEV2:
        dbg["mix2"] = nc.dram_tensor("d_mix2", [L, DH], F32, kind="ExternalOutput").ap()
        dbg["hall"] = nc.dram_tensor("d_hall", [NG, 2, 128, HID], F32,
                                     kind="ExternalOutput").ap()
        dbg["cco"] = nc.dram_tensor("d_cco", [2, L, DH], F32, kind="ExternalOutput").ap()
        dbg["w1f"] = nc.dram_tensor("d_w1f", [8, 16, 128, 1024], BF16, kind="ExternalOutput").ap()
        dbg["w2f"] = nc.dram_tensor("d_w2f", [64, 128, 2048], BF16, kind="ExternalOutput").ap()
    if DEV:
        dbg["xproj"] = nc.dram_tensor("d_xproj", [NPJ, L], F32, kind="ExternalOutput").ap()
        dbg["dt"] = nc.dram_tensor("d_dt", [8, 128, L], F32, kind="ExternalOutput").ap()
        dbg["xn"] = nc.dram_tensor("d_xn", [8, 128, L], F32, kind="ExternalOutput").ap()
        dbg["gam"] = nc.dram_tensor("d_gam", [1, 4], F32, kind="ExternalOutput").ap()
        dbg["w1"] = nc.dram_tensor("d_w1", [16, 128, 1024], BF16, kind="ExternalOutput").ap()
        dbg["mix"] = nc.dram_tensor("d_mix", [L, DH], F32, kind="ExternalOutput").ap()
        dbg["h0"] = nc.dram_tensor("d_h0", [2, 128, HID], F32, kind="ExternalOutput").ap()
        dbg["xhat"] = nc.dram_tensor("d_xhat", [128, Dm], F32, kind="ExternalOutput").ap()
        dbg["lb"] = nc.dram_tensor("d_lb", [3, 128, 4096], BF16, kind="ExternalOutput").ap()
        dbg["fst"] = nc.dram_tensor("d_fst", [128, 8], F32, kind="ExternalOutput").ap()
        dbg["stat"] = nc.dram_tensor("d_stat", [128, 64], F32, kind="ExternalOutput").ap()
        dbg["pm"] = nc.dram_tensor("d_pm", [2, 4096], F32, kind="ExternalOutput").ap()
        dbg["mbc"] = nc.dram_tensor("d_mbc", [128, 1024], F32, kind="ExternalOutput").ap()

    ctxs = []

    def sbuf(name, shape, dtype=F32):
        g = nc.sbuf_tensor(name, shape, dtype)
        t = g.__enter__()
        ctxs.append(g)
        return t

    ident = sbuf("ident", [128, 128])
    cpool = sbuf("cpool", [128, 6])
    onesr = sbuf("onesr", [1, 128])
    stat_m = sbuf("stat_m", [128, 32])
    stat_r = sbuf("stat_r", [128, 32])
    stat_v = sbuf("stat_v", [128, 32])
    wxT = sbuf("sb_wxT", [128, 16 * NPJ])
    wdtT = sbuf("sb_wdtT", [64, DH])
    bdt = sbuf("sb_bdt", [128, 8])
    A_sb = sbuf("A_sb", [128, 128])
    D_sb = sbuf("D_sb", [128, 8])
    scal = sbuf("scal", [128, 4])
    flg = sbuf("flg", [128, 2])
    gpart = sbuf("gpart", [128, 4])
    carry = sbuf("carry", [128, 128])
    fst = sbuf("fst", [128, 8])
    ln2 = sbuf("ln2", [128, 64])
    gsum = sbuf("gsum", [128, 112])
    xproj = sbuf("xproj", [96, L])

    AR = 43008
    arena = sbuf("arena", [128, AR])

    def reg(off, cols):
        assert off + cols <= AR
        return arena[:, off:off + cols]

    def reg3(off, nblk, blk):
        return arena[:, off:off + nblk * blk].rearrange("p (a b) -> p a b", a=nblk)

    def regb(off, f32cols):
        return arena[:, off:off + f32cols].bitcast(BF16)

    W_tl, W_sq, W_s1, W_s2, W_t2 = 0, 4096, 6144, 7168, 8192
    P_xt, P_xnf, P_mb, P_rb, P_dts, P_sq = 17408, 25600, 29696, 30208, 30720, 32768
    S_dt, S_xn, S_u, S_y, S_dA, S_dbx, S_h = 0, 2048, 4096, 6144, 8192, 10240, 12288
    S_y1 = 14336
    S_bcB, S_bcC, S_mx = 16384, 16896, 17408
    F_xg, F_xnt, F_spl, F_lb, F_hst, F_w1 = 19456, 21504, 23552, 24576, 30720, 34816
    C_hld, C_xht, C_spl, C_lb, C_w2, C_acc, C_fin = 0, 8192, 10240, 11264, 23552, 31744, 35840

    ps = []
    for i in range(8):
        g = nc.psum_tensor(f"ps{i}", [128, 512], F32)
        ps.append(g.__enter__())
        ctxs.append(g)

    SEMS = ["s_wl", "s_cst", "s_x", "s_xT", "s_bc", "s_st", "s_pk", "s_u",
            "s_xp", "s_xpe", "s_dtm", "s_dte", "s_p0d",
            "s_ga", "s_gb", "s_gs", "s_sg", "s_t2", "s_wpk",
            "s_ci", "s_cb", "s_cb1", "s_dA", "s_dbx", "s_sc", "s_tmp", "s_ya", "s_cy",
            "s_xg1",
            "s_mt", "s_me", "s_mpk", "s_cc",
            "s_xg", "s_pst", "s_xq", "s_xtr", "s_lb", "s_w1", "s_mm1", "s_he",
            "s_hs", "s_hpk", "s_l2s",
            "s_hl", "s_hq", "s_htr", "s_lb2", "s_w2", "s_mm2", "s_ev2",
            "s_fd", "s_fo", "s_out", "s_dbg", "s_id", "s_x0", "s_x1", "s_ps", "s_bc0", "s_bc1", "s_pxn", "s_pdt", "s_pbc", "s_wl0", "s_wl1", "s_wp0", "s_wp1", "s_w1a", "s_w1b", "s_w2a", "s_w2b", "s_hla", "s_hlb", "s_hp0", "s_hp1", "s_hp2", "s_hp3", "s_vq", "s_sq"]

    P.dma("sync", wxT.rearrange("p (dc c) -> p dc c", dc=16),
          wxT_in.rearrange("(dc p) c -> p dc c", p=128), sem="s_wl")
    P.dma("sync", wdtT[:, :], wdtT_in[:, :], sem="s_wl")
    P.dma("sync", bdt.rearrange("p (g o) -> p g o", g=8),
          bdt_in.rearrange("(g p o) -> p g o", p=128, o=1), sem="s_wl", slow=True)
    P.dma("sync", A_sb.rearrange("p (g n) -> p g n", g=8),
          alog_in.rearrange("(g p) n -> p g n", p=128), sem="s_wl")
    P.dma("sync", D_sb.rearrange("p (g o) -> p g o", g=8),
          dvec_in.rearrange("(g p o) -> p g o", p=128, o=1), sem="s_wl", slow=True)
    P.dma("sync", flg[:, :], hflag_in.broadcast_to([128, 2]), sem="s_wl")
    n_wl0 = P.val("s_wl")

    P.emit("gpsimd", lambda e: e.memset(ident[:, :], 0.0), inc=("s_id", 1))
    P.emit("gpsimd", lambda e: e.affine_select(
        out=ident[:, :], in_=ident[:, :], compare_op=ALU.not_equal, fill=1.0,
        base=0, pattern=[[-1, 128]], channel_multiplier=1),
        waits=[("s_id", 1)], inc=("s_id", 1))
    P.emit("vector", lambda e: e.memset(cpool[:, 0:1], -0.5))
    P.emit("vector", lambda e: e.memset(cpool[:, 1:2], 0.5))
    P.emit("vector", lambda e: e.memset(cpool[:, 2:3], 1.0))
    P.emit("vector", lambda e: e.memset(cpool[:, 3:4], 1e-5))
    P.emit("vector", lambda e: e.memset(cpool[:, 4:5], 1e-6))
    P.emit("vector", lambda e: e.memset(onesr[:, :], 1.0))
    P.emit("vector", lambda e: e.memset(carry[:, :], 0.0))
    P.emit("scalar", lambda e: e.activation(out=A_sb[:, :], in_=A_sb[:, :], func=AF.Exp),
           waits=[("s_wl", n_wl0)], inc=("s_cst", 1))
    P.emit("vector", lambda e: e.tensor_scalar(A_sb[:, :], A_sb[:, :], -1.0, None,
                                               op0=ALU.mult),
           waits=[("s_cst", P.val("s_cst"))], inc=("s_cst", 1))
    n_cst = P.val("s_cst")

    v_st = [0] * NC0
    v_xn = [0] * NC0
    v_mm = [0] * NC0
    v_xpe = [0] * NC0
    v_pk = [0] * NC0
    for c in range(NC0):
        ring = c % 2
        xt = reg(P_xt + ring * 4096, 4096)
        w = [("s_st", v_st[c - 2])] if c >= 2 else []
        sx = "s_x0" if ring == 0 else "s_x1"
        P.dma("sync", reg3(P_xt + ring * 4096, 2, Dm),
              x_in[c * TC0:(c + 1) * TC0, :].rearrange("(tt p) c -> p tt c", p=128),
              waits=w, sem=sx)
        nx = P.val(sx)
        sqv = reg(P_sq, Dm)
        for tt in range(2):
            ttg = c * 2 + tt
            xv = xt[:, tt * Dm:(tt + 1) * Dm]
            mcol = stat_m[:, ttg:ttg + 1]
            rcol = stat_r[:, ttg:ttg + 1]
            P.seq("vector", lambda e, xv=xv, m=mcol: e.tensor_reduce(
                m, xv, axis=AX, op=ALU.add), waits=[(sx, nx)] if tt == 0 else [])
            P.seq("vector", lambda e, xv=xv, r=rcol, sqv=sqv: e.scalar_tensor_tensor(
                sqv, xv, 1.0, xv, op0=ALU.mult, op1=ALU.mult, accum_out=r))
            P.seq("vector", lambda e, m=mcol: e.tensor_scalar(m, m, INV_D, None, op0=ALU.mult))
            P.seq("vector", lambda e, m=mcol, sqv=sqv: e.scalar_tensor_tensor(
                sqv[:, 0:1], m, 1.0, m, op0=ALU.mult, op1=ALU.mult))
            P.seq("vector", lambda e, r=rcol: e.tensor_scalar(r, r, INV_D, None, op0=ALU.mult))
            vcol = stat_v[:, ttg:ttg + 1]
            P.seq("vector", lambda e, r=rcol, v=vcol, sqv=sqv: e.tensor_tensor(
                v, r, sqv[:, 0:1], op=ALU.subtract), inc=("s_st", 1))
            nst0 = P.val("s_st")
            P.seq("scalar", lambda e, r=rcol, v=vcol: e.activation(out=r, in_=v, func=AF.Ln,
                                                                   bias=cpool[:, 3:4]),
                  waits=[("s_st", nst0)])
            P.seq("scalar", lambda e, r=rcol: e.activation(out=r, in_=r, func=AF.Exp,
                                                           scale=cpool[:, 0:1]),
                  inc=("s_st", 1))
        v_st[c] = P.val("s_st")
        w = [("s_st", v_st[c])]
        if c >= 1:
            w.append(("s_bc0" if (c - 1) % 2 == 0 else "s_bc1",
                      32 * ((c - 1) // 2 + 1)))
        P.dma("gpsimd", p_m[0:1, c * TC0:(c + 1) * TC0].rearrange(
                  "o (a p) -> p a o", p=128),
              stat_m[:, c * 2:c * 2 + 2].rearrange("p a -> p a ()"),
              waits=w, sem="s_ps", slow=True)
        P.dma("gpsimd", p_r[0:1, c * TC0:(c + 1) * TC0].rearrange(
                  "o (a p) -> p a o", p=128),
              stat_r[:, c * 2:c * 2 + 2].rearrange("p a -> p a ()"), sem="s_ps", slow=True)
        npk = P.val("s_ps")
        mb = reg(P_mb + ring * TC0, TC0)
        rb = reg(P_rb + ring * TC0, TC0)
        sbc = "s_bc0" if ring == 0 else "s_bc1"
        w = [("s_ps", npk)]
        if c >= 2:
            w.append(("s_u", v_xn[c - 2]))
        P.dma("sync", mb, p_m[0:1, c * TC0:(c + 1) * TC0].broadcast_to([128, TC0]),
              waits=w, sem=sbc)
        P.dma("sync", rb, p_r[0:1, c * TC0:(c + 1) * TC0].broadcast_to([128, TC0]),
              sem=sbc)
        nbc = P.val(sbc)
        w = [(sbc, nbc)]
        if c >= 1:
            w += [("s_xp", v_mm[c - 1]), ("s_pxn", 16 * c)]
        P.dma("sync", reg3(P_xnf, 16, TC0),
              xT_in[:, c * TC0:(c + 1) * TC0].rearrange("(dc p) c -> p dc c", p=128),
              waits=w, sem="s_xT")
        nxt = P.val("s_xT")
        xnf = arena[:, P_xnf:P_xnf + 16 * TC0].rearrange("p (dc c) -> p dc c", dc=16)
        P.emit("vector", lambda e, xnf=xnf, mb=mb: e.tensor_tensor(
            xnf, xnf, grep(mb, 16), op=ALU.subtract),
            waits=[("s_xT", nxt), (sbc, nbc)])
        P.emit("vector", lambda e, xnf=xnf, rb=rb: e.tensor_tensor(
            xnf, xnf, grep(rb, 16), op=ALU.mult), inc=("s_u", 1))
        v_xn[c] = P.val("s_u")
        nu = P.val("s_u")
        for dc in range(16):
            w = []
            if dc == 0:
                w = [("s_u", nu), ("s_wl", n_wl0)]
                if c >= 1:
                    w.append(("s_xpe", v_xpe[c - 1]))
            P.emit("tensor", lambda e, dc=dc: e.matmul(
                ps[6][0:NPJ, 0:TC0], wxT[:, dc * NPJ:(dc + 1) * NPJ],
                arena[:, P_xnf + dc * TC0:P_xnf + (dc + 1) * TC0],
                start=(dc == 0), stop=(dc == 15)),
                waits=w, inc=(("s_xp", 1) if dc == 15 else None))
        v_mm[c] = P.val("s_xp")
        P.emit("scalar", lambda e, c=c: e.copy(
            out=xproj[:, c * TC0:(c + 1) * TC0], in_=ps[6][0:NPJ, 0:TC0]),
            waits=[("s_xp", v_mm[c])], inc=("s_xpe", 1))
        v_xpe[c] = P.val("s_xpe")
        P.dma("gpsimd", p_xn[:, :, c * TC0:(c + 1) * TC0].rearrange("g p c -> p g c"), reg3(P_xnf, 8, TC0),
              waits=[("s_xp", v_mm[c])], sem="s_pxn")
        for g in range(8):
            bank = ps[7] if g % 2 == 0 else ps[5]
            if g < 2:
                w = [("s_xpe", v_xpe[c])]
                if c >= 1:
                    w.append(("s_dte", P.val("s_dte") - (2 - g)))
            else:
                w = [("s_dte", P.val("s_dte") - 1)]
            P.emit("tensor", lambda e, g=g, bank=bank, c=c: e.matmul(
                bank[:, 0:TC0], wdtT[:, g * 128:(g + 1) * 128],
                xproj[0:RK, c * TC0:(c + 1) * TC0], start=True, stop=True),
                waits=w, inc=("s_dtm", 1))
            ndtm = P.val("s_dtm")
            we = [("s_dtm", ndtm)]
            if g == 0:
                we.append(("s_cst", n_cst))
                if c >= 1:
                    we.append(("s_pdt", 16 * c))
            P.emit("scalar", lambda e, g=g, bank=bank: e.activation(
                out=arena[:, P_dts + g * TC0:P_dts + (g + 1) * TC0], in_=bank[:, 0:TC0],
                func=AF.Exp, bias=bdt[:, g:g + 1]), waits=we, inc=("s_dte", 1))
        nd8 = P.val("s_dte")
        P.emit("scalar", lambda e: e.activation(
            out=reg(P_dts, 2048), in_=reg(P_dts, 2048), func=AF.Ln, bias=cpool[:, 2:3]),
            waits=[("s_dte", nd8)], inc=("s_dte", 1))
        P.dma("gpsimd", p_dt[:, :, c * TC0:(c + 1) * TC0].rearrange("g p c -> p g c"), reg3(P_dts, 8, TC0),
              waits=[("s_dte", P.val("s_dte"))], sem="s_pdt")
        v_pk[c] = P.val("s_pdt")
    P.dma("gpsimd", p_bc[:, :], xproj[64:96, :],
          waits=[("s_xpe", P.val("s_xpe"))], sem="s_pbc")
    P.emit("gpsimd", lambda e: e.engine_nop(),
           waits=[("s_ps", P.val("s_ps")), ("s_pxn", P.val("s_pxn")),
                  ("s_pdt", P.val("s_pdt")), ("s_pbc", P.val("s_pbc"))],
           inc=("s_p0d", 1))
    if DEV:
        P.dma("sync", dbg["stat"][:, 0:32], stat_m[:, :], waits=[("s_p0d", 1)], sem="s_dbg")
        P.dma("sync", dbg["stat"][:, 32:64], stat_r[:, :], sem="s_dbg")
        P.dma("sync", dbg["pm"][0:1, :], p_m[:, :], sem="s_dbg")
        P.dma("sync", dbg["pm"][1:2, :], p_r[:, :], sem="s_dbg")
        P.dma("sync", dbg["mbc"][:, 0:512], reg(P_mb, 512), sem="s_dbg")
        P.dma("sync", dbg["mbc"][:, 512:1024], reg(P_rb, 512), sem="s_dbg")
        P.dma("sync", dbg["xproj"][:, :], xproj[:, :],
              waits=[("s_xpe", P.val("s_xpe"))], sem="s_dbg")
        P.dma("sync", dbg["dt"][:, :, :], p_dt[:, :, :], waits=[("s_p0d", 1)], sem="s_dbg")
        P.dma("sync", dbg["xn"][:, :, :], p_xn[:, :, :], waits=[("s_p0d", 1)], sem="s_dbg")

    def wprep(tile_in, park_out, gcol, n_tiles, tw):
        base_wp = (P.val("s_wp0"), P.val("s_wp1"))
        for i in range(n_tiles):
            ring = i % 4
            wv = reg(W_tl + ring * tw, tw)
            swl = "s_wl0" if ring % 2 == 0 else "s_wl1"
            w = ([("s_ga", P.val("s_ga") - 3)] if i >= 4
                 else [("s_sg", P.val("s_sg"))])
            P.dma("sync", wv, tile_in(i), waits=w, sem=swl)
            nw = P.val(swl)
            w = [(swl, nw)]
            if i >= 2:
                w.append(("s_gb", P.val("s_gb") - 1))
            P.emit("scalar", lambda e, wv=wv, i=i: e.activation(
                out=reg(W_sq, tw), in_=wv, func=AF.Abs,
                accum_out=gpart[:, i % 2:i % 2 + 1]), waits=w, inc=("s_ga", 1))
            nga = P.val("s_ga")
            if i == 0:
                P.seq("vector", lambda e: e.tensor_copy(gpart[:, 2:3], gpart[:, 0:1]),
                      waits=[("s_ga", nga), ("s_gs", P.val("s_gs"))], inc=("s_gb", 1))
            else:
                P.seq("vector", lambda e, i=i: e.tensor_tensor(
                    gpart[:, 2:3], gpart[:, 2:3], gpart[:, i % 2:i % 2 + 1], op=ALU.add),
                    waits=[("s_ga", nga)], inc=("s_gb", 1))
        ngb = P.val("s_gb")
        P.emit("tensor", lambda e: e.matmul(
            ps[4][0:1, 0:1], gpart[:, 2:3], cpool[:, 2:3], start=True, stop=True),
            waits=[("s_gb", ngb), ("s_cst", n_cst)], inc=("s_gs", 1))
        ngs = P.val("s_gs")
        ntot = float(n_tiles) * 128.0 * float(tw)
        P.seq("vector", lambda e: e.tensor_scalar(
            gpart[0:1, 3:4], ps[4][0:1, 0:1], 1.0 / ntot, None, op0=ALU.mult),
            waits=[("s_gs", ngs)])
        P.seq("vector", lambda e: e.tensor_scalar(
            gpart[0:1, 2:3], gpart[0:1, 3:4], 0.5, None, op0=ALU.mult))
        P.seq("vector", lambda e: e.tensor_scalar(
            gpart[0:1, 3:4], gpart[0:1, 3:4], 1e-6, None, op0=ALU.add))
        P.seq("vector", lambda e: e.reciprocal(gpart[0:1, 3:4], gpart[0:1, 3:4]),
              inc=("s_gs", 1))
        ngs2 = P.val("s_gs")
        P.emit("tensor", lambda e: e.matmul(
            ps[4][:, 4:6], onesr[:, :], gpart[0:1, 2:4], start=True, stop=True),
            waits=[("s_gs", ngs2)], inc=("s_gs", 1))
        ngs3 = P.val("s_gs")
        P.emit("scalar", lambda e, gcol=gcol: e.copy(out=scal[:, gcol * 2:gcol * 2 + 2],
                                                     in_=ps[4][:, 4:6]),
               waits=[("s_gs", ngs3)], inc=("s_gs", 1))
        ngs4 = P.val("s_gs")
        for i in range(n_tiles):
            ring = i % 4
            wv = reg(W_tl + ring * tw, tw)
            swl = "s_wl0" if ring % 2 == 0 else "s_wl1"
            w = [("s_sg", P.val("s_sg") - 6)] if i >= 4 else [("s_gs", ngs4)]
            P.dma("sync", wv, tile_in(i), waits=w, sem=swl)
            nw = P.val(swl)
            wsg = regb(W_s1, tw // 2)
            wsg2 = regb(W_s2, tw // 2)
            w = [(swl, nw)]
            if i == 0:
                w.append(("s_gs", ngs4))
            if i >= 1:
                w.append(("s_t2", P.val("s_t2")))
            P.emit("scalar", lambda e, wv=wv, wsg=wsg, gcol=gcol: e.activation(
                out=wsg, in_=wv, func=AF.Sign, bias=cpool[:, 0:1],
                scale=scal[:, gcol * 2 + 1:gcol * 2 + 2]), waits=w, inc=("s_sg", 1))
            P.emit("scalar", lambda e, wv=wv, wsg2=wsg2, gcol=gcol: e.activation(
                out=wsg2, in_=wv, func=AF.Sign, bias=cpool[:, 1:2],
                scale=scal[:, gcol * 2 + 1:gcol * 2 + 2]), inc=("s_sg", 1))
            nsg = P.val("s_sg")
            t2v = regb(W_t2 + ring * (tw // 2), tw // 2)
            swp = "s_wp0" if ring % 2 == 0 else "s_wp1"
            w = [("s_sg", nsg)]
            if i >= 4:
                w.append((swp, base_wp[ring % 2] + 16 * (i // 2 - 1)))
            else:
                w.append((swp, base_wp[ring % 2]))
            P.emit("vector", lambda e, t2v=t2v, wsg=wsg, wsg2=wsg2: e.tensor_tensor(
                t2v, wsg, wsg2, op=ALU.add), waits=w, inc=("s_t2", 1))
            nt2 = P.val("s_t2")
            P.dma("gpsimd", park_out(i), t2v, waits=[("s_t2", nt2)], sem=swp)

    if PH < 1:
        for tt in range(16):
            P.dma("sync", out_hbm[tt * 128:(tt + 1) * 128, :],
                  xm_in[tt * 128:(tt + 1) * 128, :], sem="s_out")
        wend = [("s_out", P.val("s_out"))]
        if DEV:
            wend.append(("s_dbg", P.val("s_dbg")))
        P.emit("sync", lambda e: e.nop(), waits=wend)
        return _finish(nc, P, SEMS, ctxs)

    def wfc_tile(i):
        dc, oc = i // 8, i % 8
        return wfcT_in[dc * 128:(dc + 1) * 128, oc * 1024:(oc + 1) * 1024]

    def wfc_park(i):
        dc, oc = i // 8, i % 8
        return p_w1[oc, dc, :, :]

    wprep(wfc_tile, wfc_park, 0, 128, 1024)
    n_gs_fc = P.val("s_gs")
    n_w1pk = (P.val("s_wp0"), P.val("s_wp1"))
    if DEV:
        P.dma("sync", dbg["gam"][0:1, 0:4], gpart[0:1, 0:4],
              waits=[("s_gs", P.val("s_gs"))], sem="s_dbg")
        P.dma("sync", dbg["w1"][:, :, :], p_w1[0, :, :, :],
              waits=[("s_wp0", n_w1pk[0]), ("s_wp1", n_w1pk[1])], sem="s_dbg")
    wprep(lambda i: wpjT_in[(i // 2) * 128:(i // 2 + 1) * 128,
                            (i % 2) * 1024:(i % 2 + 1) * 1024],
          lambda i: p_w2[i // 2, :, (i % 2) * 1024:(i % 2 + 1) * 1024], 1, 128, 1024)
    n_gs_pj = P.val("s_gs")
    n_wpk = (P.val("s_wp0"), P.val("s_wp1"))

    if PH < 2:
        for tt in range(16):
            P.dma("sync", out_hbm[tt * 128:(tt + 1) * 128, :],
                  xm_in[tt * 128:(tt + 1) * 128, :],
                  waits=([("s_wp0", P.val("s_wp0")), ("s_wp1", P.val("s_wp1"))]
                         if tt == 0 else []), sem="s_out")
        wend = [("s_out", P.val("s_out"))]
        if DEV:
            wend.append(("s_dbg", P.val("s_dbg")))
        P.emit("sync", lambda e: e.nop(), waits=wend)
        return _finish(nc, P, SEMS, ctxs)

    v_mt_ch = [0] * NCH
    v_yD_ch = [0] * NCH
    ya_hist = []
    v_nxg = [0] * 16
    v_xtr = [0] * 16

    def plan_xg_prefetch(g):
        for tt in range(2):
            ttg = g * 2 + tt
            slot = F_xg if tt == 0 else F_xnt
            sxg = "s_xg" if tt == 0 else "s_xg1"
            w = [("s_p0d", 1)] if g == 0 else [("s_xtr", v_xtr[ttg - 2])]
            P.dma("sync", reg(slot, Dm), xm_in[ttg * 128:(ttg + 1) * 128, :],
                  waits=w, sem=sxg)
            v_nxg[ttg] = P.val(sxg)

    def plan_scan_chunk(c):
        w = [("s_wp0", n_wpk[0]), ("s_wp1", n_wpk[1]), ("s_p0d", 1)]
        if c >= 1:
            w += [("s_dA", P.val("s_dA")), ("s_ya", P.val("s_ya"))]
        P.dma("sync", reg3(S_dt, 8, TC), p_dt[:, :, c * TC:(c + 1) * TC].rearrange("g p c -> p g c"),
              waits=w, sem="s_ci")
        P.dma("sync", reg3(S_xn, 8, TC), p_xn[:, :, c * TC:(c + 1) * TC].rearrange("g p c -> p g c"), sem="s_ci")
        nci = P.val("s_ci")
        P.emit("vector", lambda e: e.tensor_tensor(
            reg(S_u, 2048), reg(S_dt, 2048), reg(S_xn, 2048), op=ALU.mult),
            waits=[("s_ci", nci)], inc=("s_u", 1))
        nu = P.val("s_u")
        w = []
        if c >= 2:
            w.append(("s_mt", v_mt_ch[c - 2]))
        yoff = S_y if c % 2 == 0 else S_y1
        yv = reg(yoff, 2048)
        P.emit("gpsimd", lambda e, yv=yv: e.memset(yv, 0.0), waits=w, inc=("s_cy", 1))
        ncy = P.val("s_cy")
        for n in range(NS):
            gidx = c * NS + n
            par = n % 2
            scb = "s_cb" if par == 0 else "s_cb1"
            bcoff = par * 256
            w = []
            if gidx >= 2:
                w.append(("s_ya", ya_hist[gidx - 2]))
            P.dma("sync", reg(S_bcB + bcoff, TC),
                  p_bc[n:n + 1, c * TC:(c + 1) * TC].broadcast_to([128, TC]),
                  waits=w, sem=scb)
            P.dma("sync", reg(S_bcC + bcoff, TC),
                  p_bc[16 + n:17 + n, c * TC:(c + 1) * TC].broadcast_to([128, TC]),
                  sem=scb)
            ncb = P.val(scb)
            w = [("s_ci", nci), ("s_cst", n_cst)]
            if c > 0 or n > 0:
                w.append(("s_sc", P.val("s_sc")))
            for g in range(8):
                P.emit("scalar", lambda e, g=g, n=n: e.activation(
                    out=arena[:, S_dA + g * TC:S_dA + (g + 1) * TC],
                    in_=arena[:, S_dt + g * TC:S_dt + (g + 1) * TC],
                    func=AF.Exp, scale=A_sb[:, g * 16 + n:g * 16 + n + 1]),
                    waits=(w if g == 0 else []), inc=("s_dA", 1) if g == 7 else None)
            ndA = P.val("s_dA")
            bcb = reg(S_bcB + bcoff, TC)
            P.emit("vector", lambda e, bcb=bcb: e.tensor_tensor(
                reg3(S_dbx, 8, TC), reg3(S_u, 8, TC), grep(bcb, 8), op=ALU.mult),
                waits=[("s_u", nu), (scb, ncb)])
            w = [("s_dA", ndA)]
            for g in range(8):
                P.emit("vector", lambda e, g=g, n=n: e.tensor_tensor_scan(
                    arena[:, S_h + g * TC:S_h + (g + 1) * TC],
                    arena[:, S_dA + g * TC:S_dA + (g + 1) * TC],
                    arena[:, S_dbx + g * TC:S_dbx + (g + 1) * TC],
                    carry[:, n * 8 + g:n * 8 + g + 1],
                    op0=ALU.mult, op1=ALU.add),
                    waits=(w if g == 0 else []),
                    inc=(("s_vq", 1) if g == 7 else None))
            nvq = P.val("s_vq")
            hsl = arena[:, S_h:S_h + 2048]
            csrc = bass.AP(hsl.tensor, hsl.offset + TC - 1, [list(hsl.ap[0]), [TC, 8]])
            P.emit("vector", lambda e, n=n, csrc=csrc: e.tensor_copy(
                carry[:, n * 8:n * 8 + 8], csrc),
                waits=[("s_vq", nvq)], inc=("s_sc", 1))
            bcc = reg(S_bcC + bcoff, TC)
            P.emit("vector", lambda e, bcc=bcc: e.tensor_tensor(
                reg3(S_dbx, 8, TC), reg3(S_h, 8, TC), grep(bcc, 8), op=ALU.mult))
            P.emit("vector", lambda e, yv=yv: e.tensor_tensor(
                yv, yv, reg(S_dbx, 2048), op=ALU.add),
                waits=([("s_cy", ncy)] if n == 0 else []), inc=("s_ya", 1))
            ya_hist.append(P.val("s_ya"))
        for g in range(8):
            P.emit("vector", lambda e, g=g, yv=yv: e.scalar_tensor_tensor(
                yv[:, g * TC:(g + 1) * TC], arena[:, S_xn + g * TC:S_xn + (g + 1) * TC],
                D_sb[:, g:g + 1], yv[:, g * TC:(g + 1) * TC], op0=ALU.mult, op1=ALU.add),
                inc=("s_ya", 1) if g == 7 else None)
        v_yD_ch[c] = P.val("s_ya")

    def plan_mix_T(c):
        yoff = S_y if c % 2 == 0 else S_y1
        yv = reg(yoff, 2048)
        nyD = v_yD_ch[c]
        nmpk_prev = P.val("s_mpk")
        for tt2 in range(2):
            for gb in range(2):
                bank = ps[6 + tt2]
                for gg in range(4):
                    g = gb * 4 + gg
                    w = []
                    if gg == 0:
                        w = [("s_ya", nyD), ("s_me", P.val("s_me"))]
                        if c == 0 and tt2 == 0 and gb == 0:
                            w.append(("s_id", 2))
                    P.emit("tensor", lambda e, g=g, tt2=tt2, gg=gg, bank=bank, yv=yv:
                           e.transpose(bank[:, gg * 128:(gg + 1) * 128],
                                       yv[:, g * TC + tt2 * 128:g * TC + tt2 * 128 + 128],
                                       ident[:, :]),
                           waits=w, inc=("s_mt", 1) if gg == 3 else None)
                nmt = P.val("s_mt")
                w = [("s_mt", nmt)]
                if tt2 == 0 and gb == 0:
                    w.append(("s_mpk", nmpk_prev))
                    if c == 0:
                        w.append(("s_p0d", 1))
                P.emit("scalar", lambda e, tt2=tt2, gb=gb, bank=bank: e.copy(
                    out=reg(S_mx + tt2 * 1024 + gb * 512, 512), in_=bank[:, 0:512]),
                    waits=w, inc=("s_me", 1))
        v_mt_ch[c] = P.val("s_mt")
        nme = P.val("s_me")
        P.dma("gpsimd", cc_i[c * TC:(c + 1) * TC, :].rearrange("(tt p) d -> p tt d", p=128),
              reg3(S_mx, 2, 1024), waits=[("s_me", nme)], sem="s_mpk")

    w1_gate = {}
    bl1_defer = {}

    def emit_piece_stats(g, op8, tt, sl, nhe):
        hp = reg(F_hst + sl * 1024, 1024)
        P.seq("vector", lambda e, hp=hp, tt=tt, op8=op8: e.scalar_tensor_tensor(
            reg(S_dbx, 1024), hp, 1.0, hp, op0=ALU.mult, op1=ALU.mult,
            accum_out=gsum[:, 32 + tt * 8 + op8:33 + tt * 8 + op8]),
            waits=[("s_he", nhe)])
        P.seq("vector", lambda e, hp=hp, tt=tt, op8=op8: e.tensor_reduce(
            gsum[:, 48 + tt * 8 + op8:49 + tt * 8 + op8], hp, axis=AX,
            op=ALU.max))
        P.seq("vector", lambda e, hp=hp, tt=tt, op8=op8: e.tensor_reduce(
            gsum[:, 64 + tt * 8 + op8:65 + tt * 8 + op8], hp, axis=AX,
            op=ALU.min), inc=("s_hs", 1))
        nhs = P.val("s_hs")
        P.dma("gpsimd", p_h[g, tt, :, op8 * 1024:(op8 + 1) * 1024], hp,
              waits=[("s_hs", nhs)], sem=f"s_hp{sl}")

    def plan_bl1_defer(g):
        for it in bl1_defer.pop(g):
            if it[0] == "stats":
                _, op8, tt, sl, nhe = it
                emit_piece_stats(g, op8, tt, sl, nhe)
            else:
                _, op8, og, tt, bank, sl, ogidx, nmm = it
                w = [("s_mm1", nmm)]
                slotsem = f"s_hp{sl}"
                if og == 0:
                    w.append((slotsem, P.val(slotsem)))
                P.emit("scalar", lambda e, bank=bank, sl=sl, og=og, tt=tt,
                       ogidx=ogidx: e.activation(
                    out=arena[:, F_hst + sl * 1024 + og * 512:
                              F_hst + sl * 1024 + og * 512 + 512],
                    in_=bank[:, :], func=AF.Gelu,
                    accum_out=gsum[:, tt * 16 + ogidx:tt * 16 + ogidx + 1]),
                    waits=w, inc=("s_he", 1))
                if og == 1:
                    emit_piece_stats(g, op8, tt, sl, P.val("s_he"))
        nhs2 = P.val("s_hs")
        for tt in range(2):
            gl = g * 2 + tt
            Lc = ln2[:, gl * 4:gl * 4 + 4]
            P.seq("vector", lambda e, Lc=Lc, tt=tt: e.tensor_reduce(
                Lc[:, 0:1], gsum[:, tt * 16:tt * 16 + 16], axis=AX, op=ALU.add),
                waits=[("s_hs", nhs2)])
            P.seq("vector", lambda e, Lc=Lc: e.tensor_scalar(
                Lc[:, 0:1], Lc[:, 0:1], INV_H, None, op0=ALU.mult))
            P.seq("vector", lambda e, Lc=Lc: e.scalar_tensor_tensor(
                gsum[:, 96:97], Lc[:, 0:1], 1.0, Lc[:, 0:1], op0=ALU.mult, op1=ALU.mult))
            P.seq("vector", lambda e, tt=tt: e.tensor_reduce(
                gsum[:, 97:98], gsum[:, 32 + tt * 8:40 + tt * 8], axis=AX, op=ALU.add))
            P.seq("vector", lambda e, Lc=Lc: e.tensor_scalar(
                Lc[:, 1:2], gsum[:, 97:98], INV_H, None, op0=ALU.mult))
            P.seq("vector", lambda e, Lc=Lc: e.tensor_tensor(
                Lc[:, 1:2], Lc[:, 1:2], gsum[:, 96:97], op=ALU.subtract), inc=("s_l2s", 1))
            nl2a = P.val("s_l2s")
            P.seq("scalar", lambda e, Lc=Lc: e.activation(
                out=Lc[:, 1:2], in_=Lc[:, 1:2], func=AF.Ln, bias=cpool[:, 4:5]),
                waits=[("s_l2s", nl2a)])
            P.seq("scalar", lambda e, Lc=Lc: e.activation(
                out=Lc[:, 1:2], in_=Lc[:, 1:2], func=AF.Exp, scale=cpool[:, 0:1]),
                inc=("s_l2s", 1))
            nl2b = P.val("s_l2s")
            P.seq("vector", lambda e, tt=tt: e.tensor_reduce(
                gsum[:, 98:99], gsum[:, 48 + tt * 8:56 + tt * 8], axis=AX, op=ALU.max),
                waits=[("s_l2s", nl2b)])
            P.seq("vector", lambda e, tt=tt: e.tensor_reduce(
                gsum[:, 99:100], gsum[:, 64 + tt * 8:72 + tt * 8], axis=AX, op=ALU.min))
            P.seq("vector", lambda e, Lc=Lc: e.tensor_tensor(
                gsum[:, 98:99], gsum[:, 98:99], Lc[:, 0:1], op=ALU.subtract))
            P.seq("vector", lambda e, Lc=Lc: e.tensor_tensor(
                gsum[:, 99:100], Lc[:, 0:1], gsum[:, 99:100], op=ALU.subtract))
            P.seq("vector", lambda e: e.tensor_tensor(
                gsum[:, 98:99], gsum[:, 98:99], gsum[:, 99:100], op=ALU.max))
            P.seq("vector", lambda e, Lc=Lc: e.tensor_tensor(
                gsum[:, 98:99], gsum[:, 98:99], Lc[:, 1:2], op=ALU.mult))
            P.seq("vector", lambda e, Lc=Lc: e.reciprocal(Lc[:, 2:3], gsum[:, 98:99]))
            P.seq("vector", lambda e, Lc=Lc: e.tensor_scalar(
                Lc[:, 2:3], Lc[:, 2:3], 128.0, None, op0=ALU.mult))
            P.seq("vector", lambda e, Lc=Lc: e.tensor_scalar(
                Lc[:, 3:4], gsum[:, 98:99], 1.0 / 128.0, None, op0=ALU.mult))
            P.seq("vector", lambda e, Lc=Lc: e.tensor_tensor(
                Lc[:, 3:4], Lc[:, 3:4], scal[:, 2:3], op=ALU.mult),
                waits=[("s_gs", n_gs_pj)])
            P.seq("vector", lambda e, Lc=Lc: e.tensor_scalar(
                Lc[:, 0:1], Lc[:, 0:1], -1.0, None, op0=ALU.mult), inc=("s_l2s", 1))

    def plan_bl1_group(g):
        F = fst
        for tt in range(2):
            ttg = g * 2 + tt
            xnv = reg(F_xg if tt == 0 else F_xnt, Dm)
            nxg = v_nxg[ttg]
            sxg = "s_xg" if tt == 0 else "s_xg1"
            P.seq("vector", lambda e, ttg=ttg: e.tensor_tensor(
                F[:, 0:1], stat_m[:, ttg:ttg + 1], flg[:, 0:1], op=ALU.mult),
                waits=[(sxg, nxg), ("s_wl", n_wl0)])
            P.seq("vector", lambda e, ttg=ttg: e.tensor_tensor(
                F[:, 2:3], stat_m[:, 16 + ttg:17 + ttg], flg[:, 1:2], op=ALU.mult))
            P.seq("vector", lambda e: e.tensor_tensor(
                F[:, 0:1], F[:, 0:1], F[:, 2:3], op=ALU.add))
            P.seq("vector", lambda e: e.tensor_scalar(
                F[:, 0:1], F[:, 0:1], -1.0, None, op0=ALU.mult))
            P.seq("vector", lambda e, ttg=ttg: e.tensor_tensor(
                F[:, 1:2], stat_r[:, ttg:ttg + 1], flg[:, 0:1], op=ALU.mult))
            P.seq("vector", lambda e, ttg=ttg: e.tensor_tensor(
                F[:, 2:3], stat_r[:, 16 + ttg:17 + ttg], flg[:, 1:2], op=ALU.mult))
            P.seq("vector", lambda e: e.tensor_tensor(
                F[:, 1:2], F[:, 1:2], F[:, 2:3], op=ALU.add))
            P.seq("vector", lambda e, xnv=xnv: e.tensor_scalar(
                xnv, xnv, F[:, 0:1], F[:, 1:2], op0=ALU.add, op1=ALU.mult))
            P.seq("vector", lambda e, xnv=xnv: e.tensor_reduce(
                F[:, 4:5], xnv, axis=AX, op=ALU.max))
            P.seq("vector", lambda e, xnv=xnv: e.tensor_reduce(
                F[:, 5:6], xnv, axis=AX, op=ALU.min))
            P.seq("vector", lambda e: e.tensor_scalar(F[:, 5:6], F[:, 5:6], -1.0, None,
                                                       op0=ALU.mult))
            P.seq("vector", lambda e: e.tensor_tensor(F[:, 4:5], F[:, 4:5], F[:, 5:6],
                                                       op=ALU.max))
            P.seq("vector", lambda e, ttg=ttg: e.tensor_tensor(
                F[:, 2:3], stat_v[:, ttg:ttg + 1], flg[:, 0:1], op=ALU.mult))
            P.seq("vector", lambda e, ttg=ttg: e.tensor_tensor(
                F[:, 3:4], stat_v[:, 16 + ttg:17 + ttg], flg[:, 1:2], op=ALU.mult))
            P.seq("vector", lambda e: e.tensor_tensor(
                F[:, 2:3], F[:, 2:3], F[:, 3:4], op=ALU.add))
            P.seq("vector", lambda e: e.tensor_tensor(
                F[:, 3:4], F[:, 1:2], F[:, 1:2], op=ALU.mult))
            P.seq("vector", lambda e: e.tensor_tensor(
                F[:, 3:4], F[:, 3:4], F[:, 2:3], op=ALU.mult), inc=("s_pst", 1))
            np3 = P.val("s_pst")
            P.seq("scalar", lambda e: e.activation(out=F[:, 3:4], in_=F[:, 3:4],
                                                    func=AF.Ln, bias=cpool[:, 4:5]),
                   waits=[("s_pst", np3)])
            P.seq("scalar", lambda e: e.activation(out=F[:, 3:4], in_=F[:, 3:4],
                                                    func=AF.Exp, scale=cpool[:, 0:1]),
                   inc=("s_pst", 1))
            np3b = P.val("s_pst")
            P.seq("vector", lambda e: e.reciprocal(F[:, 6:7], F[:, 4:5]))
            P.seq("vector", lambda e: e.tensor_scalar(F[:, 6:7], F[:, 6:7], 128.0, None,
                                                       op0=ALU.mult))
            P.seq("vector", lambda e: e.tensor_scalar(F[:, 7:8], F[:, 4:5], 1.0 / 128.0,
                                                       None, op0=ALU.mult))
            P.seq("vector", lambda e, xnv=xnv: e.tensor_scalar(
                xnv, xnv, F[:, 6:7], CLIP, op0=ALU.mult, op1=ALU.min))
            P.seq("vector", lambda e, xnv=xnv: e.tensor_scalar(
                xnv, xnv, -CLIP, None, op0=ALU.max))
            P.seq("vector", lambda e: e.tensor_tensor(F[:, 7:8], F[:, 7:8],
                                                       F[:, 3:4], op=ALU.mult),
                   waits=[("s_pst", np3b)])
            P.seq("vector", lambda e: e.tensor_tensor(F[:, 7:8], F[:, 7:8],
                                                       scal[:, 0:1], op=ALU.mult),
                   waits=[("s_gs", n_gs_fc)])
            P.seq("vector", lambda e, xnv=xnv: e.tensor_scalar(
                xnv, xnv, F[:, 7:8], None, op0=ALU.mult), inc=("s_xq", 1))
            nxq = P.val("s_xq")
            if DEV and g == 0 and tt == 0:
                P.dma("sync", dbg["xhat"][:, :], xnv, waits=[("s_xq", nxq)], sem="s_dbg")
                P.dma("sync", dbg["fst"][:, :], F[:, 0:8], waits=[("s_xq", nxq)], sem="s_dbg")
            for qb in range(4):
                bank = ps[4 + qb % 2]
                w = [("s_xq", nxq), ("s_lb", max(0, P.val("s_lb") - 6)),
                     ("s_mm1", P.val("s_mm1")), ("s_id", 2)]
                for k in range(4):
                    dc = qb * 4 + k
                    P.emit("tensor", lambda e, dc=dc, k=k, bank=bank, xnv=xnv: e.transpose(
                        bank[:, k * 128:(k + 1) * 128],
                        xnv[:, dc * 128:(dc + 1) * 128], ident[:, :]),
                        waits=(w if k == 0 else []),
                        inc=("s_xtr", 1) if k == 3 else None)
                nxtr = P.val("s_xtr")

                def lbap(l, qb=qb, tt=tt):
                    v = regb(F_lb + l * 2048, 2048)
                    return bass.AP(v.tensor, v.offset + qb * 4 * 256 + tt * 128,
                                   [list(v.ap[0]), [256, 4], [1, 128]])
                bv = bank[:, :]
                psv = bass.AP(bv.tensor, bv.offset, [list(bv.ap[0]), [128, 4], [1, 128]])
                r1 = reg(F_spl, 1024)
                r1v = bass.AP(r1.tensor, r1.offset, [list(r1.ap[0]), [128, 4], [1, 128]])
                r2v = bass.AP(r1.tensor, r1.offset + 512, [list(r1.ap[0]), [128, 4], [1, 128]])
                P.emit("vector", lambda e, lb=lbap(0), psv=psv: e.tensor_copy(lb, psv),
                       waits=[("s_xtr", nxtr)], inc=("s_lb", 1))
                nl1 = P.val("s_lb")
                P.emit("vector", lambda e, lb=lbap(0), psv=psv, r1v=r1v: e.scalar_tensor_tensor(
                    r1v, lb, -1.0, psv, op0=ALU.mult, op1=ALU.add),
                    waits=[("s_lb", nl1)], inc=("s_lb", 1))
                nl2 = P.val("s_lb")
                P.emit("vector", lambda e, lb=lbap(1), r1v=r1v: e.tensor_copy(lb, r1v),
                       waits=[("s_lb", nl2)], inc=("s_lb", 1))
                nl3 = P.val("s_lb")
                P.emit("vector", lambda e, lb=lbap(1), r1v=r1v, r2v=r2v: e.scalar_tensor_tensor(
                    r2v, lb, -1.0, r1v, op0=ALU.mult, op1=ALU.add),
                    waits=[("s_lb", nl3)], inc=("s_lb", 1))
                nl4 = P.val("s_lb")
                P.emit("vector", lambda e, lb=lbap(2), r2v=r2v: e.tensor_copy(lb, r2v),
                       waits=[("s_lb", nl4)], inc=("s_lb", 1))
            v_xtr[ttg] = P.val("s_xtr")
        nlball = P.val("s_lb")
        if DEV and g == 0:
            for l in range(3):
                P.dma("sync", dbg["lb"][l, :, :], regb(F_lb + l * 2048, 2048),
                      waits=[("s_lb", nlball)], sem="s_dbg")
        he_base = P.val("s_he")
        defer = []
        for op8 in range(8):
            for og in range(2):
                half = og % 2
                sw1 = "s_w1a" if half == 0 else "s_w1b"
                w = [("s_wp0", n_w1pk[0]), ("s_wp1", n_w1pk[1])]
                key = ("w1half", half)
                prev_waves = w1_gate.get(key)
                if prev_waves is not None:
                    w.append(("s_mm1", prev_waves))
                P.dma("sync", regb(F_w1 + half * 4096, 4096).rearrange(
                    "p (dc c) -> p dc c", dc=16),
                    p_w1[op8, :, :, og * 512:(og + 1) * 512].rearrange(
                        "a p c -> p a c"), waits=w, sem=sw1)
                nw1 = P.val(sw1)
                w1v = regb(F_w1 + half * 4096, 4096)
                for tt in range(2):
                    bank = ps[og * 2 + tt]
                    first = True
                    for dc in range(16):
                        for lmb in range(3):
                            lb = regb(F_lb + lmb * 2048, 2048)
                            stat = bass.AP(lb.tensor, lb.offset + dc * 256 + tt * 128,
                                           [list(lb.ap[0]), [1, 128]])
                            mov = bass.AP(w1v.tensor, w1v.offset + dc * 512,
                                          [list(w1v.ap[0]), [1, 512]])
                            w = []
                            if first:
                                w = [("s_lb", nlball), (sw1, nw1),
                                     ("s_he", P.val("s_he"))]
                                first = False
                            P.emit("tensor", lambda e, stat=stat, mov=mov, bank=bank,
                                   st=(dc == 0 and lmb == 0),
                                   sp=(dc == 15 and lmb == 2): e.matmul(
                                bank[:, :], stat, mov, start=st, stop=sp),
                                waits=w, inc=("s_mm1", 1) if (dc == 15 and lmb == 2) else None)
                    nmm = P.val("s_mm1")
                    if tt == 1:
                        w1_gate[("w1half", half)] = nmm
                    ogidx = op8 * 2 + og
                    sl = (op8 * 2 + tt) % 4
                    if op8 >= 7:
                        defer.append(("evac", op8, og, tt, bank, sl, ogidx, nmm))
                        continue
                    w = [("s_mm1", nmm)]
                    slotsem = f"s_hp{sl}"
                    if P.val(slotsem) > 0 and og == 0:
                        w.append((slotsem, P.val(slotsem)))
                    P.emit("scalar", lambda e, bank=bank, sl=sl, og=og, tt=tt,
                           ogidx=ogidx: e.activation(
                        out=arena[:, F_hst + sl * 1024 + og * 512:
                                  F_hst + sl * 1024 + og * 512 + 512],
                        in_=bank[:, :], func=AF.Gelu,
                        accum_out=gsum[:, tt * 16 + ogidx:tt * 16 + ogidx + 1]),
                        waits=w, inc=("s_he", 1))
                    nhe = P.val("s_he")
                    if og == 1:
                        if op8 >= 5:
                            defer.append(("stats", op8, tt, sl, nhe))
                            continue
                        emit_piece_stats(g, op8, tt, sl, nhe)
        bl1_defer[g] = defer
    for g in range(NG):
        plan_scan_chunk(2 * g)
        if PH >= 3 and g >= 1:
            plan_bl1_defer(g - 1)
        if PH >= 3:
            plan_xg_prefetch(g)
        plan_scan_chunk(2 * g + 1)
        plan_mix_T(2 * g)
        plan_mix_T(2 * g + 1)
        if PH >= 3:
            plan_bl1_group(g)
    if PH >= 3:
        plan_bl1_defer(NG - 1)
    n_mpk = P.val("s_mpk")
    n_hpk = [P.val(f"s_hp{i}") for i in range(4)]
    if DEV:
        P.dma("sync", dbg["mix"][:, :], cc_i[:, :], waits=[("s_mpk", n_mpk)], sem="s_dbg")
        P.dma("sync", dbg["h0"][:, :, :], p_h[0, :, :, :],
              waits=[(f"s_hp{i}", n_hpk[i]) for i in range(4)], sem="s_dbg")

    if PH < 4:
        wend = [("s_mpk", n_mpk), ("s_sc", P.val("s_sc")),
                ("s_ya", P.val("s_ya")), ("s_me", P.val("s_me"))] + \
            [(f"s_hp{i}", n_hpk[i]) for i in range(4)]
        for tt in range(16):
            P.dma("sync", out_hbm[tt * 128:(tt + 1) * 128, :],
                  xm_in[tt * 128:(tt + 1) * 128, :],
                  waits=(wend if tt == 0 else []), sem="s_out")
        wend2 = [("s_out", P.val("s_out"))]
        if DEV:
            wend2.append(("s_dbg", P.val("s_dbg")))
        P.emit("sync", lambda e: e.nop(), waits=wend2)
        return _finish(nc, P, SEMS, ctxs)

    if SIM_NOCC:
        P.dma("gpsimd", cc_o[0, :, :], cc_i[:, :], waits=[("s_mpk", n_mpk)], sem="s_mpk")
        P.dma("gpsimd", cc_o[1, :, :], cc_i[:, :], sem="s_mpk")
        P.emit("gpsimd", lambda e: e.engine_nop(),
               waits=[("s_mpk", P.val("s_mpk"))], inc=("s_cc", 1))
    else:
        P.emit("gpsimd", lambda e: e.collective_compute(
            "AllGather", ALU.bypass, ins=[cc_i[:, :]], outs=[cc_o[:, :, :]],
            replica_groups=[[0, 1], [2, 3], [4, 5], [6, 7]]),
            waits=[("s_mpk", n_mpk)], inc=("s_cc", 1))

    guard = [("s_sc", P.val("s_sc")), ("s_ya", P.val("s_ya")),
             ("s_me", P.val("s_me")), ("s_he", P.val("s_he")),
             ("s_hs", P.val("s_hs")), ("s_mpk", n_mpk),
             ("s_lb", P.val("s_lb")), ("s_tmp", P.val("s_tmp")),
             ("s_mm1", P.val("s_mm1")), ("s_l2s", P.val("s_l2s"))] + \
        [(f"s_hp{i}", n_hpk[i]) for i in range(4)] + \
        [("s_wp0", n_wpk[0]), ("s_wp1", n_wpk[1])]
    w2_gate = {}
    lb2_gate = {}
    qts_info = {}

    def bl2_qts(g, s):
        ring = s % 2
        shl = "s_hla" if ring == 0 else "s_hlb"
        w = list(guard) if (g == 0 and s == 0) else []
        if g > 0 or s >= 2:
            w.append(("s_lb2", max(0, P.val("s_lb2") - 40)))
        P.dma("sync", reg3(C_hld + ring * 4096, 2, Dm),
              p_h[g, :, :, s * Dm:(s + 1) * Dm].rearrange("a p c -> p a c"),
              waits=w, sem=shl)
        nhl = P.val(shl)
        lbslot = (g * NSL + s) % 2
        for tt in range(2):
            gl = g * 2 + tt
            Lc = ln2[:, gl * 4:gl * 4 + 4]
            hv = reg(C_hld + ring * 4096 + tt * Dm, Dm)
            qv = reg(C_xht, Dm)
            w = [(shl, nhl)]
            if g > 0 or s > 0 or tt > 0:
                w.append(("s_lb2", P.val("s_lb2")))
        
            P.emit("vector", lambda e, hv=hv, qv=qv, Lc=Lc: e.tensor_scalar(
                qv, hv, Lc[:, 0:1], Lc[:, 1:2], op0=ALU.add, op1=ALU.mult), waits=w)
            P.emit("vector", lambda e, qv=qv, Lc=Lc: e.tensor_scalar(
                qv, qv, Lc[:, 2:3], CLIP, op0=ALU.mult, op1=ALU.min))
            P.emit("vector", lambda e, qv=qv, Lc=Lc: e.tensor_scalar(
                qv, qv, -CLIP, Lc[:, 3:4], op0=ALU.max, op1=ALU.mult),
                inc=("s_hq", 1))
            nhq = P.val("s_hq")
            for qb in range(4):
                bank = ps[4 + qb % 2]
                w = [("s_hq", nhq), ("s_lb2", max(0, P.val("s_lb2") - 6)),
                     ("s_mm2", lb2_gate.get(lbslot, 0))]
                if g == 0 and s == 0 and tt == 0:
                    w += [("s_mt", P.val("s_mt")), ("s_xtr", P.val("s_xtr"))]
                for k in range(4):
                    hc = qb * 4 + k
                    P.emit("tensor", lambda e, hc=hc, k=k, bank=bank, qv=qv: e.transpose(
                        bank[:, k * 128:(k + 1) * 128],
                        qv[:, hc * 128:(hc + 1) * 128], ident[:, :]),
                        waits=(w if k == 0 else []),
                        inc=("s_htr", 1) if k == 3 else None)
                nhtr = P.val("s_htr")

                def lb2ap(l, qb=qb, tt=tt, lbslot=lbslot):
                    v = regb(C_lb + lbslot * 6144 + l * 2048, 2048)
                    return bass.AP(v.tensor, v.offset + qb * 4 * 256 + tt * 128,
                                   [list(v.ap[0]), [256, 4], [1, 128]])
                bv = bank[:, :]
                psv = bass.AP(bv.tensor, bv.offset,
                              [list(bv.ap[0]), [128, 4], [1, 128]])
                r1 = reg(C_spl, 1024)
                r1v = bass.AP(r1.tensor, r1.offset, [list(r1.ap[0]), [128, 4], [1, 128]])
                r2v = bass.AP(r1.tensor, r1.offset + 512,
                              [list(r1.ap[0]), [128, 4], [1, 128]])
                P.emit("vector", lambda e, lb=lb2ap(0), psv=psv: e.tensor_copy(lb, psv),
                       waits=[("s_htr", nhtr)], inc=("s_lb2", 1))
                nl1 = P.val("s_lb2")
                P.emit("vector", lambda e, lb=lb2ap(0), psv=psv, r1v=r1v:
                       e.scalar_tensor_tensor(r1v, lb, -1.0, psv, op0=ALU.mult,
                                              op1=ALU.add),
                       waits=[("s_lb2", nl1)], inc=("s_lb2", 1))
                nl2 = P.val("s_lb2")
                P.emit("vector", lambda e, lb=lb2ap(1), r1v=r1v: e.tensor_copy(lb, r1v),
                       waits=[("s_lb2", nl2)], inc=("s_lb2", 1))
                nl3 = P.val("s_lb2")
                P.emit("vector", lambda e, lb=lb2ap(1), r1v=r1v, r2v=r2v:
                       e.scalar_tensor_tensor(r2v, lb, -1.0, r1v, op0=ALU.mult,
                                              op1=ALU.add),
                       waits=[("s_lb2", nl3)], inc=("s_lb2", 1))
                nl4 = P.val("s_lb2")
                P.emit("vector", lambda e, lb=lb2ap(2), r2v=r2v: e.tensor_copy(lb, r2v),
                       waits=[("s_lb2", nl4)], inc=("s_lb2", 1))
        qts_info[(g, s)] = (P.val("s_lb2"), lbslot)

    def bl2_waves(g, s):
        nlb2, lbslot = qts_info.pop((g, s))
        for oq in range(4):
            whalf = oq % 2
            sw2 = "s_w2a" if whalf == 0 else "s_w2b"
            w = []
            key = ("w2q", whalf)
            prevw = w2_gate.get(key)
            if prevw is not None:
                w.append(("s_mm2", prevw))
            if g == 0 and s == 0 and oq == 0:
                w += [("s_sc", P.val("s_sc")), ("s_me", P.val("s_me")),
                      ("s_mpk", n_mpk)] + \
                    [("s_wp0", n_wpk[0]), ("s_wp1", n_wpk[1])]
            P.dma("sync", regb(C_w2 + whalf * 4096, 4096).rearrange(
                "p (hc c) -> p hc c", hc=16),
                p_w2[s * 16:(s + 1) * 16, :, oq * 512:(oq + 1) * 512].rearrange(
                    "a p c -> p a c"),
                waits=w, sem=sw2)
            nw2 = P.val(sw2)
            w2v = regb(C_w2 + whalf * 4096, 4096)
            for tt in range(2):
                bank = ps[2 * (oq % 2) + tt]
                first = True
                for hc in range(16):
                    for lmb in range(3):
                        lb = regb(C_lb + lbslot * 6144 + lmb * 2048, 2048)
                        stat = bass.AP(lb.tensor, lb.offset + hc * 256 + tt * 128,
                                       [list(lb.ap[0]), [1, 128]])
                        mov = bass.AP(w2v.tensor, w2v.offset + hc * 512,
                                      [list(w2v.ap[0]), [1, 512]])
                        w = []
                        if first:
                            w = [("s_lb2", nlb2), (sw2, nw2),
                                 ("s_ev2", P.val("s_ev2"))]
                            first = False
                        P.emit("tensor", lambda e, stat=stat, mov=mov, bank=bank,
                               st=(hc == 0 and lmb == 0),
                               sp=(hc == 15 and lmb == 2): e.matmul(
                            bank[:, :], stat, mov, start=st, stop=sp),
                            waits=w,
                            inc=("s_mm2", 1) if (hc == 15 and lmb == 2) else None)
                nmm2 = P.val("s_mm2")
                if tt == 1:
                    w2_gate[("w2q", whalf)] = nmm2
                dst = reg(C_acc + tt * Dm + oq * 512, 512)
                w = [("s_mm2", nmm2)]
                if s == 0:
                    w.append(("s_fo", P.val("s_fo")))
                    P.emit("vector", lambda e, dst=dst, bank=bank: e.tensor_copy(
                        dst, bank[:, :]), waits=w, inc=("s_ev2", 1))
                else:
                    P.emit("vector", lambda e, dst=dst, bank=bank: e.tensor_tensor(
                        dst, dst, bank[:, :], op=ALU.add), waits=w, inc=("s_ev2", 1))
        lb2_gate[lbslot] = P.val("s_mm2")

    def bl2_finals(g):
        nev = P.val("s_ev2")
        for tt in range(2):
            ttg = g * 2 + tt
            xmv = reg(C_fin, Dm)
            cav = reg(C_fin + Dm, Dm)
            w = [("s_ev2", nev), ("s_cc", 1)]
            if g > 0 or tt > 0:
                w.append(("s_fo", P.val("s_fo")))
            P.dma("sync", xmv, xm_in[ttg * 128:(ttg + 1) * 128, :], waits=w, sem="s_fd")
            P.dma("sync", cav[:, 0:DH], cc_o[0, ttg * 128:(ttg + 1) * 128, :], sem="s_fd")
            P.dma("sync", cav[:, DH:Dm], cc_o[1, ttg * 128:(ttg + 1) * 128, :], sem="s_fd")
            nfd1 = P.val("s_fd")
            accp = reg(C_acc + tt * Dm, Dm)
            P.emit("vector", lambda e, accp=accp, cav=cav: e.scalar_tensor_tensor(
                accp, cav, flg[:, 0:1], accp, op0=ALU.mult, op1=ALU.add),
                waits=[("s_fd", nfd1), ("s_wl", n_wl0)], inc=("s_fo", 1))
            nfo1 = P.val("s_fo")
            P.dma("sync", cav[:, 0:DH], cc_o[0, 2048 + ttg * 128:2048 + (ttg + 1) * 128, :],
                  waits=[("s_fo", nfo1)], sem="s_fd")
            P.dma("sync", cav[:, DH:Dm], cc_o[1, 2048 + ttg * 128:2048 + (ttg + 1) * 128, :],
                  sem="s_fd")
            nfd2 = P.val("s_fd")
            P.emit("vector", lambda e, accp=accp, cav=cav: e.scalar_tensor_tensor(
                accp, cav, flg[:, 1:2], accp, op0=ALU.mult, op1=ALU.add),
                waits=[("s_fd", nfd2)])
            P.emit("vector", lambda e, accp=accp, xmv=xmv: e.tensor_tensor(
                accp, accp, xmv, op=ALU.add), inc=("s_fo", 1))
            nfo = P.val("s_fo")
            P.dma("sync", out_hbm[ttg * 128:(ttg + 1) * 128, :], accp,
                  waits=[("s_fo", nfo)], sem="s_out")

    bl2_qts(0, 0)
    for g in range(NG):
        for s in range(NSL):
            ni = g * NSL + s + 1
            if ni < NG * NSL:
                bl2_qts(ni // NSL, ni % NSL)
            bl2_waves(g, s)
        bl2_finals(g)
    n_out = P.val("s_out")
    if DEV2:
        P.dma("sync", dbg["mix2"][:, :], cc_i[:, :],
              waits=[("s_out", n_out)], sem="s_dbg")
        P.dma("sync", dbg["hall"][:, :, :, :], p_h[:, :, :, :], sem="s_dbg")
        P.dma("sync", dbg["cco"][:, :, :], cc_o[:, :, :], sem="s_dbg")
        P.dma("sync", dbg["w1f"][:, :, :, :], p_w1[:, :, :, :], sem="s_dbg")
        P.dma("sync", dbg["w2f"][:, :, :], p_w2[:, :, :], sem="s_dbg")
    wend = [("s_out", n_out)]
    if DEV or DEV2:
        wend.append(("s_dbg", P.val("s_dbg")))
    P.emit("sync", lambda e: e.nop(), waits=wend)

    return _finish(nc, P, SEMS, ctxs)


def _finish(nc, P, SEMS, ctxs):
    semh = {}
    for s in SEMS:
        g2 = nc.semaphore(s)
        semh[s] = g2.__enter__()
        ctxs.append(g2)

    with nc.Block() as block:
        def runner(name):
            def go(e):
                for waits, fn, inc in P.ops[name]:
                    for (s, v) in waits:
                        e.wait_ge(semh[s], v)
                    ins = fn(e)
                    if inc is not None:
                        ins.then_inc(semh[inc[0]], inc[1])
            return go
        block.sync(runner("sync"))
        block.vector(runner("vector"))
        block.scalar(runner("scalar"))
        block.tensor(runner("tensor"))
        block.gpsimd(runner("gpsimd"))

    for g2 in reversed(ctxs):
        g2.__exit__(None, None, None)
    return nc


def kernel(L0_attn_x, L0_attn_W_x, L0_attn_W_dt, L0_attn_b_dt, L0_attn_A_log,
           L0_attn_D, L0_ffn_w_fc, L0_ffn_w_proj):
    x = np.ascontiguousarray(np.asarray(L0_attn_x, dtype=np.float32))
    wx = np.asarray(L0_attn_W_x, np.float32)
    wdt = np.asarray(L0_attn_W_dt, np.float32)
    bdtv = np.asarray(L0_attn_b_dt, np.float32)
    alog = np.asarray(L0_attn_A_log, np.float32)
    dv = np.asarray(L0_attn_D, np.float32)
    wfc = np.asarray(L0_ffn_w_fc, np.float32)
    wpj = np.asarray(L0_ffn_w_proj, np.float32)

    wfcT = np.ascontiguousarray(wfc.T)
    wpjT = np.ascontiguousarray(wpj.T)
    wxT = np.ascontiguousarray(wx.T)

    in_maps = []
    for c in range(8):
        b, hf = c // 2, c % 2
        ds = slice(hf * DH, (hf + 1) * DH)
        od = slice((1 - hf) * DH, (2 - hf) * DH)
        tsl = slice(hf * 2048, (hf + 1) * 2048)
        xb = x[b]
        xTh = np.ascontiguousarray(np.concatenate([xb.T[ds], xb.T[od]], 0))
        wxTh = np.ascontiguousarray(np.concatenate([wxT[ds], wxT[od]], 0))
        in_maps.append({
            "x": xb,
            "xT": xTh,
            "xmine": np.ascontiguousarray(xb[tsl]),
            "wxT": wxTh,
            "wdtT": np.ascontiguousarray(wdt[ds].T),
            "bdt": np.ascontiguousarray(bdtv[ds]),
            "alog": np.ascontiguousarray(alog[ds]),
            "dvec": np.ascontiguousarray(dv[ds]),
            "wfcT": wfcT,
            "wpjT": wpjT,
            "hflag": np.array([[1.0 - hf, float(hf)]], np.float32),
        })
    nc = build_nc()
    res = run_bass_kernel_spmd(nc, in_maps, core_ids=list(range(8)))
    out = np.empty((B, L, Dm), np.float32)
    for c in range(8):
        b, hf = c // 2, c % 2
        out[b, hf * 2048:(hf + 1) * 2048, :] = res.results[c]["o"]
    return out
